# version 01
# baseline (speedup 1.0000x reference)
import numpy as np
import ml_dtypes
import concourse.bass as bass
import concourse.mybir as mybir
from concourse.bass_utils import run_bass_kernel_spmd
from contextlib import ExitStack

F32 = mybir.dt.float32
BF16 = mybir.dt.bfloat16
AF = mybir.ActivationFunctionType
ALU = mybir.AluOpType
AX = mybir.AxisListType

D = 2048
T = 8192
TC = 1024
NT = 8
KC = 16
INTER = 5632
IC = 44
NCORE = 8
H0, G0, HD0 = 32, 8, 64
H1, G1, HD1 = 16, 4, 128
BF = ml_dtypes.bfloat16


def core_tiles(c):
    return [c, 15 - c]


def core_token_rows(c):
    t0, t1 = core_tiles(c)
    return np.concatenate([np.arange(512 * t0, 512 * t0 + 512),
                           np.arange(512 * t1, 512 * t1 + 512)])


def rope_perm(width, hd):
    p = []
    for h in range(width // hd):
        base = h * hd
        p += [base + 2 * i for i in range(hd // 2)]
        p += [base + 2 * i + 1 for i in range(hd // 2)]
    return np.array(p)


def host_prep(inp):
    g = {}
    f32 = np.float32
    perm_q = rope_perm(2048, 64)
    perm_k = rope_perm(512, 64)
    Wq = np.ascontiguousarray(inp["L0_attn_Wq"][:, perm_q])
    Wk = np.ascontiguousarray(inp["L0_attn_Wk"][:, perm_k])
    def rhs_slabs(W, nslab, nw):
        return np.ascontiguousarray(
            W.reshape(16, 128, nslab, nw).transpose(2, 1, 0, 3))
    g["l0_wq"] = rhs_slabs(Wq, 8, 256)
    g["l0_wk"] = rhs_slabs(Wk, 2, 256)
    g["l0_wv"] = rhs_slabs(inp["L0_attn_Wv"], 2, 256)
    g["l0_wo"] = rhs_slabs(inp["L0_attn_Wo"], 8, 256)
    g["l0_bq"] = np.ascontiguousarray(inp["L0_attn_bq"][perm_q].reshape(1, 2048))
    g["l0_bk"] = np.ascontiguousarray(inp["L0_attn_bk"][perm_k].reshape(1, 512))
    g["l0_bv"] = np.ascontiguousarray(inp["L0_attn_bv"].reshape(1, 512))
    g["l0_bo"] = np.ascontiguousarray(inp["L0_attn_bo"].reshape(1, 2048))
    gw, uw, dw = inp["L0_ffn_gate_w"], inp["L0_ffn_up_w"], inp["L0_ffn_down_w"]
    g["l0_gate"] = np.ascontiguousarray(gw.reshape(44, 128, 16, 128).transpose(0, 3, 2, 1))
    g["l0_up"] = np.ascontiguousarray(uw.reshape(44, 128, 16, 128).transpose(0, 3, 2, 1))
    g["l0_down"] = np.ascontiguousarray(dw.T.reshape(44, 128, 2048))
    g["l0_gs"] = np.ascontiguousarray(inp["L0_ffn_gate_s"].reshape(44, 128).T)
    g["l0_us"] = np.ascontiguousarray(inp["L0_ffn_up_s"].reshape(44, 128).T)
    g["l0_ds"] = np.ascontiguousarray(np.broadcast_to(inp["L0_ffn_down_s"], (128, 2048)))
    g["l1_wq"] = rhs_slabs(np.ascontiguousarray(inp["L1_attn_q_project"].T), 8, 256)
    g["l1_wk"] = rhs_slabs(np.ascontiguousarray(inp["L1_attn_k_project"].T), 2, 256)
    g["l1_wv"] = rhs_slabs(np.ascontiguousarray(inp["L1_attn_v_project"].T), 2, 256)
    g["l1_wo"] = rhs_slabs(np.ascontiguousarray(inp["L1_attn_o_project"].T), 8, 256)
    g["l1_bq"] = np.ascontiguousarray(inp["L1_attn_q_bias"].reshape(1, 2048))
    g["l1_bk"] = np.ascontiguousarray(inp["L1_attn_k_bias"].reshape(1, 512))
    g["l1_bv"] = np.ascontiguousarray(inp["L1_attn_v_bias"].reshape(1, 512))
    g["l1_lnw"] = np.ascontiguousarray(np.broadcast_to(inp["L1_attn_ln_weight"], (128, 2048)))
    sw, lu, ld = inp["L1_ffn_select_weight"], inp["L1_ffn_lora_update"], inp["L1_ffn_lora_degen"]
    md = inp["L1_ffn_select_mean_diff"]
    def cat_t(a):
        return np.ascontiguousarray(
            a.transpose(1, 0, 2).reshape(16, 128, 64).transpose(1, 0, 2))
    g["sw_t"] = cat_t(sw)
    g["lu_t"] = cat_t(lu)
    g["ld_cat"] = np.ascontiguousarray(ld.transpose(0, 2, 1).reshape(64, 2048))
    def colT(a):
        return np.ascontiguousarray(a.T.reshape(16, 128, 8).transpose(1, 0, 2))
    g["md_t"] = colT(md)
    g["dw_t"] = colT(inp["L1_ffn_debias_w"])
    g["bw"] = np.ascontiguousarray(inp["L1_ffn_bias_w"])
    g["cen_bc"] = np.ascontiguousarray(np.broadcast_to(inp["L1_ffn_imp_center"], (128, 8)))
    g["slo_bc"] = np.ascontiguousarray(np.broadcast_to(inp["L1_ffn_imp_slope"], (128, 8)))
    g["ident"] = np.eye(128, dtype=f32)
    kp = np.arange(128)[:, None]
    qf = np.arange(512)[None, :]
    g["masks"] = np.ascontiguousarray(np.concatenate(
        [(qf >= kp + 128 * j).astype(f32) for j in range(4)], axis=1))
    pos = np.arange(T, dtype=np.float64)
    inv0 = 10000.0 ** (-np.arange(0, 64, 2).astype(np.float64) / 64.0)
    inv1 = 10000.0 ** (-np.arange(0, 128, 2).astype(np.float64) / 128.0)
    ang0 = (pos[:, None] * inv0[None, :]).astype(f32)
    ang1 = (pos[:, None] * inv1[None, :]).astype(f32)
    g["cos0_full"] = np.cos(ang0).astype(f32)
    g["sin0_full"] = np.sin(ang0).astype(f32)
    g["cos1_full"] = np.cos(ang1).astype(f32)
    g["sin1_full"] = np.sin(ang1).astype(f32)
    return g


def kvi_table(c):
    t = np.zeros((2, 2, 128, 192), np.int32)
    ZROW = 12352
    pp = np.arange(128)
    for l in range(2):
        G = 8 if l == 0 else 4
        HD = 64 if l == 0 else 128
        for qh in range(2):
            t5top = c if qh == 0 else 15 - c
            for it in range(16):
                t5 = t5top - it
                if t5 >= 0:
                    rank = t5 if t5 < 8 else 15 - t5
                    half = 0 if t5 < 8 else 1
                    kbase, kcol = rank * 1544, half * 512
                    vbase = rank * 1544 + 520 + 512 * half
                else:
                    kbase, kcol, vbase = ZROW, 0, ZROW
                for g in range(G):
                    if it * G + g < 16 * G:
                        rowk = kbase + (0 if t5 < 0 else g * HD) + (pp % HD)
                        t[l, qh, :, it * G + g] = (rowk * 1040 + kcol) // 8
                for j in range(4):
                    rowv = vbase + (0 if t5 < 0 else 128 * j) + pp
                    t[l, qh, :, 16 * G + it * 4 + j] = (rowv * 1040) // 8
    return t


def per_core_inputs(g, x_full):
    maps = []
    for c in range(NCORE):
        rows = core_token_rows(c)
        m = {k: v for k, v in g.items() if not k.endswith("_full")}
        m["xin"] = np.ascontiguousarray(x_full[rows])
        m["kvi"] = kvi_table(c)
        m["cos0"] = np.ascontiguousarray(g["cos0_full"][rows])
        m["sin0"] = np.ascontiguousarray(g["sin0_full"][rows])
        m["cos1"] = np.ascontiguousarray(g["cos1_full"][rows])
        m["sin1"] = np.ascontiguousarray(g["sin1_full"][rows])
        maps.append(m)
    return maps



class PB:

    def __init__(self, core_id):
        self.c = core_id
        self.nc = bass.Bass("TRN2", target_bir_lowering=False, debug=False)
        self.ctx = ExitStack()
        self.cnt = {}
        self.sems = {}
        self.hw = {}
        self.inputs = {}

    def din(self, name, shape, dtype=F32):
        ap = self.nc.dram_tensor(name, list(shape), dtype, kind="ExternalInput").ap()
        self.inputs[name] = ap
        return ap

    def dout(self, name, shape, dtype=F32):
        return self.nc.dram_tensor(name, list(shape), dtype, kind="ExternalOutput").ap()

    def dint(self, name, shape, dtype=BF16, **kw):
        return self.nc.dram_tensor(name, list(shape), dtype, kind="Internal", **kw).ap()

    def sb(self, name, shape, dtype):
        return self.ctx.enter_context(self.nc.sbuf_tensor(name, list(shape), dtype))

    def psum(self, name, shape, dtype=F32):
        return self.ctx.enter_context(self.nc.psum_tensor(name, list(shape), dtype))

    def sem(self, name):
        s = self.ctx.enter_context(self.nc.semaphore(name))
        self.sems[name] = s
        self.cnt[name] = 0
        return s

    def inc(self, name, inst, k=1):
        inst.then_inc(self.sems[name], k)
        self.cnt[name] += k
        return (name, self.cnt[name])

    def tok(self, name):
        return (name, self.cnt[name])

    def wait_eng(self, eng, engname, toks):
        for t in toks:
            if t is None:
                continue
            name, v = t
            if v <= 0:
                continue
            key = (engname, name)
            if self.hw.get(key, 0) < v:
                eng.wait_ge(self.sems[name], v)
                self.hw[key] = v

    @property
    def te(self): return self.nc.tensor
    @property
    def ve(self): return self.nc.vector
    @property
    def se(self): return self.nc.scalar
    @property
    def pe(self): return self.nc.gpsimd
    @property
    def sy(self): return self.nc.sync

    def wt(self, *toks): self.wait_eng(self.te, "T", toks)
    def wv(self, *toks): self.wait_eng(self.ve, "V", toks)
    def ws(self, *toks): self.wait_eng(self.se, "S", toks)
    def wp(self, *toks): self.wait_eng(self.pe, "P", toks)
    def wy(self, *toks): self.wait_eng(self.sy, "Y", toks)

    def all_toks(self):
        return [self.tok(s) for s in self.sems]

    def join(self):
        toks = self.all_toks()
        self.wt(*toks); self.wv(*toks); self.ws(*toks); self.wp(*toks); self.wy(*toks)
        return toks


class Ring:

    def __init__(self, views):
        self.views = views
        self.ready = [None] * len(views)
        self.free = [None] * len(views)
        self.i = 0

    def next(self):
        j = self.i % len(self.views)
        self.i += 1
        return j, self.views[j]


def alloc_common(p: PB):
    for s in ("dg", "ds", "da", "cc", "eT", "eV", "eS", "eP"):
        p.sem(s)
    p.psA = p.psum("psA", [128, 1024])
    p.psB = p.psum("psB", [128, 1024])
    p.psC = p.psum("psC", [128, 1024])
    p.psD = p.psum("psD", [128, 1024])
    p.stream = p.sb("stream", [128, NT * D], F32)
    p.abuf = p.sb("abuf", [128, 16384], BF16)
    p.bigbuf = p.sb("bigbuf", [128, 24704], BF16)
    p.scr32 = p.sb("scr32", [128, 2560], F32)
    p.wbuf = p.sb("wbuf", [128, 8192], BF16)
    p.ptbuf = p.sb("ptbuf", [128, 4096], BF16)
    p.kvbuf = p.sb("kvbuf", [128, 2112], BF16)
    p.trash = p.sb("trash", [128, 1024], BF16)
    p.identb = p.sb("identb", [128, 128], BF16)
    p.maskb = p.sb("maskb", [128, 2048], BF16)
    p.cos0b = p.sb("cos0b", [128, 256], F32)
    p.sin0b = p.sb("sin0b", [128, 256], F32)
    p.cos1b = p.sb("cos1b", [128, 512], F32)
    p.sin1b = p.sb("sin1b", [128, 512], F32)
    p.mcol = p.sb("mcol", [128, 8], F32)
    p.acol = p.sb("acol", [128, 8], F32)
    p.stat = p.sb("stat", [128, 8], F32)
    p.statS = p.sb("statS", [128, 4], F32)
    p.birow = p.sb("birow", [1, 3072], BF16)
    p.onesc = p.sb("onesc", [1, 128], BF16)
    p.onesf = p.sb("onesf", [1, 128], F32)
    p.onek = p.sb("onek", [128, 2], BF16)
    p.qn = p.sb("qn", [128, 32], F32)
    p.negb = p.sb("negb", [128, 32], F32)
    p.knrow = p.sb("knrow", [8, 1024], BF16)
    p.smallf = p.sb("smallf", [1, 128], F32)
    p.linv = p.sb("linv", [1, 512], F32)
    p.lbc = p.scr32[:, 0:1024]
    p.gsb = p.sb("gsb", [128, 44], F32)
    p.usb = p.sb("usb", [128, 44], F32)
    p.cenb = p.sb("cenb", [128, 8], F32)
    p.slob = p.sb("slob", [128, 8], F32)
    p.l1tmp = p.sb("l1tmp", [128, 64], F32)
    p.ccol = p.sb("ccol", [64, 1], F32)
    p.kvib = p.sb("kvib", [128, 192], mybir.dt.int32)
    p.trviews = [p.psD[:, k * 64:(k + 1) * 64].bitcast(BF16) for k in range(16)]


TR_ORDER = [0, 8, 1, 9, 2, 10, 3, 11, 4, 12, 5, 13, 6, 14, 7, 15]


class TrSeq:

    def __init__(self, p):
        self.p = p
        self.k = 0
        self.view_free = [None] * 16
        self.hist = []

    def transpose(self, src, dst, extra_dep=(), rows=128):
        p = self.p
        P = src.shape[0]
        F = src.shape[-1]
        v = TR_ORDER[self.k % 16]
        trv = p.trviews[v][0:F, 0:P]
        if len(self.hist) >= 2:
            p.wt(self.hist[-2])
        p.wt(self.view_free[v], *extra_dep)
        tt = p.inc("eT", p.te.transpose(trv, src, p.identb[0:P, 0:P]))
        p.ws(tt)
        tev = p.inc("eS", p.se.copy(dst, trv))
        self.view_free[v] = tev
        self.hist.append(tev)
        self.k += 1
        return tev


def emit_prologue(p: PB):
    g = p.pe
    p.inc("dg", g.dma_start(out=p.identb[:, :], in_=p.inputs["ident"][:, :]), 16)
    p.inc("dg", g.dma_start(out=p.maskb[:, :], in_=p.inputs["masks"][:, :]), 16)
    for nm, dst, w in (("cos0", p.cos0b, 32), ("sin0", p.sin0b, 32),
                       ("cos1", p.cos1b, 64), ("sin1", p.sin1b, 64)):
        p.inc("ds", p.sy.dma_start(
            out=dst[:, :].rearrange("p (i w) -> p i w", w=w),
            in_=p.inputs[nm].rearrange("(i p) w -> p i w", p=128)), 16)
    p.inc("dg", g.dma_start(out=p.gsb[:, :], in_=p.inputs["l0_gs"][:, :]), 16)
    p.inc("dg", g.dma_start(out=p.usb[:, :], in_=p.inputs["l0_us"][:, :]), 16)
    p.inc("dg", g.dma_start(out=p.cenb[:, :], in_=p.inputs["cen_bc"][:, :]), 16)
    p.inc("dg", g.dma_start(out=p.slob[:, :], in_=p.inputs["slo_bc"][:, :]), 16)
    p.inc("eV", p.ve.memset(p.onesc[:, :], 1.0))
    p.inc("eV", p.ve.memset(p.onesf[:, :], 1.0))
    p.inc("eV", p.ve.memset(p.onek[:, :], 1.0))


def emit_stream_load(p: PB, src_ap):
    return p.inc("ds", p.sy.dma_start(
        out=p.stream[:, :].rearrange("p (i d) -> p i d", d=D),
        in_=src_ap.rearrange("(i p) d -> p i d", p=128)), 16)


def emit_stream_store(p: PB, dst_ap):
    return p.inc("ds", p.sy.dma_start(
        out=dst_ap.rearrange("(i p) d -> p i d", p=128),
        in_=p.stream[:, :].rearrange("p (i d) -> p i d", d=D)), 16)


def emit_ln(p: PB, mode, lnw_view=None):
    stg_ring = Ring([p.ptbuf[:, 0:2048], p.ptbuf[:, 2048:4096]])
    trseq = TrSeq(p)
    prev_act_read = None
    for i in range(NT):
        st = p.stream[:, i * D:(i + 1) * D]
        tv = p.inc("eV", p.ve.tensor_reduce(p.stat[:, 0:1], st, AX.X, ALU.add))
        p.ws(prev_act_read)
        p.se.activation(
            p.trash[:, 0:1024], st[:, 0:1024], AF.Square, bias=0.0, scale=1.0,
            accum_out=p.statS[:, 0:1])
        ts_ = p.inc("eS", p.se.activation(
            p.trash[:, 0:1024], st[:, 1024:2048], AF.Square, bias=0.0, scale=1.0,
            accum_out=p.statS[:, 1:2]))
        p.wv(tv, ts_)
        t1 = p.inc("eV", p.ve.tensor_scalar_mul(p.mcol[:, i:i + 1], p.stat[:, 0:1], 1.0 / D))
        t2 = p.inc("eV", p.ve.tensor_tensor(p.stat[:, 2:3], p.statS[:, 0:1], p.statS[:, 1:2], ALU.add))
        p.wv(t2)
        t3 = p.inc("eV", p.ve.tensor_scalar_mul(p.stat[:, 2:3], p.stat[:, 2:3], 1.0 / D))
        prev_act_read = p.tok("eV")
        p.wv(t1)
        t4 = p.inc("eV", p.ve.tensor_tensor(
            p.stat[:, 3:4], p.mcol[:, i:i + 1], p.mcol[:, i:i + 1], ALU.mult))
        p.wv(t3, t4)
        t5 = p.inc("eV", p.ve.tensor_tensor(
            p.stat[:, 4:5], p.stat[:, 2:3], p.stat[:, 3:4], ALU.subtract))
        p.wv(t5)
        if mode == "l1attn":
            tden = p.inc("eV", p.ve.tensor_scalar(
                p.stat[:, 5:6], p.stat[:, 4:5], 1.0 + 1e-6, 1e-11, ALU.mult, ALU.add))
        else:
            tden = p.inc("eV", p.ve.tensor_scalar(
                p.stat[:, 5:6], p.stat[:, 4:5], 1.0, 1e-5, ALU.mult, ALU.add))
        p.ws(tden)
        tsq = p.inc("eS", p.se.sqrt(p.statS[:, 2:3], p.stat[:, 5:6]))
        p.wv(tsq)
        trec = p.inc("eV", p.ve.reciprocal(p.acol[:, i:i + 1], p.statS[:, 2:3]))
        p.wv(trec)
        sj, stg = stg_ring.next()
        p.wv(stg_ring.free[sj])
        if mode == "l1attn":
            p.inc("eV", p.ve.tensor_scalar(
                p.scr32[:, 0:2048], st, p.mcol[:, i:i + 1], p.acol[:, i:i + 1],
                ALU.subtract, ALU.mult))
            txn = p.inc("eV", p.ve.tensor_tensor(
                stg, p.scr32[:, 0:2048], lnw_view, ALU.mult))
        else:
            txn = p.inc("eV", p.ve.tensor_scalar(
                stg, st, p.mcol[:, i:i + 1], p.acol[:, i:i + 1],
                ALU.subtract, ALU.mult))
        stg_ring.ready[sj] = txn
        for kc in range(KC):
            trseq.transpose(
                stg[:, kc * 128:(kc + 1) * 128],
                p.abuf[:, kc * 1024 + i * 128: kc * 1024 + (i + 1) * 128],
                extra_dep=(txn,))
        stg_ring.free[sj] = p.tok("eS")
    return p.tok("eS")


def emit_ffn0(p: PB, first_time=True):
    HT = p.bigbuf
    gate, up = p.inputs["l0_gate"], p.inputs["l0_up"]
    dd = p.ddown

    evac_done = None
    stream_evac = [None] * NT
    slot_last = [None, None]
    for half in range(2):
        gu_free = list(slot_last)
        psum_free = None
        for j in range(22):
            m = half * 22 + j
            k = j % 2
            gslot = p.wbuf[:, k * 4096:k * 4096 + 2048]
            uslot = p.wbuf[:, k * 4096 + 2048:k * 4096 + 4096]
            p.wp(gu_free[k])
            tg = p.inc("dg", p.pe.dma_start(
                out=gslot.rearrange("p (kc m) -> p kc m", m=128), in_=gate[m]), 16)
            tu = p.inc("dg", p.pe.dma_start(
                out=uslot.rearrange("p (kc m) -> p kc m", m=128), in_=up[m]), 16)
            p.wt(tg, tu, psum_free, evac_done)
            mm = None
            for ng in range(2):
                for kc in range(KC):
                    mm = p.te.matmul(
                        p.psA[:, ng * 512:(ng + 1) * 512],
                        gslot[:, kc * 128:(kc + 1) * 128],
                        p.abuf[:, kc * 1024 + ng * 512: kc * 1024 + (ng + 1) * 512],
                        start=(kc == 0), stop=(kc == KC - 1))
            tG = p.inc("eT", mm)
            for ng in range(2):
                for kc in range(KC):
                    mm = p.te.matmul(
                        p.psB[:, ng * 512:(ng + 1) * 512],
                        uslot[:, kc * 128:(kc + 1) * 128],
                        p.abuf[:, kc * 1024 + ng * 512: kc * 1024 + (ng + 1) * 512],
                        start=(kc == 0), stop=(kc == KC - 1))
            tU = p.inc("eT", mm)
            gu_free[k] = tU
            gstage = p.ptbuf[:, 0:1024]
            p.ws(tG, psum_free)
            tsg = p.inc("eS", p.se.activation(
                gstage, p.psA[:, 0:1024], AF.Silu, bias=0.0,
                scale=p.gsb[:, m:m + 1]))
            p.wv(tU, tsg)
            th = p.inc("eV", p.ve.scalar_tensor_tensor(
                HT[:, j * 1024:(j + 1) * 1024], p.psB[:, 0:1024],
                p.usb[:, m:m + 1], gstage, ALU.mult, ALU.mult))
            psum_free = th
        h_done = p.tok("eV")
        dn_free = [p.tok("eT"), p.tok("eT")]
        tmm = None
        for tt in range(NT):
            for j in range(22):
                k = j % 2
                dslot = p.wbuf[:, k * 4096:k * 4096 + 2048]
                p.wy(dn_free[k])
                tload = p.inc("ds", p.sy.dma_start(out=dslot, in_=dd[half * 22 + j]), 16)
                p.wt(tload, h_done)
                if j == 0:
                    p.wt(evac_done)
                mm = None
                for ng in range(4):
                    pst = p.psA if ng < 2 else p.psB
                    mm = p.te.matmul(
                        pst[:, (ng % 2) * 512:(ng % 2 + 1) * 512],
                        HT[:, j * 1024 + tt * 128: j * 1024 + (tt + 1) * 128],
                        dslot[:, ng * 512:(ng + 1) * 512],
                        start=(j == 0), stop=(j == 21))
                tmm = p.inc("eT", mm)
                dn_free[k] = tmm
                slot_last[k] = tmm
            p.wv(tmm)
            if half == 1:
                p.wv(stream_evac[tt])
            for ngg in range(2):
                pst = p.psA if ngg == 0 else p.psB
                ssl = slice(tt * D + ngg * 1024, tt * D + (ngg + 1) * 1024)
                p.inc("eV", p.ve.tensor_tensor(
                    p.stream[:, ssl], p.stream[:, ssl], pst[:, 0:1024], ALU.add))
            evac_done = p.tok("eV")
            stream_evac[tt] = evac_done
        p.wt(p.tok("eT"))
    return p.tok("eV")


def emit_down_precast(p: PB):
    down = p.inputs["l0_down"]
    dd = p.ddown
    tds = p.inc("ds", p.sy.dma_start(out=p.scr32[:, 0:2048], in_=p.inputs["l0_ds"][:, :]), 16)
    slot_free = [None, None]
    for m in range(IC):
        k = m % 2
        slot = p.wbuf[:, k * 4096:k * 4096 + 2048]
        slot2 = p.wbuf[:, k * 4096 + 2048:k * 4096 + 4096]
        p.wp(slot_free[k])
        tl = p.inc("dg", p.pe.dma_start(out=slot, in_=down[m]), 16)
        p.wv(tl, tds)
        tsc = p.inc("eV", p.ve.tensor_tensor(slot2, slot, p.scr32[:, 0:2048], ALU.mult))
        p.wy(tsc)
        slot_free[k] = p.inc("ds", p.sy.dma_start(out=dd[m], in_=slot2), 16)
    return p.tok("ds")



import math

QT_OFF = 0
KT_OFF = 16384
VR_OFF = 20480


def attn_params(l):
    if l == 0:
        return dict(G=G0, HD=HD0, H=H0, scale=1.0 / 8.0,
                    Wv=8 * 65, vstride=65, ones_col=True,
                    wq="l0_wq", wk="l0_wk", wv="l0_wv", wo="l0_wo",
                    bq="l0_bq", bk="l0_bk", bv="l0_bv", bo="l0_bo")
    return dict(G=G1, HD=HD1, H=H1, scale=1.0 / math.sqrt(128.0),
                Wv=4 * 128 + 8, vstride=128, ones_col=False,
                wq="l1_wq", wk="l1_wk", wv="l1_wv", wo="l1_wo",
                bq="l1_bq", bk="l1_bk", bv="l1_bv", bo=None)


def emit_qkv(p: PB, l):
    pr = attn_params(l)
    G, HD, H = pr["G"], pr["HD"], pr["H"]
    hw = HD // 2
    cosb = p.cos0b if l == 0 else p.cos1b
    sinb = p.sin0b if l == 0 else p.sin1b
    tb1 = p.inc("dg", p.pe.dma_start(out=p.birow[0:1, 0:2048], in_=p.inputs[pr["bq"]][:, :]), 16)
    tb2 = p.inc("dg", p.pe.dma_start(out=p.birow[0:1, 2048:2560], in_=p.inputs[pr["bk"]][:, :]), 16)
    tb3 = p.inc("dg", p.pe.dma_start(out=p.birow[0:1, 2560:3072], in_=p.inputs[pr["bv"]][:, :]), 16)
    tones = None
    if pr["ones_col"]:
        tones = p.inc("eV", p.ve.memset(
            p.bigbuf[:, VR_OFF:VR_OFF + 8 * pr["Wv"]]
            .rearrange("p (t g dd) -> p t g dd", g=G, dd=65)[:, :, :, 64:65], 1.0))
    else:
        tones = p.inc("eV", p.ve.memset(
            p.bigbuf[:, VR_OFF:VR_OFF + 8 * pr["Wv"]]
            .rearrange("p (t w) -> p t w", w=pr["Wv"])[:, :, 512:520], 1.0))
    tqn = p.inc("eV", p.ve.memset(p.qn[:, 0:H], 0.0))
    p.wv(tqn)

    trseq = TrSeq(p)
    pslots = [p.psA[:, 0:256], p.psB[:, 0:256], p.psC[:, 0:256]]
    slot_read = [None, None, None]
    sidx = 0
    stg_slots = [p.ptbuf[:, 2048 + 256 * k:2048 + 256 * (k + 1)] for k in range(2)]
    stg_free = [None, None]
    wslots = [p.wbuf[:, 0:4096], p.wbuf[:, 4096:8192]]
    w_free = [None, None]
    p_scnt = 0
    projs = [("q", pr["wq"], 8, 0), ("k", pr["wk"], 2, 2048), ("v", pr["wv"], 2, 2560)]
    for proj, wname, nslab, boff in projs:
        W = p.inputs[wname]
        for s in range(nslab):
            wk_ = p_scnt % 2
            p_scnt += 1
            wslot = wslots[wk_]
            p.wp(w_free[wk_])
            twl = p.inc("dg", p.pe.dma_start(
                out=wslot.rearrange("p (kc n) -> p kc n", n=256), in_=W[s]), 16)
            for ti in range(NT):
                j = sidx % 3
                ps = pslots[j]
                sidx += 1
                p.wt(slot_read[j], twl, tb1, tb2, tb3)
                for kc in range(KC):
                    p.te.matmul(ps, p.abuf[:, kc * 1024 + ti * 128:kc * 1024 + (ti + 1) * 128],
                                wslot[:, kc * 256:(kc + 1) * 256],
                                start=(kc == 0), stop=False)
                tmm = p.inc("eT", p.te.matmul(
                    ps, p.onesc[0:1, 0:128], p.birow[0:1, boff + s * 256:boff + s * 256 + 256],
                    start=False, stop=True))
                w_free[wk_] = tmm
                if proj == "v":
                    p.wv(tmm, tones)
                    if pr["ones_col"]:
                        dst = (p.bigbuf[:, VR_OFF + ti * pr["Wv"]:VR_OFF + (ti + 1) * pr["Wv"]]
                               .rearrange("p (g dd) -> p g dd", dd=65)[:, 4 * s:4 * s + 4, 0:64])
                        srcv = ps.rearrange("p (g dd) -> p g dd", dd=64)
                    else:
                        dst = p.bigbuf[:, VR_OFF + ti * pr["Wv"] + s * 256:VR_OFF + ti * pr["Wv"] + s * 256 + 256]
                        srcv = ps
                    tev = p.inc("eV", p.ve.tensor_copy(dst, srcv))
                    slot_read[j] = tev
                else:
                    nh = 256 // HD
                    C3 = cosb[:, ti * hw:(ti + 1) * hw].unsqueeze(1).broadcast_to([128, nh, hw])
                    S3 = sinb[:, ti * hw:(ti + 1) * hw].unsqueeze(1).broadcast_to([128, nh, hw])
                    ps3 = ps.rearrange("p (h w) -> p h w", w=HD)
                    psev, psod = ps3[:, :, 0:hw], ps3[:, :, hw:HD]
                    t1 = p.scr32[:, 2048:2176].rearrange("p (h w) -> p h w", w=hw)[:, 0:nh]
                    t2 = p.scr32[:, 2176:2304].rearrange("p (h w) -> p h w", w=hw)[:, 0:nh]
                    sk = sidx % 2
                    stg = stg_slots[sk]
                    stg3 = stg.rearrange("p (h w) -> p h w", w=HD)
                    p.wv(tmm, stg_free[sk])
                    p.inc("eV", p.ve.tensor_tensor(t1, psev, C3, ALU.mult))
                    i2 = p.inc("eV", p.ve.tensor_tensor(t2, psod, S3, ALU.mult))
                    p.wv(i2)
                    i3 = p.inc("eV", p.ve.tensor_tensor(stg3[:, :, 0:hw], t1, t2, ALU.subtract))
                    p.inc("eV", p.ve.tensor_tensor(t1, psod, C3, ALU.mult))
                    i5 = p.inc("eV", p.ve.tensor_tensor(t2, psev, S3, ALU.mult))
                    p.wv(i5)
                    i6 = p.inc("eV", p.ve.tensor_tensor(stg3[:, :, hw:HD], t1, t2, ALU.add))
                    slot_read[j] = i5
                    p.ws(i3, i6)
                    sq = p.inc("eS", p.se.activation(p.trash[:, 0:256], stg, AF.Square,
                                                     bias=0.0, scale=1.0))
                    p.wv(sq)
                    if proj == "q":
                        tr_ = p.inc("eV", p.ve.tensor_reduce(
                            p.stat[:, 0:nh], p.trash[:, 0:256].rearrange("p (h w) -> p h w", w=HD),
                            AX.X, ALU.add))
                        p.wv(tr_)
                        hb = s * nh
                        tmx = p.inc("eV", p.ve.tensor_tensor(
                            p.qn[:, hb:hb + nh], p.qn[:, hb:hb + nh], p.stat[:, 0:nh], ALU.max))
                        p.wv(tmx)
                    else:
                        gb = s * nh
                        tr_ = p.inc("eV", p.ve.tensor_reduce(
                            p.l1tmp[:, ti * 8 + gb:ti * 8 + gb + nh],
                            p.trash[:, 0:256].rearrange("p (h w) -> p h w", w=HD),
                            AX.X, ALU.add))
                        p.wv(tr_)
                    for jj in range(2):
                        kc_out = s * 2 + jj
                        base = QT_OFF if proj == "q" else KT_OFF
                        trseq.transpose(
                            stg[:, jj * 128:(jj + 1) * 128],
                            p.bigbuf[:, base + kc_out * 1024 + ti * 128:base + kc_out * 1024 + (ti + 1) * 128],
                            extra_dep=(i3, i6))
                    stg_free[sk] = p.tok("eS")
    return trseq


def emit_gather(p: PB, l, trseq):
    pr = attn_params(l)
    G, HD, H = pr["G"], pr["HD"], pr["H"]
    rep = H // G
    cc_in = p.cc_in[l]
    cc_out = p.cc_out[l]
    Wv0 = pr["Wv"]
    tz = p.inc("eV", p.ve.memset(p.trash[:, 0:1024], 0.0))
    p.wy(tz)
    for r0 in range(520, 1544, 128):
        r1 = min(r0 + 128, 1544)
        p.inc("ds", p.sy.dma_start(out=cc_in[r0:r1, Wv0:1040], in_=p.trash[0:r1 - r0, 0:1040 - Wv0]), 16)
    for r0 in range(0, 520, 128):
        r1 = min(r0 + 128, 520)
        p.inc("ds", p.sy.dma_start(out=cc_in[r0:r1, 1024:1040], in_=p.trash[0:r1 - r0, 0:16]), 16)
    for r0 in range(12352, 13120, 128):
        p.inc("ds", p.sy.dma_start(out=cc_out[r0:r0 + 128, 0:1024], in_=p.trash[:, 0:1024]), 16)
        p.inc("ds", p.sy.dma_start(out=cc_out[r0:r0 + 128, 1024:1040], in_=p.trash[:, 0:16]), 16)
    p.wv(p.tok("ds"))
    for ti in range(NT):
        tc_ = p.inc("eV", p.ve.tensor_copy(p.trash[:, 256:256 + 8], p.l1tmp[:, ti * 8:ti * 8 + 8]))
        p.wv(tc_)
        trseq.transpose(p.trash[:, 256:264], p.knrow[0:8, ti * 128:(ti + 1) * 128],
                        extra_dep=(tc_,), rows=8)
    knt = p.tok("eS")
    for kc in range(4):
        p.wy(p.tok("eS"))
        p.inc("ds", p.sy.dma_start(
            out=cc_in[kc * 128:(kc + 1) * 128, 0:1024],
            in_=p.bigbuf[:, KT_OFF + kc * 1024:KT_OFF + (kc + 1) * 1024]), 16)
    p.wy(knt)
    p.inc("ds", p.sy.dma_start(out=cc_in[512:520, 0:1024], in_=p.knrow[0:8, :]), 16)
    Wv = pr["Wv"]
    for ti in range(NT):
        p.inc("ds", p.sy.dma_start(
            out=cc_in[520 + 128 * ti:520 + 128 * (ti + 1), 0:Wv],
            in_=p.bigbuf[:, VR_OFF + ti * Wv:VR_OFF + (ti + 1) * Wv]), 16)
    p.wp(p.tok("ds"), p.tok("eV"), p.tok("eS"))
    tcc = p.inc("cc", p.pe.collective_compute(
        "AllGather", ALU.bypass, replica_groups=[list(range(8))],
        ins=[cc_in[:, :].opt()], outs=[cc_out[0:12352, :].opt()]))
    p.cc_done = tcc

    tq0 = p.inc("eV", p.ve.tensor_copy(p.trash[:, 256:256 + H], p.qn[:, 0:H]))
    p.wv(tq0)
    tq1 = trseq.transpose(p.trash[:, 256:256 + H], p.ptbuf[0:H, 0:128], rows=H, extra_dep=(tq0,))
    p.wv(tq1)
    tq2 = p.inc("eV", p.ve.tensor_reduce(p.stat[0:H, 0:1], p.ptbuf[0:H, 0:128], AX.X, ALU.max))
    p.wv(tq2)
    tq3 = p.inc("eV", p.ve.tensor_copy(p.trash[0:H, 256:257], p.stat[0:H, 0:1]))
    p.wv(tq3)
    tq4 = trseq.transpose(p.trash[0:H, 256:257], p.smallf[0:1, 0:H], rows=1, extra_dep=(tq3,))
    p.wp(tcc)
    p.wy(tcc)
    tkm = None
    for r in range(NCORE):
        p.wy(tkm)
        tld = p.inc("ds", p.sy.dma_start(
            out=p.knrow[0:8, :], in_=cc_out[r * 1544 + 512:r * 1544 + 520, 0:1024]), 16)
        p.wv(tld)
        tr_ = p.inc("eV", p.ve.tensor_reduce(p.stat[0:8, 1:2], p.knrow[0:8, :], AX.X, ALU.max))
        p.wv(tr_)
        if r == 0:
            tkm = p.inc("eV", p.ve.tensor_copy(p.stat[0:8, 2:3], p.stat[0:8, 1:2]))
        else:
            tkm = p.inc("eV", p.ve.tensor_tensor(
                p.stat[0:8, 2:3], p.stat[0:8, 2:3], p.stat[0:8, 1:2], ALU.max))
        p.wv(tkm)
    tk3 = p.inc("eV", p.ve.tensor_copy(p.trash[0:8, 256:257], p.stat[0:8, 2:3]))
    p.wv(tk3)
    tk4 = trseq.transpose(p.trash[0:8, 256:257], p.smallf[0:1, 32:40], rows=1, extra_dep=(tk3,))
    p.wv(tq4, tk4)
    kr = p.smallf[0:1, 32:32 + G].unsqueeze(2).broadcast_to([1, G, rep])
    tn1 = p.inc("eV", p.ve.tensor_tensor(
        p.smallf[0:1, 64:64 + H].rearrange("o (g r) -> o g r", r=rep),
        p.smallf[0:1, 0:H].rearrange("o (g r) -> o g r", r=rep), kr, ALU.add))
    p.wv(tn1)
    tn2 = p.inc("eV", p.ve.tensor_scalar_mul(
        p.smallf[0:1, 96:96 + H], p.smallf[0:1, 64:64 + H], -1.02 * pr["scale"] * 0.5))
    p.wv(tn2)
    tn3 = p.inc("eV", p.ve.tensor_copy(p.trash[0:1, 256:256 + H], p.smallf[0:1, 96:96 + H]))
    p.wt(tn3)
    tn4 = p.inc("eT", p.te.matmul(p.psA[:, 0:H], p.onesc[0:1, 0:128],
                                  p.trash[0:1, 256:256 + H], start=True, stop=True))
    p.wv(tn4)
    tn5 = p.inc("eV", p.ve.tensor_copy(p.negb[:, 0:H], p.psA[:, 0:H]))
    p.wv(tn5)
    return tn5


def emit_attn_loop(p: PB, l):
    pr = attn_params(l)
    G, HD, H = pr["G"], pr["HD"], pr["H"]
    rep = H // G
    cc_out = p.cc_out[l]
    Wv = pr["Wv"]
    kslots = [p.kvbuf[:, 0:512], p.kvbuf[:, 512:1024]]
    vslot_w = 544
    vslots = [p.kvbuf[:, 1024 + k * vslot_w:1024 + k * vslot_w + Wv] for k in range(2)]
    ptslots = [p.ptbuf[:, 512 * k:512 * (k + 1)] for k in range(4)]
    k_free = [None, None]
    v_free = [None, None]
    pt_free = [None, None, None, None]
    sA_read, sB_read = None, None
    o_read = [None, None]
    nbatch = 1 if l == 0 else 2
    hpb = 4 if l == 0 else 2
    kcnt, vcnt, ptcnt = 0, 0, 0
    ev_scr_read = None
    if not hasattr(p, "cc_flat"):
        p.cc_flat = {}
    p.cc_flat[l] = cc_out.rearrange("r (c e) -> (r c) e", e=8)
    for qhalf in range(2):
        NIT = 8 if qhalf == 0 else 16
        qsl = slice(qhalf * 512, qhalf * 512 + 512)
        p.wy(p.tok("dg"))
        p.kvib_ready = p.inc("ds", p.sy.dma_start(
            out=p.kvib[:, :], in_=p.inputs["kvi"][l, qhalf]), 16)
        p.wp(p.kvib_ready)
        for g in range(G):
            for hb in range(nbatch):
                heads = [g * rep + hb * hpb + i for i in range(hpb)]
                if l == 0:
                    ovs = [p.psC[0:65, 0:512], p.psC[0:65, 512:1024],
                           p.psD[0:65, 0:512], p.psD[0:65, 512:1024]]
                    lvs = [None] * 4
                else:
                    ovs = [p.psC[:, 0:512], p.psC[:, 512:1024]]
                    lvs = [p.psD[0:1, 0:512], p.psD[0:1, 512:1024]]
                for it in range(NIT):
                    kk = kcnt % 2
                    kcnt += 1
                    p.wp(k_free[kk], p.kvib_ready)
                    tkl = p.inc("dg", p.pe.indirect_dma_start(
                        kslots[kk][:, :], None, p.cc_flat[l],
                        bass.IndirectOffsetOnAxis(p.kvib[:, it * G + g:it * G + g + 1], 0)), 16)
                    for j in range(4):
                        vv = vcnt % 2
                        vcnt += 1
                        p.wp(v_free[vv])
                        tvl = p.inc("dg", p.pe.indirect_dma_start(
                            vslots[vv], None, p.cc_flat[l],
                            bass.IndirectOffsetOnAxis(
                                p.kvib[:, 16 * G + it * 4 + j:16 * G + it * 4 + j + 1], 0)), 16)
                        for hi, h in enumerate(heads):
                            use_a = (ptcnt % 2 == 0)
                            sps = p.psA[:, 0:512] if use_a else p.psB[:, 0:512]
                            p.wt(tkl, sA_read if use_a else sB_read)
                            qrow = (h % 2) * 64 if l == 0 else 0
                            qchunk = h // 2 if l == 0 else h
                            tsm = p.inc("eT", p.te.matmul(
                                sps, kslots[kk][qrow:qrow + HD, j * 128:(j + 1) * 128],
                                p.bigbuf[qrow:qrow + HD,
                                         QT_OFF + qchunk * 1024 + qhalf * 512:
                                         QT_OFF + qchunk * 1024 + qhalf * 512 + 512],
                                start=True, stop=True))
                            pt = ptslots[ptcnt % 4]
                            ptk = ptcnt % 4
                            ptcnt += 1
                            p.ws(tsm, pt_free[ptk])
                            texp = p.inc("eS", p.se.activation(
                                pt, sps, AF.Exp, bias=p.negb[:, h:h + 1], scale=pr["scale"]))
                            if use_a:
                                sA_read = texp
                            else:
                                sB_read = texp
                            tmask = None
                            if it == 0:
                                p.wv(texp)
                                tmask = p.inc("eV", p.ve.tensor_tensor(
                                    pt, pt, p.maskb[:, j * 512:(j + 1) * 512], ALU.mult))
                            p.wt(texp, tmask, tvl)
                            if it == 0 and j == 0:
                                p.wt(o_read[0], o_read[1])
                            vls = vslots[vv][:, g * pr["vstride"]:g * pr["vstride"] + (65 if l == 0 else 128)]
                            tav = p.inc("eT", p.te.matmul(
                                ovs[hi], vls, pt,
                                start=(it == 0 and j == 0), stop=(it == NIT - 1 and j == 3)))
                            pt_free[ptk] = tav
                            if l == 1:
                                tl_ = p.inc("eT", p.te.matmul(
                                    lvs[hi], vslots[vv][:, 512:513], pt,
                                    start=(it == 0 and j == 0), stop=(it == NIT - 1 and j == 3)))
                                pt_free[ptk] = tl_
                            k_free[kk] = p.tok("eT")
                            v_free[vv] = p.tok("eT")
                tavl = p.tok("eT")
                for hi, h in enumerate(heads):
                    if l == 0:
                        lrow = ovs[hi][64:65, :]
                        orows = ovs[hi][0:64, :]
                        dst = p.abuf[(h % 2) * 64:(h % 2) * 64 + 64,
                                     (h // 2) * 1024 + qhalf * 512:(h // 2) * 1024 + qhalf * 512 + 512]
                        drows = 64
                    else:
                        lrow = lvs[hi]
                        orows = ovs[hi]
                        dst = p.abuf[:, h * 1024 + qhalf * 512:h * 1024 + qhalf * 512 + 512]
                        drows = 128
                    p.wv(tavl)
                    tr1 = p.inc("eV", p.ve.reciprocal(p.linv[0:1, 0:512], lrow))
                    p.wt(tr1, sA_read, sB_read)
                    tb = p.inc("eT", p.te.matmul(
                        p.psA[0:drows, 0:512], p.onesf[0:1, 0:drows],
                        p.linv[0:1, 0:512], start=True, stop=True))
                    p.ws(tb, ev_scr_read)
                    tcp = p.inc("eS", p.se.copy(p.scr32[0:drows, 0:512], p.psA[0:drows, 0:512]))
                    sA_read = tcp
                    p.wv(tcp)
                    tev = p.inc("eV", p.ve.tensor_tensor(dst, orows, p.scr32[0:drows, 0:512], ALU.mult))
                    ev_scr_read = tev
                    if l == 0:
                        o_read[hi // 2] = tev
                    else:
                        o_read[0] = tev
                        o_read[1] = tev
    return p.tok("eV")


def emit_outproj(p: PB, l):
    pr = attn_params(l)
    W = p.inputs[pr["wo"]]
    has_bias = pr["bo"] is not None
    if has_bias:
        tbo = p.inc("dg", p.pe.dma_start(out=p.birow[0:1, 0:2048], in_=p.inputs[pr["bo"]][:, :]), 16)
    else:
        tbo = None
    wslots = [p.wbuf[:, 0:4096], p.wbuf[:, 4096:8192]]
    w_free = [None, None]
    pslots = [p.psA[:, 0:256], p.psB[:, 0:256], p.psC[:, 0:256]]
    slot_read = [None, None, None]
    sidx = 0
    for s in range(8):
        wk_ = s % 2
        wslot = wslots[wk_]
        p.wp(w_free[wk_])
        twl = p.inc("dg", p.pe.dma_start(
            out=wslot.rearrange("p (kc n) -> p kc n", n=256), in_=W[s]), 16)
        for ti in range(NT):
            j = sidx % 3
            ps = pslots[j]
            sidx += 1
            p.wt(slot_read[j], twl, tbo)
            mm = None
            for kc in range(KC):
                mm = p.te.matmul(ps, p.abuf[:, kc * 1024 + ti * 128:kc * 1024 + (ti + 1) * 128],
                                 wslot[:, kc * 256:(kc + 1) * 256],
                                 start=(kc == 0), stop=(not has_bias and kc == KC - 1))
            if has_bias:
                mm = p.te.matmul(
                    ps, p.onesc[0:1, 0:128], p.birow[0:1, s * 256:s * 256 + 256],
                    start=False, stop=True)
            tmm = p.inc("eT", mm)
            w_free[wk_] = tmm
            p.wv(tmm)
            ssl = slice(ti * D + s * 256, ti * D + s * 256 + 256)
            tev = p.inc("eV", p.ve.tensor_tensor(
                p.stream[:, ssl], p.stream[:, ssl], ps, ALU.add))
            slot_read[j] = tev
    return p.tok("eV")



F1_SW, F1_LU, F1_MD, F1_DW, F1_LD, F1_BW = 0, 1024, 2048, 2176, 2304, 4352


def emit_ffn1(p: PB):
    bb = p.bigbuf
    tl = []
    for nm, off, rows, width in (("sw_t", F1_SW, 128, 1024), ("lu_t", F1_LU, 128, 1024),
                                 ("md_t", F1_MD, 128, 128), ("dw_t", F1_DW, 128, 128),
                                 ("ld_cat", F1_LD, 64, 2048), ("bw", F1_BW, 8, 2048)):
        src = p.inputs[nm]
        ap = src if src.ndim == 2 else src.rearrange("p a b -> p (a b)")
        tl.append(p.inc("dg", p.pe.dma_start(out=bb[0:rows, off:off + width], in_=ap), 16))
    ti1 = p.inc("eP", p.pe.iota(p.trash[0:8, 512:576], [[1, 8], [0, 8]],
                                base=0, channel_multiplier=-1,
                                allow_small_or_imprecise_dtypes=True))
    p.wv(ti1)
    te1 = p.inc("eV", p.ve.tensor_single_scalar(
        p.trash[0:8, 512:576], p.trash[0:8, 512:576], 0.0, ALU.is_equal))
    p.wt(*tl)
    mm = None
    for n in range(8):
        for kc in range(KC):
            mm = p.te.matmul(p.psA[0:1, 8 * n:8 * n + 8],
                             bb[:, F1_MD + kc * 8 + n:F1_MD + kc * 8 + n + 1],
                             bb[:, F1_SW + kc * 64 + 8 * n:F1_SW + kc * 64 + 8 * n + 8],
                             start=(kc == 0), stop=(kc == KC - 1))
    t_wmd = p.inc("eT", mm)
    for n in range(8):
        for kc in range(KC):
            mm = p.te.matmul(p.psB[0:1, n:n + 1],
                             bb[:, F1_MD + kc * 8 + n:F1_MD + kc * 8 + n + 1],
                             bb[:, F1_MD + kc * 8 + n:F1_MD + kc * 8 + n + 1],
                             start=(kc == 0), stop=(kc == KC - 1))
    t_mdsq = p.inc("eT", mm)
    for n in range(8):
        for kc in range(KC):
            mm = p.te.matmul(p.psC[0:1, 8 * n:8 * n + 8],
                             bb[:, F1_DW + kc * 8 + n:F1_DW + kc * 8 + n + 1],
                             bb[:, F1_LU + kc * 64 + 8 * n:F1_LU + kc * 64 + 8 * n + 8],
                             start=(kc == 0), stop=(kc == KC - 1))
    t_cc = p.inc("eT", mm)
    p.wv(t_wmd, t_mdsq, t_cc)
    tw1 = p.inc("eV", p.ve.tensor_copy(p.smallf[0:1, 0:64], p.psA[0:1, 0:64]))
    tw2 = p.inc("eV", p.ve.tensor_copy(p.smallf[0:1, 64:72], p.psB[0:1, 0:8]))
    tw3 = p.inc("eV", p.ve.tensor_copy(p.trash[0:1, 576:640], p.psC[0:1, 0:64]))
    trseq = TrSeq(p)
    tc4 = trseq.transpose(p.trash[0:1, 576:640], p.ccol[0:64, 0:1], extra_dep=(tw3,))
    p.wv(tw1, tw2)
    tb1 = p.inc("eV", p.ve.tensor_copy(p.trash[0:1, 640:704], p.smallf[0:1, 0:64]))
    tb2 = p.inc("eV", p.ve.tensor_copy(p.trash[0:1, 704:712], p.smallf[0:1, 64:72]))
    p.wt(tb1, tb2, tw1, tw2)
    tm1 = p.inc("eT", p.te.matmul(p.psD[:, 0:64], p.onesc[0:1, 0:128],
                                  p.trash[0:1, 640:704], start=True, stop=True))
    tm2 = p.inc("eT", p.te.matmul(p.psD[:, 64:72], p.onesc[0:1, 0:128],
                                  p.trash[0:1, 704:712], start=True, stop=True))
    p.wv(tm1, tm2)
    tb3 = p.inc("eV", p.ve.tensor_copy(p.l1tmp[:, 0:64], p.psD[:, 0:64]))
    tb4 = p.inc("eV", p.ve.tensor_copy(p.negb[:, 0:8], p.psD[:, 64:72]))
    p.wv(tb3, tb4)

    Q0, VSQ, SCR, GAT, CMX, OH, MCOL = 0, 64, 72, 80, 88, 96, 104
    psAB_free = p.tok("eV")
    psD_free = p.tok("eV")
    psC_free = p.tok("eV")
    ohT_t = None
    st_upd = None
    for ti in range(NT):
        xsl = slice(ti * 128, (ti + 1) * 128)
        p.wt(psAB_free, st_upd)
        mm = None
        for kc in range(KC):
            lhs = p.abuf[:, kc * 1024 + ti * 128:kc * 1024 + (ti + 1) * 128]
            mm = p.te.matmul(p.psA[:, 0:64], lhs,
                             bb[:, F1_SW + kc * 64:F1_SW + (kc + 1) * 64],
                             start=(kc == 0), stop=(kc == KC - 1))
        tA = p.inc("eT", mm)
        for kc in range(KC):
            lhs = p.abuf[:, kc * 1024 + ti * 128:kc * 1024 + (ti + 1) * 128]
            mm = p.te.matmul(p.psA[:, 64:72], lhs,
                             bb[:, F1_MD + kc * 8:F1_MD + (kc + 1) * 8],
                             start=(kc == 0), stop=(kc == KC - 1))
        tB = p.inc("eT", mm)
        for kc in range(KC):
            mm = p.te.matmul(p.psB[0:64, 0:128],
                             bb[:, F1_LU + kc * 64:F1_LU + (kc + 1) * 64],
                             p.abuf[:, kc * 1024 + ti * 128:kc * 1024 + (ti + 1) * 128],
                             start=(kc == 0), stop=(kc == KC - 1))
        tC = p.inc("eT", mm)
        p.wv(tA, tB)
        t1 = p.inc("eV", p.ve.tensor_tensor(
            p.scr32[:, MCOL + 1:MCOL + 2], p.acol[:, ti:ti + 1], p.acol[:, ti:ti + 1], ALU.mult))
        p.wv(t1)
        t2 = p.inc("eV", p.ve.tensor_scalar(
            p.scr32[:, MCOL:MCOL + 1], p.scr32[:, MCOL + 1:MCOL + 2],
            -float(D) * 1e-5, float(D), ALU.mult, ALU.add))
        tq = p.inc("eV", p.ve.tensor_tensor(
            p.scr32[:, Q0:Q0 + 64], p.psA[:, 0:64], p.l1tmp[:, 0:64], ALU.subtract))
        p.ws(tq)
        tsq = p.inc("eS", p.se.activation(p.trash[:, 0:64], p.scr32[:, Q0:Q0 + 64],
                                          AF.Square, bias=0.0, scale=1.0))
        p.wv(tsq, t2)
        tr_ = p.inc("eV", p.ve.tensor_reduce(
            p.scr32[:, SCR:SCR + 8],
            p.trash[:, 0:64].rearrange("p (n s) -> p n s", s=8), AX.X, ALU.add))
        tv1 = p.inc("eV", p.ve.tensor_scalar(
            p.scr32[:, VSQ:VSQ + 8], p.psA[:, 64:72], -2.0,
            p.scr32[:, MCOL:MCOL + 1], ALU.mult, ALU.add))
        p.wv(tv1)
        tv2 = p.inc("eV", p.ve.tensor_tensor(
            p.scr32[:, VSQ:VSQ + 8], p.scr32[:, VSQ:VSQ + 8], p.negb[:, 0:8], ALU.add))
        psAB_free = tv2
        p.wv(tv2)
        tv3 = p.inc("eV", p.ve.reciprocal(p.scr32[:, OH:OH + 8], p.scr32[:, VSQ:VSQ + 8]))
        p.wv(tv3, tr_)
        tsc = p.inc("eV", p.ve.tensor_tensor(
            p.scr32[:, SCR:SCR + 8], p.scr32[:, SCR:SCR + 8], p.scr32[:, OH:OH + 8], ALU.mult))
        p.wv(tsc)
        tg1 = p.inc("eV", p.ve.tensor_tensor(
            p.scr32[:, GAT:GAT + 8], p.scr32[:, SCR:SCR + 8], p.cenb[:, :], ALU.subtract))
        p.wv(tg1)
        tg2 = p.inc("eV", p.ve.tensor_tensor(
            p.scr32[:, GAT:GAT + 8], p.scr32[:, GAT:GAT + 8], p.slob[:, :], ALU.mult))
        p.ws(tg2)
        tg3 = p.inc("eS", p.se.activation(p.scr32[:, GAT:GAT + 8], p.scr32[:, GAT:GAT + 8],
                                          AF.Sigmoid, bias=0.0, scale=1.0))
        p.wv(tg3)
        tmx = p.inc("eV", p.ve.tensor_reduce(
            p.scr32[:, MCOL + 2:MCOL + 3], p.scr32[:, GAT:GAT + 8], AX.X, ALU.max))
        tcm = p.inc("eV", p.ve.memset(p.scr32[:, CMX:CMX + 1], -1.0))
        for n in range(1, 8):
            p.wv(tcm)
            tcm = p.inc("eV", p.ve.tensor_tensor(
                p.scr32[:, CMX + n:CMX + n + 1], p.scr32[:, CMX + n - 1:CMX + n],
                p.scr32[:, GAT + n - 1:GAT + n], ALU.max))
        p.wv(tcm, tmx)
        to1 = p.inc("eV", p.ve.tensor_scalar(
            p.scr32[:, OH:OH + 8], p.scr32[:, GAT:GAT + 8],
            p.scr32[:, MCOL + 2:MCOL + 3], None, ALU.is_ge))
        to2 = p.inc("eV", p.ve.tensor_scalar(
            p.scr32[:, SCR:SCR + 8], p.scr32[:, CMX:CMX + 8],
            p.scr32[:, MCOL + 2:MCOL + 3], None, ALU.is_lt))
        p.wv(to1, to2)
        to3 = p.inc("eV", p.ve.tensor_tensor(
            p.scr32[:, OH:OH + 8], p.scr32[:, OH:OH + 8], p.scr32[:, SCR:SCR + 8], ALU.mult))
        p.wv(to3)
        tob = p.inc("eV", p.ve.tensor_copy(p.trash[:, 64:72], p.scr32[:, OH:OH + 8]))
        tohT = trseq.transpose(p.trash[:, 64:72], p.knrow[0:8, ti * 128:(ti + 1) * 128],
                               extra_dep=(tob,), rows=8)
        p.wt(tohT, te1, psC_free)
        trep = p.inc("eT", p.te.matmul(
            p.psC[0:64, 0:128], p.trash[0:8, 512:576],
            p.knrow[0:8, ti * 128:(ti + 1) * 128], start=True, stop=True))
        p.wv(tC, tc4)
        tmod = p.inc("eV", p.ve.tensor_scalar(
            p.scr32[0:64, 128:256], p.psB[0:64, 0:128], p.ccol[0:64, 0:1], None, ALU.subtract))
        p.wv(trep, tmod)
        trc = p.inc("eV", p.ve.tensor_copy(p.trash[0:64, 128:256], p.psC[0:64, 0:128]))
        psC_free = trc
        p.wv(trc)
        tmm2 = p.inc("eV", p.ve.tensor_tensor(
            p.ptbuf[0:64, 0:128], p.scr32[0:64, 128:256], p.trash[0:64, 128:256], ALU.mult))
        p.wt(tmm2, psD_free, psAB_free)
        for ng in range(4):
            pst = p.psD if ng < 2 else p.psA
            psl = pst[:, (ng % 2) * 512:(ng % 2 + 1) * 512]
            p.te.matmul(psl, p.ptbuf[0:64, 0:128],
                        bb[0:64, F1_LD + ng * 512:F1_LD + (ng + 1) * 512],
                        start=True, stop=False)
            mm = p.te.matmul(psl, p.knrow[0:8, ti * 128:(ti + 1) * 128],
                             bb[0:8, F1_BW + ng * 512:F1_BW + (ng + 1) * 512],
                             start=False, stop=True)
        tdg = p.inc("eT", mm)
        tS1 = p.inc("eV", p.ve.tensor_scalar(
            p.scr32[:, MCOL + 3:MCOL + 4], p.scr32[:, MCOL + 2:MCOL + 3],
            -1.0, 1.0, ALU.mult, ALU.add))
        txq = p.inc("eV", p.ve.tensor_scalar(
            p.scr32[:, 512:2560], p.stream[:, ti * D:(ti + 1) * D],
            p.mcol[:, ti:ti + 1], p.acol[:, ti:ti + 1], ALU.subtract, ALU.mult))
        p.wv(tdg, tS1, txq, st_upd)
        t_u1 = p.inc("eV", p.ve.scalar_tensor_tensor(
            p.stream[:, ti * D:ti * D + 1024], p.psD[:, 0:1024],
            p.scr32[:, MCOL + 2:MCOL + 3], p.stream[:, ti * D:ti * D + 1024],
            ALU.mult, ALU.add))
        p.wv(t_u1)
        t_u2 = p.inc("eV", p.ve.scalar_tensor_tensor(
            p.stream[:, ti * D + 1024:(ti + 1) * D], p.psA[:, 0:1024],
            p.scr32[:, MCOL + 2:MCOL + 3], p.stream[:, ti * D + 1024:(ti + 1) * D],
            ALU.mult, ALU.add))
        p.wv(t_u2)
        t_u3 = p.inc("eV", p.ve.scalar_tensor_tensor(
            p.stream[:, ti * D:(ti + 1) * D], p.scr32[:, 512:2560],
            p.scr32[:, MCOL + 3:MCOL + 4], p.stream[:, ti * D:(ti + 1) * D],
            ALU.mult, ALU.add))
        st_upd = t_u3
        psD_free = t_u1
        psAB_free = t_u2
    return p.tok("eV")



INPUT_SPECS = {
    "xin": (TC, D), "cos0": (TC, 32), "sin0": (TC, 32),
    "cos1": (TC, 64), "sin1": (TC, 64),
    "ident": (128, 128), "masks": (128, 2048),
    "l0_wq": (8, 128, 16, 256), "l0_wk": (2, 128, 16, 256),
    "l0_wv": (2, 128, 16, 256), "l0_wo": (8, 128, 16, 256),
    "l0_bq": (1, 2048), "l0_bk": (1, 512), "l0_bv": (1, 512), "l0_bo": (1, 2048),
    "l0_gate": (44, 128, 16, 128), "l0_up": (44, 128, 16, 128),
    "l0_down": (44, 128, 2048),
    "l0_gs": (128, 44), "l0_us": (128, 44), "l0_ds": (128, 2048),
    "l1_wq": (8, 128, 16, 256), "l1_wk": (2, 128, 16, 256),
    "l1_wv": (2, 128, 16, 256), "l1_wo": (8, 128, 16, 256),
    "l1_bq": (1, 2048), "l1_bk": (1, 512), "l1_bv": (1, 512),
    "l1_lnw": (128, 2048),
    "sw_t": (128, 16, 64), "lu_t": (128, 16, 64), "ld_cat": (64, 2048),
    "md_t": (128, 16, 8), "dw_t": (128, 16, 8), "bw": (8, 2048),
    "cen_bc": (128, 8), "slo_bc": (128, 8),
}


def declare_all(p: PB):
    for k, shp in INPUT_SPECS.items():
        p.din(k, shp, F32)
    p.din("kvi", (2, 2, 128, 192), mybir.dt.int32)


def build_full():
    p = PB(0)
    declare_all(p)
    yout = p.dout("yout", [TC, D], F32)
    L_alloc = alloc_common(p)
    p.ddown = p.dint("ddown", [44, 128, 2048], BF16)
    p.cc_in = {0: p.dint("cc_in0", [1544, 1040], BF16),
               1: p.dint("cc_in1", [1544, 1040], BF16)}
    p.cc_out = {0: p.dint("cc_out0", [13120, 1040], BF16),
                1: p.dint("cc_out1", [13120, 1040], BF16)}
    emit_prologue(p)
    emit_down_precast(p)
    emit_stream_load(p, p.inputs["xin"])
    p.join()
    emit_ln(p, "plain"); p.join()
    trseq = emit_qkv(p, 0); p.join()
    emit_gather(p, 0, trseq); p.join()
    emit_attn_loop(p, 0); p.join()
    emit_outproj(p, 0); p.join()
    emit_ln(p, "plain"); p.join()
    emit_ffn0(p); p.join()
    p.inc("dg", p.pe.dma_start(out=p.bigbuf[:, 16384:18432], in_=p.inputs["l1_lnw"][:, :]), 16)
    p.join()
    emit_ln(p, "l1attn", lnw_view=p.bigbuf[:, 16384:18432]); p.join()
    trseq = emit_qkv(p, 1); p.join()
    emit_gather(p, 1, trseq); p.join()
    emit_attn_loop(p, 1); p.join()
    emit_outproj(p, 1); p.join()
    emit_ln(p, "plain"); p.join()
    emit_ffn1(p); p.join()
    tst = emit_stream_store(p, yout)
    p.wy(tst)
    return p


_PROG = None


def kernel(**inputs):
    global _PROG
    g = host_prep(inputs)
    x_full = np.asarray(inputs["L0_attn_x"])[0]
    maps = per_core_inputs(g, x_full)
    if _PROG is None:
        _PROG = build_full()
    p = _PROG
    fmaps = [{k: mm[k] for k in p.inputs} for mm in maps]
    res = run_bass_kernel_spmd(p.nc, fmaps, core_ids=list(range(NCORE)))
    y = np.zeros((T, D), np.float32)
    for c in range(NCORE):
        y[core_token_rows(c)] = res.results[c]["yout"]
    return y.reshape(1, T, D)


# version 02
# speedup vs baseline: 1.4259x; 1.4259x over previous
import numpy as np
import ml_dtypes
import concourse.bass as bass
import concourse.mybir as mybir
from concourse.bass_utils import run_bass_kernel_spmd
from contextlib import ExitStack

F32 = mybir.dt.float32
BF16 = mybir.dt.bfloat16
AF = mybir.ActivationFunctionType
ALU = mybir.AluOpType
AX = mybir.AxisListType

D = 2048
T = 8192
TC = 1024
NT = 8
KC = 16
INTER = 5632
IC = 44
NCORE = 8
H0, G0, HD0 = 32, 8, 64
H1, G1, HD1 = 16, 4, 128
BF = ml_dtypes.bfloat16


def core_tiles(c):
    return [c, 15 - c]


def core_token_rows(c):
    t0, t1 = core_tiles(c)
    return np.concatenate([np.arange(512 * t0, 512 * t0 + 512),
                           np.arange(512 * t1, 512 * t1 + 512)])


def rope_perm(width, hd):
    p = []
    for h in range(width // hd):
        base = h * hd
        p += [base + 2 * i for i in range(hd // 2)]
        p += [base + 2 * i + 1 for i in range(hd // 2)]
    return np.array(p)


def host_prep(inp):
    g = {}
    f32 = np.float32
    perm_q = rope_perm(2048, 64)
    perm_k = rope_perm(512, 64)
    Wq = np.ascontiguousarray(inp["L0_attn_Wq"][:, perm_q])
    Wk = np.ascontiguousarray(inp["L0_attn_Wk"][:, perm_k])
    def rhs_slabs(W, nslab, nw):
        return np.ascontiguousarray(
            W.reshape(16, 128, nslab, nw).transpose(2, 1, 0, 3))
    g["l0_wq"] = rhs_slabs(Wq, 8, 256)
    g["l0_wk"] = rhs_slabs(Wk, 2, 256)
    g["l0_wv"] = rhs_slabs(inp["L0_attn_Wv"], 2, 256)
    g["l0_wo"] = rhs_slabs(inp["L0_attn_Wo"], 8, 256)
    g["l0_bq"] = np.ascontiguousarray(inp["L0_attn_bq"][perm_q].reshape(1, 2048))
    g["l0_bk"] = np.ascontiguousarray(inp["L0_attn_bk"][perm_k].reshape(1, 512))
    g["l0_bv"] = np.ascontiguousarray(inp["L0_attn_bv"].reshape(1, 512))
    g["l0_bo"] = np.ascontiguousarray(inp["L0_attn_bo"].reshape(1, 2048))
    gw, uw, dw = inp["L0_ffn_gate_w"], inp["L0_ffn_up_w"], inp["L0_ffn_down_w"]
    g["l0_gate"] = np.ascontiguousarray(gw.reshape(44, 128, 16, 128).transpose(0, 3, 2, 1))
    g["l0_up"] = np.ascontiguousarray(uw.reshape(44, 128, 16, 128).transpose(0, 3, 2, 1))
    g["l0_down"] = np.ascontiguousarray(dw.T.reshape(44, 128, 2048))
    g["l0_gs"] = np.ascontiguousarray(inp["L0_ffn_gate_s"].reshape(44, 128).T)
    g["l0_us"] = np.ascontiguousarray(inp["L0_ffn_up_s"].reshape(44, 128).T)
    g["l0_ds"] = np.ascontiguousarray(np.broadcast_to(inp["L0_ffn_down_s"], (128, 2048)))
    g["l1_wq"] = rhs_slabs(np.ascontiguousarray(inp["L1_attn_q_project"].T), 8, 256)
    g["l1_wk"] = rhs_slabs(np.ascontiguousarray(inp["L1_attn_k_project"].T), 2, 256)
    g["l1_wv"] = rhs_slabs(np.ascontiguousarray(inp["L1_attn_v_project"].T), 2, 256)
    g["l1_wo"] = rhs_slabs(np.ascontiguousarray(inp["L1_attn_o_project"].T), 8, 256)
    g["l1_bq"] = np.ascontiguousarray(inp["L1_attn_q_bias"].reshape(1, 2048))
    g["l1_bk"] = np.ascontiguousarray(inp["L1_attn_k_bias"].reshape(1, 512))
    g["l1_bv"] = np.ascontiguousarray(inp["L1_attn_v_bias"].reshape(1, 512))
    g["l1_lnw"] = np.ascontiguousarray(np.broadcast_to(inp["L1_attn_ln_weight"], (128, 2048)))
    sw, lu, ld = inp["L1_ffn_select_weight"], inp["L1_ffn_lora_update"], inp["L1_ffn_lora_degen"]
    md = inp["L1_ffn_select_mean_diff"]
    def cat_t(a):
        return np.ascontiguousarray(
            a.transpose(1, 0, 2).reshape(16, 128, 64).transpose(1, 0, 2))
    g["sw_t"] = cat_t(sw)
    g["lu_t"] = cat_t(lu)
    g["ld_cat"] = np.ascontiguousarray(ld.transpose(0, 2, 1).reshape(64, 2048))
    def colT(a):
        return np.ascontiguousarray(a.T.reshape(16, 128, 8).transpose(1, 0, 2))
    g["md_t"] = colT(md)
    g["dw_t"] = colT(inp["L1_ffn_debias_w"])
    g["bw"] = np.ascontiguousarray(inp["L1_ffn_bias_w"])
    g["cen_bc"] = np.ascontiguousarray(np.broadcast_to(inp["L1_ffn_imp_center"], (128, 8)))
    g["slo_bc"] = np.ascontiguousarray(np.broadcast_to(inp["L1_ffn_imp_slope"], (128, 8)))
    g["ident"] = np.eye(128, dtype=f32)
    kp = np.arange(128)[:, None]
    qf = np.arange(512)[None, :]
    g["masks"] = np.ascontiguousarray(np.concatenate(
        [(qf >= kp + 128 * j).astype(f32) for j in range(4)], axis=1))
    pos = np.arange(T, dtype=np.float64)
    inv0 = 10000.0 ** (-np.arange(0, 64, 2).astype(np.float64) / 64.0)
    inv1 = 10000.0 ** (-np.arange(0, 128, 2).astype(np.float64) / 128.0)
    ang0 = (pos[:, None] * inv0[None, :]).astype(f32)
    ang1 = (pos[:, None] * inv1[None, :]).astype(f32)
    g["cos0_full"] = np.cos(ang0).astype(f32)
    g["sin0_full"] = np.sin(ang0).astype(f32)
    g["cos1_full"] = np.cos(ang1).astype(f32)
    g["sin1_full"] = np.sin(ang1).astype(f32)
    return g


def kvi_table(c):
    t = np.zeros((2, 2, 128, 192), np.int32)
    ZROW = 12352
    pp = np.arange(128)
    for l in range(2):
        G = 8 if l == 0 else 4
        HD = 64 if l == 0 else 128
        for qh in range(2):
            t5top = c if qh == 0 else 15 - c
            for it in range(16):
                t5 = t5top - it
                if t5 >= 0:
                    rank = t5 if t5 < 8 else 15 - t5
                    half = 0 if t5 < 8 else 1
                    kbase, kcol = rank * 1544, half * 512
                    vbase = rank * 1544 + 520 + 512 * half
                else:
                    kbase, kcol, vbase = ZROW, 0, ZROW
                for g in range(G):
                    if it * G + g < 16 * G:
                        rowk = kbase + (0 if t5 < 0 else g * HD) + (pp % HD)
                        t[l, qh, :, it * G + g] = (rowk * 1040 + kcol) // 8
                for j in range(4):
                    rowv = vbase + (0 if t5 < 0 else 128 * j) + pp
                    t[l, qh, :, 16 * G + it * 4 + j] = (rowv * 1040) // 8
    return t


def per_core_inputs(g, x_full):
    maps = []
    for c in range(NCORE):
        rows = core_token_rows(c)
        m = {k: v for k, v in g.items() if not k.endswith("_full")}
        m["xin"] = np.ascontiguousarray(x_full[rows])
        m["kvi"] = kvi_table(c)
        m["cos0"] = np.ascontiguousarray(g["cos0_full"][rows])
        m["sin0"] = np.ascontiguousarray(g["sin0_full"][rows])
        m["cos1"] = np.ascontiguousarray(g["cos1_full"][rows])
        m["sin1"] = np.ascontiguousarray(g["sin1_full"][rows])
        maps.append(m)
    return maps



class PB:

    def __init__(self, core_id):
        self.c = core_id
        self.nc = bass.Bass("TRN2", target_bir_lowering=False, debug=False)
        self.ctx = ExitStack()
        self.cnt = {}
        self.sems = {}
        self.hw = {}
        self.inputs = {}

    def din(self, name, shape, dtype=F32):
        ap = self.nc.dram_tensor(name, list(shape), dtype, kind="ExternalInput").ap()
        self.inputs[name] = ap
        return ap

    def dout(self, name, shape, dtype=F32):
        return self.nc.dram_tensor(name, list(shape), dtype, kind="ExternalOutput").ap()

    def dint(self, name, shape, dtype=BF16, **kw):
        return self.nc.dram_tensor(name, list(shape), dtype, kind="Internal", **kw).ap()

    def sb(self, name, shape, dtype):
        return self.ctx.enter_context(self.nc.sbuf_tensor(name, list(shape), dtype))

    def psum(self, name, shape, dtype=F32):
        return self.ctx.enter_context(self.nc.psum_tensor(name, list(shape), dtype))

    def sem(self, name):
        s = self.ctx.enter_context(self.nc.semaphore(name))
        self.sems[name] = s
        self.cnt[name] = 0
        return s

    def inc(self, name, inst, k=1):
        inst.then_inc(self.sems[name], k)
        self.cnt[name] += k
        return (name, self.cnt[name])

    def tok(self, name):
        return (name, self.cnt[name])

    def wait_eng(self, eng, engname, toks):
        for t in toks:
            if t is None:
                continue
            name, v = t
            if v <= 0:
                continue
            key = (engname, name)
            if self.hw.get(key, 0) < v:
                eng.wait_ge(self.sems[name], v)
                self.hw[key] = v

    @property
    def te(self): return self.nc.tensor
    @property
    def ve(self): return self.nc.vector
    @property
    def se(self): return self.nc.scalar
    @property
    def pe(self): return self.nc.gpsimd
    @property
    def sy(self): return self.nc.sync

    def wt(self, *toks): self.wait_eng(self.te, "T", toks)
    def wv(self, *toks): self.wait_eng(self.ve, "V", toks)
    def ws(self, *toks): self.wait_eng(self.se, "S", toks)
    def wp(self, *toks): self.wait_eng(self.pe, "P", toks)
    def wy(self, *toks): self.wait_eng(self.sy, "Y", toks)

    def all_toks(self):
        return [self.tok(s) for s in self.sems]

    def join(self):
        toks = self.all_toks()
        self.wt(*toks); self.wv(*toks); self.ws(*toks); self.wp(*toks); self.wy(*toks)
        return toks


class Ring:

    def __init__(self, views):
        self.views = views
        self.ready = [None] * len(views)
        self.free = [None] * len(views)
        self.i = 0

    def next(self):
        j = self.i % len(self.views)
        self.i += 1
        return j, self.views[j]


def alloc_common(p: PB):
    for s in ("dg", "ds", "da", "cc", "eT", "eV", "eS", "eP"):
        p.sem(s)
    p.psA = p.psum("psA", [128, 1024])
    p.psB = p.psum("psB", [128, 1024])
    p.psC = p.psum("psC", [128, 1024])
    p.psD = p.psum("psD", [128, 1024])
    p.stream = p.sb("stream", [128, NT * D], F32)
    p.abuf = p.sb("abuf", [128, 16384], BF16)
    p.bigbuf = p.sb("bigbuf", [128, 24704], BF16)
    p.scr32 = p.sb("scr32", [128, 2560], F32)
    p.wbuf = p.sb("wbuf", [128, 8192], BF16)
    p.ptbuf = p.sb("ptbuf", [128, 4096], BF16)
    p.kvbuf = p.sb("kvbuf", [128, 2112], BF16)
    p.trash = p.sb("trash", [128, 1024], BF16)
    p.identb = p.sb("identb", [128, 128], BF16)
    p.maskb = p.sb("maskb", [128, 2048], BF16)
    p.cos0b = p.sb("cos0b", [128, 256], F32)
    p.sin0b = p.sb("sin0b", [128, 256], F32)
    p.cos1b = p.sb("cos1b", [128, 512], F32)
    p.sin1b = p.sb("sin1b", [128, 512], F32)
    p.mcol = p.sb("mcol", [128, 8], F32)
    p.acol = p.sb("acol", [128, 8], F32)
    p.stat = p.sb("stat", [128, 8], F32)
    p.statS = p.sb("statS", [128, 4], F32)
    p.birow = p.sb("birow", [1, 3072], BF16)
    p.onesc = p.sb("onesc", [1, 128], BF16)
    p.onesf = p.sb("onesf", [1, 128], F32)
    p.onek = p.sb("onek", [128, 2], BF16)
    p.qn = p.sb("qn", [128, 32], F32)
    p.negb = p.sb("negb", [128, 32], F32)
    p.knrow = p.sb("knrow", [8, 1024], BF16)
    p.smallf = p.sb("smallf", [1, 128], F32)
    p.linv = p.sb("linv", [1, 512], F32)
    p.lbc = p.scr32[:, 0:1024]
    p.gsb = p.sb("gsb", [128, 44], F32)
    p.usb = p.sb("usb", [128, 44], F32)
    p.cenb = p.sb("cenb", [128, 8], F32)
    p.slob = p.sb("slob", [128, 8], F32)
    p.l1tmp = p.sb("l1tmp", [128, 64], F32)
    p.ccol = p.sb("ccol", [64, 1], F32)
    p.kvib = p.sb("kvib", [128, 192], mybir.dt.int32)
    p.trviews = [p.psD[:, k * 64:(k + 1) * 64].bitcast(BF16) for k in range(16)]


TR_ORDER = [0, 8, 1, 9, 2, 10, 3, 11, 4, 12, 5, 13, 6, 14, 7, 15]


class TrSeq:

    def __init__(self, p):
        self.p = p
        self.k = 0
        self.view_free = [None] * 16
        self.hist = []

    def transpose(self, src, dst, extra_dep=(), rows=128):
        p = self.p
        P = src.shape[0]
        F = src.shape[-1]
        v = TR_ORDER[self.k % 16]
        trv = p.trviews[v][0:F, 0:P]
        if len(self.hist) >= 2:
            p.wt(self.hist[-2])
        p.wt(self.view_free[v], *extra_dep)
        tt = p.inc("eT", p.te.transpose(trv, src, p.identb[0:P, 0:P]))
        p.ws(tt)
        tev = p.inc("eS", p.se.copy(dst, trv))
        self.view_free[v] = tev
        self.hist.append(tev)
        self.k += 1
        return tev


def emit_prologue(p: PB):
    g = p.pe
    p.inc("dg", g.dma_start(out=p.identb[:, :], in_=p.inputs["ident"][:, :]), 16)
    p.inc("dg", g.dma_start(out=p.maskb[:, :], in_=p.inputs["masks"][:, :]), 16)
    for nm, dst, w in (("cos0", p.cos0b, 32), ("sin0", p.sin0b, 32),
                       ("cos1", p.cos1b, 64), ("sin1", p.sin1b, 64)):
        p.inc("ds", p.sy.dma_start(
            out=dst[:, :].rearrange("p (i w) -> p i w", w=w),
            in_=p.inputs[nm].rearrange("(i p) w -> p i w", p=128)), 16)
    p.inc("dg", g.dma_start(out=p.gsb[:, :], in_=p.inputs["l0_gs"][:, :]), 16)
    p.inc("dg", g.dma_start(out=p.usb[:, :], in_=p.inputs["l0_us"][:, :]), 16)
    p.inc("dg", g.dma_start(out=p.cenb[:, :], in_=p.inputs["cen_bc"][:, :]), 16)
    p.inc("dg", g.dma_start(out=p.slob[:, :], in_=p.inputs["slo_bc"][:, :]), 16)
    p.inc("eV", p.ve.memset(p.onesc[:, :], 1.0))
    p.inc("eV", p.ve.memset(p.onesf[:, :], 1.0))
    p.inc("eV", p.ve.memset(p.onek[:, :], 1.0))


def emit_stream_load(p: PB, src_ap):
    return p.inc("ds", p.sy.dma_start(
        out=p.stream[:, :].rearrange("p (i d) -> p i d", d=D),
        in_=src_ap.rearrange("(i p) d -> p i d", p=128)), 16)


def emit_stream_store(p: PB, dst_ap):
    return p.inc("ds", p.sy.dma_start(
        out=dst_ap.rearrange("(i p) d -> p i d", p=128),
        in_=p.stream[:, :].rearrange("p (i d) -> p i d", d=D)), 16)


def emit_ln(p: PB, mode, lnw_view=None):
    stg_ring = Ring([p.ptbuf[:, 0:2048], p.ptbuf[:, 2048:4096]])
    trseq = TrSeq(p)
    prev_act_read = None
    for i in range(NT):
        st = p.stream[:, i * D:(i + 1) * D]
        tv = p.inc("eV", p.ve.tensor_reduce(p.stat[:, 0:1], st, AX.X, ALU.add))
        p.ws(prev_act_read)
        p.se.activation(
            p.trash[:, 0:1024], st[:, 0:1024], AF.Square, bias=0.0, scale=1.0,
            accum_out=p.statS[:, 0:1])
        ts_ = p.inc("eS", p.se.activation(
            p.trash[:, 0:1024], st[:, 1024:2048], AF.Square, bias=0.0, scale=1.0,
            accum_out=p.statS[:, 1:2]))
        p.wv(tv, ts_)
        t1 = p.inc("eV", p.ve.tensor_scalar_mul(p.mcol[:, i:i + 1], p.stat[:, 0:1], 1.0 / D))
        t2 = p.inc("eV", p.ve.tensor_tensor(p.stat[:, 2:3], p.statS[:, 0:1], p.statS[:, 1:2], ALU.add))
        p.wv(t2)
        t3 = p.inc("eV", p.ve.tensor_scalar_mul(p.stat[:, 2:3], p.stat[:, 2:3], 1.0 / D))
        prev_act_read = p.tok("eV")
        p.wv(t1)
        t4 = p.inc("eV", p.ve.tensor_tensor(
            p.stat[:, 3:4], p.mcol[:, i:i + 1], p.mcol[:, i:i + 1], ALU.mult))
        p.wv(t3, t4)
        t5 = p.inc("eV", p.ve.tensor_tensor(
            p.stat[:, 4:5], p.stat[:, 2:3], p.stat[:, 3:4], ALU.subtract))
        p.wv(t5)
        if mode == "l1attn":
            tden = p.inc("eV", p.ve.tensor_scalar(
                p.stat[:, 5:6], p.stat[:, 4:5], 1.0 + 1e-6, 1e-11, ALU.mult, ALU.add))
        else:
            tden = p.inc("eV", p.ve.tensor_scalar(
                p.stat[:, 5:6], p.stat[:, 4:5], 1.0, 1e-5, ALU.mult, ALU.add))
        p.ws(tden)
        tsq = p.inc("eS", p.se.sqrt(p.statS[:, 2:3], p.stat[:, 5:6]))
        p.wv(tsq)
        trec = p.inc("eV", p.ve.reciprocal(p.acol[:, i:i + 1], p.statS[:, 2:3]))
        p.wv(trec)
        sj, stg = stg_ring.next()
        p.wv(stg_ring.free[sj])
        if mode == "l1attn":
            p.inc("eV", p.ve.tensor_scalar(
                p.scr32[:, 0:2048], st, p.mcol[:, i:i + 1], p.acol[:, i:i + 1],
                ALU.subtract, ALU.mult))
            txn = p.inc("eV", p.ve.tensor_tensor(
                stg, p.scr32[:, 0:2048], lnw_view, ALU.mult))
        else:
            txn = p.inc("eV", p.ve.tensor_scalar(
                stg, st, p.mcol[:, i:i + 1], p.acol[:, i:i + 1],
                ALU.subtract, ALU.mult))
        stg_ring.ready[sj] = txn
        for kc in range(KC):
            trseq.transpose(
                stg[:, kc * 128:(kc + 1) * 128],
                p.abuf[:, kc * 1024 + i * 128: kc * 1024 + (i + 1) * 128],
                extra_dep=(txn,))
        stg_ring.free[sj] = p.tok("eS")
    return p.tok("eS")


def emit_ffn0(p: PB, first_time=True):
    HT = p.bigbuf
    gate, up = p.inputs["l0_gate"], p.inputs["l0_up"]
    dd = p.ddown

    evac_done = None
    stream_evac = [None] * NT
    slot_last = [None, None]
    for half in range(2):
        gu_free = list(slot_last)
        psum_free = None
        for j in range(22):
            m = half * 22 + j
            k = j % 2
            gslot = p.wbuf[:, k * 4096:k * 4096 + 2048]
            uslot = p.wbuf[:, k * 4096 + 2048:k * 4096 + 4096]
            p.wp(gu_free[k])
            tg = p.inc("dg", p.pe.dma_start(
                out=gslot.rearrange("p (kc m) -> p kc m", m=128), in_=gate[m]), 16)
            tu = p.inc("dg", p.pe.dma_start(
                out=uslot.rearrange("p (kc m) -> p kc m", m=128), in_=up[m]), 16)
            p.wt(tg, tu, psum_free, evac_done)
            mm = None
            for ng in range(2):
                for kc in range(KC):
                    mm = p.te.matmul(
                        p.psA[:, ng * 512:(ng + 1) * 512],
                        gslot[:, kc * 128:(kc + 1) * 128],
                        p.abuf[:, kc * 1024 + ng * 512: kc * 1024 + (ng + 1) * 512],
                        start=(kc == 0), stop=(kc == KC - 1))
            tG = p.inc("eT", mm)
            for ng in range(2):
                for kc in range(KC):
                    mm = p.te.matmul(
                        p.psB[:, ng * 512:(ng + 1) * 512],
                        uslot[:, kc * 128:(kc + 1) * 128],
                        p.abuf[:, kc * 1024 + ng * 512: kc * 1024 + (ng + 1) * 512],
                        start=(kc == 0), stop=(kc == KC - 1))
            tU = p.inc("eT", mm)
            gu_free[k] = tU
            gstage = p.ptbuf[:, 0:1024]
            p.ws(tG, psum_free)
            tsg = p.inc("eS", p.se.activation(
                gstage, p.psA[:, 0:1024], AF.Silu, bias=0.0,
                scale=p.gsb[:, m:m + 1]))
            p.wv(tU, tsg)
            th = p.inc("eV", p.ve.scalar_tensor_tensor(
                HT[:, j * 1024:(j + 1) * 1024], p.psB[:, 0:1024],
                p.usb[:, m:m + 1], gstage, ALU.mult, ALU.mult))
            psum_free = th
        h_done = p.tok("eV")
        dn_free = [p.tok("eT"), p.tok("eT")]
        tmm = None
        for tt in range(NT):
            for j in range(22):
                k = j % 2
                dslot = p.wbuf[:, k * 4096:k * 4096 + 2048]
                p.wy(dn_free[k])
                tload = p.inc("ds", p.sy.dma_start(out=dslot, in_=dd[half * 22 + j]), 16)
                p.wt(tload, h_done)
                if j == 0:
                    p.wt(evac_done)
                mm = None
                for ng in range(4):
                    pst = p.psA if ng < 2 else p.psB
                    mm = p.te.matmul(
                        pst[:, (ng % 2) * 512:(ng % 2 + 1) * 512],
                        HT[:, j * 1024 + tt * 128: j * 1024 + (tt + 1) * 128],
                        dslot[:, ng * 512:(ng + 1) * 512],
                        start=(j == 0), stop=(j == 21))
                tmm = p.inc("eT", mm)
                dn_free[k] = tmm
                slot_last[k] = tmm
            p.wv(tmm)
            if half == 1:
                p.wv(stream_evac[tt])
            for ngg in range(2):
                pst = p.psA if ngg == 0 else p.psB
                ssl = slice(tt * D + ngg * 1024, tt * D + (ngg + 1) * 1024)
                p.inc("eV", p.ve.tensor_tensor(
                    p.stream[:, ssl], p.stream[:, ssl], pst[:, 0:1024], ALU.add))
            evac_done = p.tok("eV")
            stream_evac[tt] = evac_done
        p.wt(p.tok("eT"))
    return p.tok("eV")


def emit_down_precast(p: PB):
    down = p.inputs["l0_down"]
    dd = p.ddown
    tds = p.inc("ds", p.sy.dma_start(out=p.scr32[:, 0:2048], in_=p.inputs["l0_ds"][:, :]), 16)
    slot_free = [None, None]
    for m in range(IC):
        k = m % 2
        slot = p.wbuf[:, k * 4096:k * 4096 + 2048]
        slot2 = p.wbuf[:, k * 4096 + 2048:k * 4096 + 4096]
        p.wp(slot_free[k])
        tl = p.inc("dg", p.pe.dma_start(out=slot, in_=down[m]), 16)
        p.wv(tl, tds)
        tsc = p.inc("eV", p.ve.tensor_tensor(slot2, slot, p.scr32[:, 0:2048], ALU.mult))
        p.wy(tsc)
        slot_free[k] = p.inc("ds", p.sy.dma_start(out=dd[m], in_=slot2), 16)
    return p.tok("ds")



import math

QT_OFF = 0
KT_OFF = 16384
VR_OFF = 20480


def attn_params(l):
    if l == 0:
        return dict(G=G0, HD=HD0, H=H0, scale=1.0 / 8.0,
                    Wv=8 * 65, vstride=65, ones_col=True,
                    wq="l0_wq", wk="l0_wk", wv="l0_wv", wo="l0_wo",
                    bq="l0_bq", bk="l0_bk", bv="l0_bv", bo="l0_bo")
    return dict(G=G1, HD=HD1, H=H1, scale=1.0 / math.sqrt(128.0),
                Wv=4 * 128 + 8, vstride=128, ones_col=False,
                wq="l1_wq", wk="l1_wk", wv="l1_wv", wo="l1_wo",
                bq="l1_bq", bk="l1_bk", bv="l1_bv", bo=None)


def emit_qkv(p: PB, l):
    pr = attn_params(l)
    G, HD, H = pr["G"], pr["HD"], pr["H"]
    hw = HD // 2
    cosb = p.cos0b if l == 0 else p.cos1b
    sinb = p.sin0b if l == 0 else p.sin1b
    tb1 = p.inc("dg", p.pe.dma_start(out=p.birow[0:1, 0:2048], in_=p.inputs[pr["bq"]][:, :]), 16)
    tb2 = p.inc("dg", p.pe.dma_start(out=p.birow[0:1, 2048:2560], in_=p.inputs[pr["bk"]][:, :]), 16)
    tb3 = p.inc("dg", p.pe.dma_start(out=p.birow[0:1, 2560:3072], in_=p.inputs[pr["bv"]][:, :]), 16)
    tones = None
    if pr["ones_col"]:
        tones = p.inc("eV", p.ve.memset(
            p.bigbuf[:, VR_OFF:VR_OFF + 8 * pr["Wv"]]
            .rearrange("p (t g dd) -> p t g dd", g=G, dd=65)[:, :, :, 64:65], 1.0))
    else:
        tones = p.inc("eV", p.ve.memset(
            p.bigbuf[:, VR_OFF:VR_OFF + 8 * pr["Wv"]]
            .rearrange("p (t w) -> p t w", w=pr["Wv"])[:, :, 512:520], 1.0))
    tqn = p.inc("eV", p.ve.memset(p.qn[:, 0:H], 0.0))
    p.wv(tqn)

    trseq = TrSeq(p)
    pslots = [p.psA[:, 0:256], p.psB[:, 0:256], p.psC[:, 0:256]]
    slot_read = [None, None, None]
    sidx = 0
    stg_slots = [p.ptbuf[:, 2048 + 256 * k:2048 + 256 * (k + 1)] for k in range(2)]
    stg_free = [None, None]
    wslots = [p.wbuf[:, 0:4096], p.wbuf[:, 4096:8192]]
    w_free = [None, None]
    p_scnt = 0
    projs = [("q", pr["wq"], 8, 0), ("k", pr["wk"], 2, 2048), ("v", pr["wv"], 2, 2560)]
    for proj, wname, nslab, boff in projs:
        W = p.inputs[wname]
        for s in range(nslab):
            wk_ = p_scnt % 2
            p_scnt += 1
            wslot = wslots[wk_]
            p.wp(w_free[wk_])
            twl = p.inc("dg", p.pe.dma_start(
                out=wslot.rearrange("p (kc n) -> p kc n", n=256), in_=W[s]), 16)
            for ti in range(NT):
                j = sidx % 3
                ps = pslots[j]
                sidx += 1
                p.wt(slot_read[j], twl, tb1, tb2, tb3)
                for kc in range(KC):
                    p.te.matmul(ps, p.abuf[:, kc * 1024 + ti * 128:kc * 1024 + (ti + 1) * 128],
                                wslot[:, kc * 256:(kc + 1) * 256],
                                start=(kc == 0), stop=False)
                tmm = p.inc("eT", p.te.matmul(
                    ps, p.onesc[0:1, 0:128], p.birow[0:1, boff + s * 256:boff + s * 256 + 256],
                    start=False, stop=True))
                w_free[wk_] = tmm
                if proj == "v":
                    p.wv(tmm, tones)
                    if pr["ones_col"]:
                        dst = (p.bigbuf[:, VR_OFF + ti * pr["Wv"]:VR_OFF + (ti + 1) * pr["Wv"]]
                               .rearrange("p (g dd) -> p g dd", dd=65)[:, 4 * s:4 * s + 4, 0:64])
                        srcv = ps.rearrange("p (g dd) -> p g dd", dd=64)
                    else:
                        dst = p.bigbuf[:, VR_OFF + ti * pr["Wv"] + s * 256:VR_OFF + ti * pr["Wv"] + s * 256 + 256]
                        srcv = ps
                    tev = p.inc("eV", p.ve.tensor_copy(dst, srcv))
                    slot_read[j] = tev
                else:
                    nh = 256 // HD
                    C3 = cosb[:, ti * hw:(ti + 1) * hw].unsqueeze(1).broadcast_to([128, nh, hw])
                    S3 = sinb[:, ti * hw:(ti + 1) * hw].unsqueeze(1).broadcast_to([128, nh, hw])
                    ps3 = ps.rearrange("p (h w) -> p h w", w=HD)
                    psev, psod = ps3[:, :, 0:hw], ps3[:, :, hw:HD]
                    t1 = p.scr32[:, 2048:2176].rearrange("p (h w) -> p h w", w=hw)[:, 0:nh]
                    t2 = p.scr32[:, 2176:2304].rearrange("p (h w) -> p h w", w=hw)[:, 0:nh]
                    sk = sidx % 2
                    stg = stg_slots[sk]
                    stg3 = stg.rearrange("p (h w) -> p h w", w=HD)
                    p.wv(tmm, stg_free[sk])
                    p.inc("eV", p.ve.tensor_tensor(t1, psev, C3, ALU.mult))
                    i2 = p.inc("eV", p.ve.tensor_tensor(t2, psod, S3, ALU.mult))
                    p.wv(i2)
                    i3 = p.inc("eV", p.ve.tensor_tensor(stg3[:, :, 0:hw], t1, t2, ALU.subtract))
                    p.inc("eV", p.ve.tensor_tensor(t1, psod, C3, ALU.mult))
                    i5 = p.inc("eV", p.ve.tensor_tensor(t2, psev, S3, ALU.mult))
                    p.wv(i5)
                    i6 = p.inc("eV", p.ve.tensor_tensor(stg3[:, :, hw:HD], t1, t2, ALU.add))
                    slot_read[j] = i5
                    p.ws(i3, i6)
                    sq = p.inc("eS", p.se.activation(p.trash[:, 0:256], stg, AF.Square,
                                                     bias=0.0, scale=1.0))
                    p.wv(sq)
                    if proj == "q":
                        tr_ = p.inc("eV", p.ve.tensor_reduce(
                            p.stat[:, 0:nh], p.trash[:, 0:256].rearrange("p (h w) -> p h w", w=HD),
                            AX.X, ALU.add))
                        p.wv(tr_)
                        hb = s * nh
                        tmx = p.inc("eV", p.ve.tensor_tensor(
                            p.qn[:, hb:hb + nh], p.qn[:, hb:hb + nh], p.stat[:, 0:nh], ALU.max))
                        p.wv(tmx)
                    else:
                        gb = s * nh
                        tr_ = p.inc("eV", p.ve.tensor_reduce(
                            p.l1tmp[:, ti * 8 + gb:ti * 8 + gb + nh],
                            p.trash[:, 0:256].rearrange("p (h w) -> p h w", w=HD),
                            AX.X, ALU.add))
                        p.wv(tr_)
                    for jj in range(2):
                        kc_out = s * 2 + jj
                        base = QT_OFF if proj == "q" else KT_OFF
                        trseq.transpose(
                            stg[:, jj * 128:(jj + 1) * 128],
                            p.bigbuf[:, base + kc_out * 1024 + ti * 128:base + kc_out * 1024 + (ti + 1) * 128],
                            extra_dep=(i3, i6))
                    stg_free[sk] = p.tok("eS")
    return trseq


def emit_gather(p: PB, l, trseq):
    pr = attn_params(l)
    G, HD, H = pr["G"], pr["HD"], pr["H"]
    rep = H // G
    cc_in = p.cc_in[l]
    cc_out = p.cc_out[l]
    Wv0 = pr["Wv"]
    tz = p.inc("eV", p.ve.memset(p.trash[:, 0:1024], 0.0))
    p.wy(tz)
    for r0 in range(520, 1544, 128):
        r1 = min(r0 + 128, 1544)
        p.inc("ds", p.sy.dma_start(out=cc_in[r0:r1, Wv0:1040], in_=p.trash[0:r1 - r0, 0:1040 - Wv0]), 16)
    for r0 in range(0, 520, 128):
        r1 = min(r0 + 128, 520)
        p.inc("ds", p.sy.dma_start(out=cc_in[r0:r1, 1024:1040], in_=p.trash[0:r1 - r0, 0:16]), 16)
    for r0 in range(12352, 13120, 128):
        p.inc("ds", p.sy.dma_start(out=cc_out[r0:r0 + 128, 0:1024], in_=p.trash[:, 0:1024]), 16)
        p.inc("ds", p.sy.dma_start(out=cc_out[r0:r0 + 128, 1024:1040], in_=p.trash[:, 0:16]), 16)
    p.wv(p.tok("ds"))
    for ti in range(NT):
        tc_ = p.inc("eV", p.ve.tensor_copy(p.trash[:, 256:256 + 8], p.l1tmp[:, ti * 8:ti * 8 + 8]))
        p.wv(tc_)
        trseq.transpose(p.trash[:, 256:264], p.knrow[0:8, ti * 128:(ti + 1) * 128],
                        extra_dep=(tc_,), rows=8)
    knt = p.tok("eS")
    for kc in range(4):
        p.wy(p.tok("eS"))
        p.inc("ds", p.sy.dma_start(
            out=cc_in[kc * 128:(kc + 1) * 128, 0:1024],
            in_=p.bigbuf[:, KT_OFF + kc * 1024:KT_OFF + (kc + 1) * 1024]), 16)
    p.wy(knt)
    p.inc("ds", p.sy.dma_start(out=cc_in[512:520, 0:1024], in_=p.knrow[0:8, :]), 16)
    Wv = pr["Wv"]
    for ti in range(NT):
        p.inc("ds", p.sy.dma_start(
            out=cc_in[520 + 128 * ti:520 + 128 * (ti + 1), 0:Wv],
            in_=p.bigbuf[:, VR_OFF + ti * Wv:VR_OFF + (ti + 1) * Wv]), 16)
    p.wp(p.tok("ds"), p.tok("eV"), p.tok("eS"))
    tcc = p.inc("cc", p.pe.collective_compute(
        "AllGather", ALU.bypass, replica_groups=[list(range(8))],
        ins=[cc_in[:, :].opt()], outs=[cc_out[0:12352, :].opt()]))
    p.cc_done = tcc

    tq0 = p.inc("eV", p.ve.tensor_copy(p.trash[:, 256:256 + H], p.qn[:, 0:H]))
    p.wv(tq0)
    tq1 = trseq.transpose(p.trash[:, 256:256 + H], p.ptbuf[0:H, 0:128], rows=H, extra_dep=(tq0,))
    p.wv(tq1)
    tq2 = p.inc("eV", p.ve.tensor_reduce(p.stat[0:H, 0:1], p.ptbuf[0:H, 0:128], AX.X, ALU.max))
    p.wv(tq2)
    tq3 = p.inc("eV", p.ve.tensor_copy(p.trash[0:H, 256:257], p.stat[0:H, 0:1]))
    p.wv(tq3)
    tq4 = trseq.transpose(p.trash[0:H, 256:257], p.smallf[0:1, 0:H], rows=1, extra_dep=(tq3,))
    p.wp(tcc)
    p.wy(tcc)
    tkm = None
    for r in range(NCORE):
        p.wy(tkm)
        tld = p.inc("ds", p.sy.dma_start(
            out=p.knrow[0:8, :], in_=cc_out[r * 1544 + 512:r * 1544 + 520, 0:1024]), 16)
        p.wv(tld)
        tr_ = p.inc("eV", p.ve.tensor_reduce(p.stat[0:8, 1:2], p.knrow[0:8, :], AX.X, ALU.max))
        p.wv(tr_)
        if r == 0:
            tkm = p.inc("eV", p.ve.tensor_copy(p.stat[0:8, 2:3], p.stat[0:8, 1:2]))
        else:
            tkm = p.inc("eV", p.ve.tensor_tensor(
                p.stat[0:8, 2:3], p.stat[0:8, 2:3], p.stat[0:8, 1:2], ALU.max))
        p.wv(tkm)
    tk3 = p.inc("eV", p.ve.tensor_copy(p.trash[0:8, 256:257], p.stat[0:8, 2:3]))
    p.wv(tk3)
    tk4 = trseq.transpose(p.trash[0:8, 256:257], p.smallf[0:1, 32:40], rows=1, extra_dep=(tk3,))
    p.wv(tq4, tk4)
    kr = p.smallf[0:1, 32:32 + G].unsqueeze(2).broadcast_to([1, G, rep])
    tn1 = p.inc("eV", p.ve.tensor_tensor(
        p.smallf[0:1, 64:64 + H].rearrange("o (g r) -> o g r", r=rep),
        p.smallf[0:1, 0:H].rearrange("o (g r) -> o g r", r=rep), kr, ALU.add))
    p.wv(tn1)
    tn2 = p.inc("eV", p.ve.tensor_scalar_mul(
        p.smallf[0:1, 96:96 + H], p.smallf[0:1, 64:64 + H], -1.02 * pr["scale"] * 0.5))
    p.wv(tn2)
    tn3 = p.inc("eV", p.ve.tensor_copy(p.trash[0:1, 256:256 + H], p.smallf[0:1, 96:96 + H]))
    p.wt(tn3)
    tn4 = p.inc("eT", p.te.matmul(p.psA[:, 0:H], p.onesc[0:1, 0:128],
                                  p.trash[0:1, 256:256 + H], start=True, stop=True))
    p.wv(tn4)
    tn5 = p.inc("eV", p.ve.tensor_copy(p.negb[:, 0:H], p.psA[:, 0:H]))
    p.wv(tn5)
    return tn5


def emit_attn_loop(p: PB, l):
    pr = attn_params(l)
    G, HD, H = pr["G"], pr["HD"], pr["H"]
    rep = H // G
    cc_out = p.cc_out[l]
    Wv = pr["Wv"]
    kslots = [p.kvbuf[:, 0:512], p.kvbuf[:, 512:1024]]
    vslot_w = 544
    vslots = [p.kvbuf[:, 1024 + k * vslot_w:1024 + k * vslot_w + Wv] for k in range(2)]
    ptslots = [p.ptbuf[:, 512 * k:512 * (k + 1)] for k in range(4)]
    k_free = [None, None]
    v_free = [None, None]
    pt_free = [None, None, None, None]
    sA_read, sB_read = None, None
    o_read = [None, None]
    nbatch = 1 if l == 0 else 2
    hpb = 4 if l == 0 else 2
    kcnt, vcnt, ptcnt = 0, 0, 0
    ev_scr_read = None
    if not hasattr(p, "cc_flat"):
        p.cc_flat = {}
    p.cc_flat[l] = cc_out.rearrange("r (c e) -> (r c) e", e=8)
    pend = [None]

    def _emit_av(d):
        p.wt(d["texp"], d["tmask"], d["tvl"])
        if d["start"]:
            p.wt(o_read[0], o_read[1])
        tav = p.inc("eT", p.te.matmul(
            d["ovs"], d["vls"], d["pt"], start=d["start"], stop=d["stop"]))
        pt_free[d["ptk"]] = tav
        if l == 1:
            tav = p.inc("eT", p.te.matmul(
                d["lvs"], d["ind"], d["pt"], start=d["start"], stop=d["stop"]))
            pt_free[d["ptk"]] = tav
        k_free[d["kk"]] = p.tok("eT")
        v_free[d["vv"]] = p.tok("eT")

    for qhalf in range(2):
        NIT = 8 if qhalf == 0 else 16
        qsl = slice(qhalf * 512, qhalf * 512 + 512)
        p.wy(p.tok("dg"))
        p.kvib_ready = p.inc("ds", p.sy.dma_start(
            out=p.kvib[:, :], in_=p.inputs["kvi"][l, qhalf]), 16)
        p.wp(p.kvib_ready)
        for g in range(G):
            for hb in range(nbatch):
                heads = [g * rep + hb * hpb + i for i in range(hpb)]
                if l == 0:
                    ovs = [p.psC[0:65, 0:512], p.psC[0:65, 512:1024],
                           p.psD[0:65, 0:512], p.psD[0:65, 512:1024]]
                    lvs = [None] * 4
                else:
                    ovs = [p.psC[:, 0:512], p.psC[:, 512:1024]]
                    lvs = [p.psD[0:1, 0:512], p.psD[0:1, 512:1024]]
                for it in range(NIT):
                    kk = kcnt % 2
                    kcnt += 1
                    p.wp(k_free[kk], p.kvib_ready)
                    tkl = p.inc("dg", p.pe.indirect_dma_start(
                        kslots[kk][:, :], None, p.cc_flat[l],
                        bass.IndirectOffsetOnAxis(p.kvib[:, it * G + g:it * G + g + 1], 0)), 16)
                    for j in range(4):
                        vv = vcnt % 2
                        vcnt += 1
                        p.wp(v_free[vv])
                        tvl = p.inc("dg", p.pe.indirect_dma_start(
                            vslots[vv], None, p.cc_flat[l],
                            bass.IndirectOffsetOnAxis(
                                p.kvib[:, 16 * G + it * 4 + j:16 * G + it * 4 + j + 1], 0)), 16)
                        for hi, h in enumerate(heads):
                            use_a = (ptcnt % 2 == 0)
                            sps = p.psA[:, 0:512] if use_a else p.psB[:, 0:512]
                            p.wt(tkl, sA_read if use_a else sB_read)
                            qrow = (h % 2) * 64 if l == 0 else 0
                            qchunk = h // 2 if l == 0 else h
                            tsm = p.inc("eT", p.te.matmul(
                                sps, kslots[kk][qrow:qrow + HD, j * 128:(j + 1) * 128],
                                p.bigbuf[qrow:qrow + HD,
                                         QT_OFF + qchunk * 1024 + qhalf * 512:
                                         QT_OFF + qchunk * 1024 + qhalf * 512 + 512],
                                start=True, stop=True))
                            pt = ptslots[ptcnt % 4]
                            ptk = ptcnt % 4
                            ptcnt += 1
                            p.ws(tsm, pt_free[ptk])
                            texp = p.inc("eS", p.se.activation(
                                pt, sps, AF.Exp, bias=p.negb[:, h:h + 1], scale=pr["scale"]))
                            if use_a:
                                sA_read = texp
                            else:
                                sB_read = texp
                            tmask = None
                            if it == 0:
                                p.wv(texp)
                                tmask = p.inc("eV", p.ve.tensor_tensor(
                                    pt, pt, p.maskb[:, j * 512:(j + 1) * 512], ALU.mult))
                            if pend[0] is not None:
                                _emit_av(pend[0])
                            pend[0] = dict(
                                ovs=ovs[hi], lvs=lvs[hi],
                                vls=vslots[vv][:, g * pr["vstride"]:g * pr["vstride"] + (65 if l == 0 else 128)],
                                ind=vslots[vv][:, 512:513],
                                pt=pt, ptk=ptk, texp=texp, tmask=tmask, tvl=tvl,
                                start=(it == 0 and j == 0),
                                stop=(it == NIT - 1 and j == 3),
                                kk=kk, vv=vv)
                if pend[0] is not None:
                    _emit_av(pend[0])
                    pend[0] = None
                tavl = p.tok("eT")
                for hi, h in enumerate(heads):
                    if l == 0:
                        lrow = ovs[hi][64:65, :]
                        orows = ovs[hi][0:64, :]
                        dst = p.abuf[(h % 2) * 64:(h % 2) * 64 + 64,
                                     (h // 2) * 1024 + qhalf * 512:(h // 2) * 1024 + qhalf * 512 + 512]
                        drows = 64
                    else:
                        lrow = lvs[hi]
                        orows = ovs[hi]
                        dst = p.abuf[:, h * 1024 + qhalf * 512:h * 1024 + qhalf * 512 + 512]
                        drows = 128
                    p.wv(tavl)
                    tr1 = p.inc("eV", p.ve.reciprocal(p.linv[0:1, 0:512], lrow))
                    p.wt(tr1, sA_read, sB_read)
                    tb = p.inc("eT", p.te.matmul(
                        p.psA[0:drows, 0:512], p.onesf[0:1, 0:drows],
                        p.linv[0:1, 0:512], start=True, stop=True))
                    p.ws(tb, ev_scr_read)
                    tcp = p.inc("eS", p.se.copy(p.scr32[0:drows, 0:512], p.psA[0:drows, 0:512]))
                    sA_read = tcp
                    p.wv(tcp)
                    tev = p.inc("eV", p.ve.tensor_tensor(dst, orows, p.scr32[0:drows, 0:512], ALU.mult))
                    ev_scr_read = tev
                    if l == 0:
                        o_read[hi // 2] = tev
                    else:
                        o_read[0] = tev
                        o_read[1] = tev
    return p.tok("eV")


def emit_outproj(p: PB, l):
    pr = attn_params(l)
    W = p.inputs[pr["wo"]]
    has_bias = pr["bo"] is not None
    if has_bias:
        tbo = p.inc("dg", p.pe.dma_start(out=p.birow[0:1, 0:2048], in_=p.inputs[pr["bo"]][:, :]), 16)
    else:
        tbo = None
    wslots = [p.wbuf[:, 0:4096], p.wbuf[:, 4096:8192]]
    w_free = [None, None]
    pslots = [p.psA[:, 0:256], p.psB[:, 0:256], p.psC[:, 0:256]]
    slot_read = [None, None, None]
    sidx = 0
    for s in range(8):
        wk_ = s % 2
        wslot = wslots[wk_]
        p.wp(w_free[wk_])
        twl = p.inc("dg", p.pe.dma_start(
            out=wslot.rearrange("p (kc n) -> p kc n", n=256), in_=W[s]), 16)
        for ti in range(NT):
            j = sidx % 3
            ps = pslots[j]
            sidx += 1
            p.wt(slot_read[j], twl, tbo)
            mm = None
            for kc in range(KC):
                mm = p.te.matmul(ps, p.abuf[:, kc * 1024 + ti * 128:kc * 1024 + (ti + 1) * 128],
                                 wslot[:, kc * 256:(kc + 1) * 256],
                                 start=(kc == 0), stop=(not has_bias and kc == KC - 1))
            if has_bias:
                mm = p.te.matmul(
                    ps, p.onesc[0:1, 0:128], p.birow[0:1, s * 256:s * 256 + 256],
                    start=False, stop=True)
            tmm = p.inc("eT", mm)
            w_free[wk_] = tmm
            p.wv(tmm)
            ssl = slice(ti * D + s * 256, ti * D + s * 256 + 256)
            tev = p.inc("eV", p.ve.tensor_tensor(
                p.stream[:, ssl], p.stream[:, ssl], ps, ALU.add))
            slot_read[j] = tev
    return p.tok("eV")



F1_SW, F1_LU, F1_MD, F1_DW, F1_LD, F1_BW = 0, 1024, 2048, 2176, 2304, 4352


def emit_ffn1(p: PB):
    bb = p.bigbuf
    tl = []
    for nm, off, rows, width in (("sw_t", F1_SW, 128, 1024), ("lu_t", F1_LU, 128, 1024),
                                 ("md_t", F1_MD, 128, 128), ("dw_t", F1_DW, 128, 128),
                                 ("ld_cat", F1_LD, 64, 2048), ("bw", F1_BW, 8, 2048)):
        src = p.inputs[nm]
        ap = src if src.ndim == 2 else src.rearrange("p a b -> p (a b)")
        tl.append(p.inc("dg", p.pe.dma_start(out=bb[0:rows, off:off + width], in_=ap), 16))
    ti1 = p.inc("eP", p.pe.iota(p.trash[0:8, 512:576], [[1, 8], [0, 8]],
                                base=0, channel_multiplier=-1,
                                allow_small_or_imprecise_dtypes=True))
    p.wv(ti1)
    te1 = p.inc("eV", p.ve.tensor_single_scalar(
        p.trash[0:8, 512:576], p.trash[0:8, 512:576], 0.0, ALU.is_equal))
    p.wt(*tl)
    mm = None
    for n in range(8):
        for kc in range(KC):
            mm = p.te.matmul(p.psA[0:1, 8 * n:8 * n + 8],
                             bb[:, F1_MD + kc * 8 + n:F1_MD + kc * 8 + n + 1],
                             bb[:, F1_SW + kc * 64 + 8 * n:F1_SW + kc * 64 + 8 * n + 8],
                             start=(kc == 0), stop=(kc == KC - 1))
    t_wmd = p.inc("eT", mm)
    for n in range(8):
        for kc in range(KC):
            mm = p.te.matmul(p.psB[0:1, n:n + 1],
                             bb[:, F1_MD + kc * 8 + n:F1_MD + kc * 8 + n + 1],
                             bb[:, F1_MD + kc * 8 + n:F1_MD + kc * 8 + n + 1],
                             start=(kc == 0), stop=(kc == KC - 1))
    t_mdsq = p.inc("eT", mm)
    for n in range(8):
        for kc in range(KC):
            mm = p.te.matmul(p.psC[0:1, 8 * n:8 * n + 8],
                             bb[:, F1_DW + kc * 8 + n:F1_DW + kc * 8 + n + 1],
                             bb[:, F1_LU + kc * 64 + 8 * n:F1_LU + kc * 64 + 8 * n + 8],
                             start=(kc == 0), stop=(kc == KC - 1))
    t_cc = p.inc("eT", mm)
    p.wv(t_wmd, t_mdsq, t_cc)
    tw1 = p.inc("eV", p.ve.tensor_copy(p.smallf[0:1, 0:64], p.psA[0:1, 0:64]))
    tw2 = p.inc("eV", p.ve.tensor_copy(p.smallf[0:1, 64:72], p.psB[0:1, 0:8]))
    tw3 = p.inc("eV", p.ve.tensor_copy(p.trash[0:1, 576:640], p.psC[0:1, 0:64]))
    trseq = TrSeq(p)
    tc4 = trseq.transpose(p.trash[0:1, 576:640], p.ccol[0:64, 0:1], extra_dep=(tw3,))
    p.wv(tw1, tw2)
    tb1 = p.inc("eV", p.ve.tensor_copy(p.trash[0:1, 640:704], p.smallf[0:1, 0:64]))
    tb2 = p.inc("eV", p.ve.tensor_copy(p.trash[0:1, 704:712], p.smallf[0:1, 64:72]))
    p.wt(tb1, tb2, tw1, tw2)
    tm1 = p.inc("eT", p.te.matmul(p.psD[:, 0:64], p.onesc[0:1, 0:128],
                                  p.trash[0:1, 640:704], start=True, stop=True))
    tm2 = p.inc("eT", p.te.matmul(p.psD[:, 64:72], p.onesc[0:1, 0:128],
                                  p.trash[0:1, 704:712], start=True, stop=True))
    p.wv(tm1, tm2)
    tb3 = p.inc("eV", p.ve.tensor_copy(p.l1tmp[:, 0:64], p.psD[:, 0:64]))
    tb4 = p.inc("eV", p.ve.tensor_copy(p.negb[:, 0:8], p.psD[:, 64:72]))
    p.wv(tb3, tb4)

    Q0, VSQ, SCR, GAT, CMX, OH, MCOL = 0, 64, 72, 80, 88, 96, 104
    psAB_free = p.tok("eV")
    psD_free = p.tok("eV")
    psC_free = p.tok("eV")
    ohT_t = None
    st_upd = None
    for ti in range(NT):
        xsl = slice(ti * 128, (ti + 1) * 128)
        p.wt(psAB_free, st_upd)
        mm = None
        for kc in range(KC):
            lhs = p.abuf[:, kc * 1024 + ti * 128:kc * 1024 + (ti + 1) * 128]
            mm = p.te.matmul(p.psA[:, 0:64], lhs,
                             bb[:, F1_SW + kc * 64:F1_SW + (kc + 1) * 64],
                             start=(kc == 0), stop=(kc == KC - 1))
        tA = p.inc("eT", mm)
        for kc in range(KC):
            lhs = p.abuf[:, kc * 1024 + ti * 128:kc * 1024 + (ti + 1) * 128]
            mm = p.te.matmul(p.psA[:, 64:72], lhs,
                             bb[:, F1_MD + kc * 8:F1_MD + (kc + 1) * 8],
                             start=(kc == 0), stop=(kc == KC - 1))
        tB = p.inc("eT", mm)
        for kc in range(KC):
            mm = p.te.matmul(p.psB[0:64, 0:128],
                             bb[:, F1_LU + kc * 64:F1_LU + (kc + 1) * 64],
                             p.abuf[:, kc * 1024 + ti * 128:kc * 1024 + (ti + 1) * 128],
                             start=(kc == 0), stop=(kc == KC - 1))
        tC = p.inc("eT", mm)
        p.wv(tA, tB)
        t1 = p.inc("eV", p.ve.tensor_tensor(
            p.scr32[:, MCOL + 1:MCOL + 2], p.acol[:, ti:ti + 1], p.acol[:, ti:ti + 1], ALU.mult))
        p.wv(t1)
        t2 = p.inc("eV", p.ve.tensor_scalar(
            p.scr32[:, MCOL:MCOL + 1], p.scr32[:, MCOL + 1:MCOL + 2],
            -float(D) * 1e-5, float(D), ALU.mult, ALU.add))
        tq = p.inc("eV", p.ve.tensor_tensor(
            p.scr32[:, Q0:Q0 + 64], p.psA[:, 0:64], p.l1tmp[:, 0:64], ALU.subtract))
        p.ws(tq)
        tsq = p.inc("eS", p.se.activation(p.trash[:, 0:64], p.scr32[:, Q0:Q0 + 64],
                                          AF.Square, bias=0.0, scale=1.0))
        p.wv(tsq, t2)
        tr_ = p.inc("eV", p.ve.tensor_reduce(
            p.scr32[:, SCR:SCR + 8],
            p.trash[:, 0:64].rearrange("p (n s) -> p n s", s=8), AX.X, ALU.add))
        tv1 = p.inc("eV", p.ve.tensor_scalar(
            p.scr32[:, VSQ:VSQ + 8], p.psA[:, 64:72], -2.0,
            p.scr32[:, MCOL:MCOL + 1], ALU.mult, ALU.add))
        p.wv(tv1)
        tv2 = p.inc("eV", p.ve.tensor_tensor(
            p.scr32[:, VSQ:VSQ + 8], p.scr32[:, VSQ:VSQ + 8], p.negb[:, 0:8], ALU.add))
        psAB_free = tv2
        p.wv(tv2)
        tv3 = p.inc("eV", p.ve.reciprocal(p.scr32[:, OH:OH + 8], p.scr32[:, VSQ:VSQ + 8]))
        p.wv(tv3, tr_)
        tsc = p.inc("eV", p.ve.tensor_tensor(
            p.scr32[:, SCR:SCR + 8], p.scr32[:, SCR:SCR + 8], p.scr32[:, OH:OH + 8], ALU.mult))
        p.wv(tsc)
        tg1 = p.inc("eV", p.ve.tensor_tensor(
            p.scr32[:, GAT:GAT + 8], p.scr32[:, SCR:SCR + 8], p.cenb[:, :], ALU.subtract))
        p.wv(tg1)
        tg2 = p.inc("eV", p.ve.tensor_tensor(
            p.scr32[:, GAT:GAT + 8], p.scr32[:, GAT:GAT + 8], p.slob[:, :], ALU.mult))
        p.ws(tg2)
        tg3 = p.inc("eS", p.se.activation(p.scr32[:, GAT:GAT + 8], p.scr32[:, GAT:GAT + 8],
                                          AF.Sigmoid, bias=0.0, scale=1.0))
        p.wv(tg3)
        tmx = p.inc("eV", p.ve.tensor_reduce(
            p.scr32[:, MCOL + 2:MCOL + 3], p.scr32[:, GAT:GAT + 8], AX.X, ALU.max))
        tcm = p.inc("eV", p.ve.memset(p.scr32[:, CMX:CMX + 1], -1.0))
        for n in range(1, 8):
            p.wv(tcm)
            tcm = p.inc("eV", p.ve.tensor_tensor(
                p.scr32[:, CMX + n:CMX + n + 1], p.scr32[:, CMX + n - 1:CMX + n],
                p.scr32[:, GAT + n - 1:GAT + n], ALU.max))
        p.wv(tcm, tmx)
        to1 = p.inc("eV", p.ve.tensor_scalar(
            p.scr32[:, OH:OH + 8], p.scr32[:, GAT:GAT + 8],
            p.scr32[:, MCOL + 2:MCOL + 3], None, ALU.is_ge))
        to2 = p.inc("eV", p.ve.tensor_scalar(
            p.scr32[:, SCR:SCR + 8], p.scr32[:, CMX:CMX + 8],
            p.scr32[:, MCOL + 2:MCOL + 3], None, ALU.is_lt))
        p.wv(to1, to2)
        to3 = p.inc("eV", p.ve.tensor_tensor(
            p.scr32[:, OH:OH + 8], p.scr32[:, OH:OH + 8], p.scr32[:, SCR:SCR + 8], ALU.mult))
        p.wv(to3)
        tob = p.inc("eV", p.ve.tensor_copy(p.trash[:, 64:72], p.scr32[:, OH:OH + 8]))
        tohT = trseq.transpose(p.trash[:, 64:72], p.knrow[0:8, ti * 128:(ti + 1) * 128],
                               extra_dep=(tob,), rows=8)
        p.wt(tohT, te1, psC_free)
        trep = p.inc("eT", p.te.matmul(
            p.psC[0:64, 0:128], p.trash[0:8, 512:576],
            p.knrow[0:8, ti * 128:(ti + 1) * 128], start=True, stop=True))
        p.wv(tC, tc4)
        tmod = p.inc("eV", p.ve.tensor_scalar(
            p.scr32[0:64, 128:256], p.psB[0:64, 0:128], p.ccol[0:64, 0:1], None, ALU.subtract))
        p.wv(trep, tmod)
        trc = p.inc("eV", p.ve.tensor_copy(p.trash[0:64, 128:256], p.psC[0:64, 0:128]))
        psC_free = trc
        p.wv(trc)
        tmm2 = p.inc("eV", p.ve.tensor_tensor(
            p.ptbuf[0:64, 0:128], p.scr32[0:64, 128:256], p.trash[0:64, 128:256], ALU.mult))
        p.wt(tmm2, psD_free, psAB_free)
        for ng in range(4):
            pst = p.psD if ng < 2 else p.psA
            psl = pst[:, (ng % 2) * 512:(ng % 2 + 1) * 512]
            p.te.matmul(psl, p.ptbuf[0:64, 0:128],
                        bb[0:64, F1_LD + ng * 512:F1_LD + (ng + 1) * 512],
                        start=True, stop=False)
            mm = p.te.matmul(psl, p.knrow[0:8, ti * 128:(ti + 1) * 128],
                             bb[0:8, F1_BW + ng * 512:F1_BW + (ng + 1) * 512],
                             start=False, stop=True)
        tdg = p.inc("eT", mm)
        tS1 = p.inc("eV", p.ve.tensor_scalar(
            p.scr32[:, MCOL + 3:MCOL + 4], p.scr32[:, MCOL + 2:MCOL + 3],
            -1.0, 1.0, ALU.mult, ALU.add))
        txq = p.inc("eV", p.ve.tensor_scalar(
            p.scr32[:, 512:2560], p.stream[:, ti * D:(ti + 1) * D],
            p.mcol[:, ti:ti + 1], p.acol[:, ti:ti + 1], ALU.subtract, ALU.mult))
        p.wv(tdg, tS1, txq, st_upd)
        t_u1 = p.inc("eV", p.ve.scalar_tensor_tensor(
            p.stream[:, ti * D:ti * D + 1024], p.psD[:, 0:1024],
            p.scr32[:, MCOL + 2:MCOL + 3], p.stream[:, ti * D:ti * D + 1024],
            ALU.mult, ALU.add))
        p.wv(t_u1)
        t_u2 = p.inc("eV", p.ve.scalar_tensor_tensor(
            p.stream[:, ti * D + 1024:(ti + 1) * D], p.psA[:, 0:1024],
            p.scr32[:, MCOL + 2:MCOL + 3], p.stream[:, ti * D + 1024:(ti + 1) * D],
            ALU.mult, ALU.add))
        p.wv(t_u2)
        t_u3 = p.inc("eV", p.ve.scalar_tensor_tensor(
            p.stream[:, ti * D:(ti + 1) * D], p.scr32[:, 512:2560],
            p.scr32[:, MCOL + 3:MCOL + 4], p.stream[:, ti * D:(ti + 1) * D],
            ALU.mult, ALU.add))
        st_upd = t_u3
        psD_free = t_u1
        psAB_free = t_u2
    return p.tok("eV")



INPUT_SPECS = {
    "xin": (TC, D), "cos0": (TC, 32), "sin0": (TC, 32),
    "cos1": (TC, 64), "sin1": (TC, 64),
    "ident": (128, 128), "masks": (128, 2048),
    "l0_wq": (8, 128, 16, 256), "l0_wk": (2, 128, 16, 256),
    "l0_wv": (2, 128, 16, 256), "l0_wo": (8, 128, 16, 256),
    "l0_bq": (1, 2048), "l0_bk": (1, 512), "l0_bv": (1, 512), "l0_bo": (1, 2048),
    "l0_gate": (44, 128, 16, 128), "l0_up": (44, 128, 16, 128),
    "l0_down": (44, 128, 2048),
    "l0_gs": (128, 44), "l0_us": (128, 44), "l0_ds": (128, 2048),
    "l1_wq": (8, 128, 16, 256), "l1_wk": (2, 128, 16, 256),
    "l1_wv": (2, 128, 16, 256), "l1_wo": (8, 128, 16, 256),
    "l1_bq": (1, 2048), "l1_bk": (1, 512), "l1_bv": (1, 512),
    "l1_lnw": (128, 2048),
    "sw_t": (128, 16, 64), "lu_t": (128, 16, 64), "ld_cat": (64, 2048),
    "md_t": (128, 16, 8), "dw_t": (128, 16, 8), "bw": (8, 2048),
    "cen_bc": (128, 8), "slo_bc": (128, 8),
}


def declare_all(p: PB):
    for k, shp in INPUT_SPECS.items():
        p.din(k, shp, F32)
    p.din("kvi", (2, 2, 128, 192), mybir.dt.int32)


def build_full():
    p = PB(0)
    declare_all(p)
    yout = p.dout("yout", [TC, D], F32)
    L_alloc = alloc_common(p)
    p.ddown = p.dint("ddown", [44, 128, 2048], BF16)
    p.cc_in = {0: p.dint("cc_in0", [1544, 1040], BF16),
               1: p.dint("cc_in1", [1544, 1040], BF16)}
    p.cc_out = {0: p.dint("cc_out0", [13120, 1040], BF16),
                1: p.dint("cc_out1", [13120, 1040], BF16)}
    emit_prologue(p)
    emit_down_precast(p)
    emit_stream_load(p, p.inputs["xin"])
    p.join()
    emit_ln(p, "plain"); p.join()
    trseq = emit_qkv(p, 0); p.join()
    emit_gather(p, 0, trseq); p.join()
    emit_attn_loop(p, 0); p.join()
    emit_outproj(p, 0); p.join()
    emit_ln(p, "plain"); p.join()
    emit_ffn0(p); p.join()
    p.inc("dg", p.pe.dma_start(out=p.bigbuf[:, 16384:18432], in_=p.inputs["l1_lnw"][:, :]), 16)
    p.join()
    emit_ln(p, "l1attn", lnw_view=p.bigbuf[:, 16384:18432]); p.join()
    trseq = emit_qkv(p, 1); p.join()
    emit_gather(p, 1, trseq); p.join()
    emit_attn_loop(p, 1); p.join()
    emit_outproj(p, 1); p.join()
    emit_ln(p, "plain"); p.join()
    emit_ffn1(p); p.join()
    tst = emit_stream_store(p, yout)
    p.wy(tst)
    return p


_PROG = None


def kernel(**inputs):
    global _PROG
    g = host_prep(inputs)
    x_full = np.asarray(inputs["L0_attn_x"])[0]
    maps = per_core_inputs(g, x_full)
    if _PROG is None:
        _PROG = build_full()
    p = _PROG
    fmaps = [{k: mm[k] for k in p.inputs} for mm in maps]
    res = run_bass_kernel_spmd(p.nc, fmaps, core_ids=list(range(NCORE)))
    y = np.zeros((T, D), np.float32)
    for c in range(NCORE):
        y[core_token_rows(c)] = res.results[c]["yout"]
    return y.reshape(1, T, D)


# version 04 (final)
# speedup vs baseline: 1.4876x; 1.0432x over previous
import numpy as np
import ml_dtypes
import concourse.bass as bass
import concourse.mybir as mybir
from concourse.bass_utils import run_bass_kernel_spmd
from contextlib import ExitStack

F32 = mybir.dt.float32
BF16 = mybir.dt.bfloat16
AF = mybir.ActivationFunctionType
ALU = mybir.AluOpType
AX = mybir.AxisListType

D = 2048
T = 8192
TC = 1024
NT = 8
KC = 16
INTER = 5632
IC = 44
NCORE = 8
H0, G0, HD0 = 32, 8, 64
H1, G1, HD1 = 16, 4, 128
BF = ml_dtypes.bfloat16


def core_tiles(c):
    return [c, 15 - c]


def core_token_rows(c):
    t0, t1 = core_tiles(c)
    return np.concatenate([np.arange(512 * t0, 512 * t0 + 512),
                           np.arange(512 * t1, 512 * t1 + 512)])


def rope_perm(width, hd):
    p = []
    for h in range(width // hd):
        base = h * hd
        p += [base + 2 * i for i in range(hd // 2)]
        p += [base + 2 * i + 1 for i in range(hd // 2)]
    return np.array(p)


def host_prep(inp):
    g = {}
    f32 = np.float32
    perm_q = rope_perm(2048, 64)
    perm_k = rope_perm(512, 64)
    Wq = np.ascontiguousarray(inp["L0_attn_Wq"][:, perm_q])
    Wk = np.ascontiguousarray(inp["L0_attn_Wk"][:, perm_k])
    def rhs_slabs(W, nslab, nw):
        return np.ascontiguousarray(
            W.reshape(16, 128, nslab, nw).transpose(2, 1, 0, 3))
    g["l0_wq"] = rhs_slabs(Wq, 8, 256)
    g["l0_wk"] = rhs_slabs(Wk, 2, 256)
    g["l0_wv"] = rhs_slabs(inp["L0_attn_Wv"], 2, 256)
    g["l0_wo"] = rhs_slabs(inp["L0_attn_Wo"], 8, 256)
    g["l0_bq"] = np.ascontiguousarray(inp["L0_attn_bq"][perm_q].reshape(1, 2048))
    g["l0_bk"] = np.ascontiguousarray(inp["L0_attn_bk"][perm_k].reshape(1, 512))
    g["l0_bv"] = np.ascontiguousarray(inp["L0_attn_bv"].reshape(1, 512))
    g["l0_bo"] = np.ascontiguousarray(inp["L0_attn_bo"].reshape(1, 2048))
    gw, uw, dw = inp["L0_ffn_gate_w"], inp["L0_ffn_up_w"], inp["L0_ffn_down_w"]
    g["l0_gate"] = np.ascontiguousarray(gw.reshape(44, 128, 16, 128).transpose(0, 3, 2, 1))
    g["l0_up"] = np.ascontiguousarray(uw.reshape(44, 128, 16, 128).transpose(0, 3, 2, 1))
    g["l0_down"] = np.ascontiguousarray(dw.T.reshape(44, 128, 2048))
    g["l0_gs"] = np.ascontiguousarray(inp["L0_ffn_gate_s"].reshape(44, 128).T)
    g["l0_us"] = np.ascontiguousarray(inp["L0_ffn_up_s"].reshape(44, 128).T)
    g["l0_ds"] = np.ascontiguousarray(np.broadcast_to(inp["L0_ffn_down_s"], (128, 2048)))
    g["l1_wq"] = rhs_slabs(np.ascontiguousarray(inp["L1_attn_q_project"].T), 8, 256)
    g["l1_wk"] = rhs_slabs(np.ascontiguousarray(inp["L1_attn_k_project"].T), 2, 256)
    g["l1_wv"] = rhs_slabs(np.ascontiguousarray(inp["L1_attn_v_project"].T), 2, 256)
    g["l1_wo"] = rhs_slabs(np.ascontiguousarray(inp["L1_attn_o_project"].T), 8, 256)
    g["l1_bq"] = np.ascontiguousarray(inp["L1_attn_q_bias"].reshape(1, 2048))
    g["l1_bk"] = np.ascontiguousarray(inp["L1_attn_k_bias"].reshape(1, 512))
    g["l1_bv"] = np.ascontiguousarray(inp["L1_attn_v_bias"].reshape(1, 512))
    g["l1_lnw"] = np.ascontiguousarray(np.broadcast_to(inp["L1_attn_ln_weight"], (128, 2048)))
    sw, lu, ld = inp["L1_ffn_select_weight"], inp["L1_ffn_lora_update"], inp["L1_ffn_lora_degen"]
    md = inp["L1_ffn_select_mean_diff"]
    def cat_t(a):
        return np.ascontiguousarray(
            a.transpose(1, 0, 2).reshape(16, 128, 64).transpose(1, 0, 2))
    g["sw_t"] = cat_t(sw)
    g["lu_t"] = cat_t(lu)
    g["ld_cat"] = np.ascontiguousarray(ld.transpose(0, 2, 1).reshape(64, 2048))
    def colT(a):
        return np.ascontiguousarray(a.T.reshape(16, 128, 8).transpose(1, 0, 2))
    g["md_t"] = colT(md)
    g["dw_t"] = colT(inp["L1_ffn_debias_w"])
    g["bw"] = np.ascontiguousarray(inp["L1_ffn_bias_w"])
    g["cen_bc"] = np.ascontiguousarray(np.broadcast_to(inp["L1_ffn_imp_center"], (128, 8)))
    g["slo_bc"] = np.ascontiguousarray(np.broadcast_to(inp["L1_ffn_imp_slope"], (128, 8)))
    g["ident"] = np.eye(128, dtype=f32)
    kp = np.arange(128)[:, None]
    qf = np.arange(512)[None, :]
    g["masks"] = np.ascontiguousarray(np.concatenate(
        [(qf >= kp + 128 * j).astype(f32) for j in range(4)], axis=1))
    pos = np.arange(T, dtype=np.float64)
    inv0 = 10000.0 ** (-np.arange(0, 64, 2).astype(np.float64) / 64.0)
    inv1 = 10000.0 ** (-np.arange(0, 128, 2).astype(np.float64) / 128.0)
    ang0 = (pos[:, None] * inv0[None, :]).astype(f32)
    ang1 = (pos[:, None] * inv1[None, :]).astype(f32)
    g["cos0_full"] = np.cos(ang0).astype(f32)
    g["sin0_full"] = np.sin(ang0).astype(f32)
    g["cos1_full"] = np.cos(ang1).astype(f32)
    g["sin1_full"] = np.sin(ang1).astype(f32)
    return g


def kvi_table(c):
    t = np.zeros((2, 2, 128, 192), np.int32)
    ZROW = 12352
    pp = np.arange(128)
    for l in range(2):
        G = 8 if l == 0 else 4
        HD = 64 if l == 0 else 128
        for qh in range(2):
            t5top = c if qh == 0 else 15 - c
            for it in range(16):
                t5 = t5top - it
                if t5 >= 0:
                    rank = t5 if t5 < 8 else 15 - t5
                    half = 0 if t5 < 8 else 1
                    kbase, kcol = rank * 1544, half * 512
                    vbase = rank * 1544 + 520 + 512 * half
                else:
                    kbase, kcol, vbase = ZROW, 0, ZROW
                for g in range(G):
                    if it * G + g < 16 * G:
                        rowk = kbase + (0 if t5 < 0 else g * HD) + (pp % HD)
                        t[l, qh, :, it * G + g] = (rowk * 1040 + kcol) // 8
                for j in range(4):
                    rowv = vbase + (0 if t5 < 0 else 128 * j) + pp
                    t[l, qh, :, 16 * G + it * 4 + j] = (rowv * 1040) // 8
    return t


def per_core_inputs(g, x_full):
    maps = []
    for c in range(NCORE):
        rows = core_token_rows(c)
        m = {k: v for k, v in g.items() if not k.endswith("_full")}
        m["xin"] = np.ascontiguousarray(x_full[rows])
        m["kvi"] = kvi_table(c)
        m["cos0"] = np.ascontiguousarray(g["cos0_full"][rows])
        m["sin0"] = np.ascontiguousarray(g["sin0_full"][rows])
        m["cos1"] = np.ascontiguousarray(g["cos1_full"][rows])
        m["sin1"] = np.ascontiguousarray(g["sin1_full"][rows])
        maps.append(m)
    return maps



class PB:

    def __init__(self, core_id):
        self.c = core_id
        self.nc = bass.Bass("TRN2", target_bir_lowering=False, debug=False)
        self.ctx = ExitStack()
        self.cnt = {}
        self.sems = {}
        self.hw = {}
        self.inputs = {}

    def din(self, name, shape, dtype=F32):
        ap = self.nc.dram_tensor(name, list(shape), dtype, kind="ExternalInput").ap()
        self.inputs[name] = ap
        return ap

    def dout(self, name, shape, dtype=F32):
        return self.nc.dram_tensor(name, list(shape), dtype, kind="ExternalOutput").ap()

    def dint(self, name, shape, dtype=BF16, **kw):
        return self.nc.dram_tensor(name, list(shape), dtype, kind="Internal", **kw).ap()

    def sb(self, name, shape, dtype):
        return self.ctx.enter_context(self.nc.sbuf_tensor(name, list(shape), dtype))

    def psum(self, name, shape, dtype=F32):
        return self.ctx.enter_context(self.nc.psum_tensor(name, list(shape), dtype))

    def sem(self, name):
        s = self.ctx.enter_context(self.nc.semaphore(name))
        self.sems[name] = s
        self.cnt[name] = 0
        return s

    def inc(self, name, inst, k=1):
        inst.then_inc(self.sems[name], k)
        self.cnt[name] += k
        return (name, self.cnt[name])

    def tok(self, name):
        return (name, self.cnt[name])

    def wait_eng(self, eng, engname, toks):
        for t in toks:
            if t is None:
                continue
            name, v = t
            if v <= 0:
                continue
            key = (engname, name)
            if self.hw.get(key, 0) < v:
                eng.wait_ge(self.sems[name], v)
                self.hw[key] = v

    @property
    def te(self): return self.nc.tensor
    @property
    def ve(self): return self.nc.vector
    @property
    def se(self): return self.nc.scalar
    @property
    def pe(self): return self.nc.gpsimd
    @property
    def sy(self): return self.nc.sync

    def wt(self, *toks): self.wait_eng(self.te, "T", toks)
    def wv(self, *toks): self.wait_eng(self.ve, "V", toks)
    def ws(self, *toks): self.wait_eng(self.se, "S", toks)
    def wp(self, *toks): self.wait_eng(self.pe, "P", toks)
    def wy(self, *toks): self.wait_eng(self.sy, "Y", toks)

    def all_toks(self):
        return [self.tok(s) for s in self.sems]

    def join(self):
        toks = self.all_toks()
        self.wt(*toks); self.wv(*toks); self.ws(*toks); self.wp(*toks); self.wy(*toks)
        return toks


class Ring:

    def __init__(self, views):
        self.views = views
        self.ready = [None] * len(views)
        self.free = [None] * len(views)
        self.i = 0

    def next(self):
        j = self.i % len(self.views)
        self.i += 1
        return j, self.views[j]


def alloc_common(p: PB):
    for s in ("dg", "ds", "da", "cc", "eT", "eV", "eS", "eP"):
        p.sem(s)
    p.psA = p.psum("psA", [128, 1024])
    p.psB = p.psum("psB", [128, 1024])
    p.psC = p.psum("psC", [128, 1024])
    p.psD = p.psum("psD", [128, 1024])
    p.stream = p.sb("stream", [128, NT * D], F32)
    p.abuf = p.sb("abuf", [128, 16384], BF16)
    p.bigbuf = p.sb("bigbuf", [128, 24704], BF16)
    p.scr32 = p.sb("scr32", [128, 2560], F32)
    p.wbuf = p.sb("wbuf", [128, 8192], BF16)
    p.ptbuf = p.sb("ptbuf", [128, 4096], BF16)
    p.kvbuf = p.sb("kvbuf", [128, 2112], BF16)
    p.trash = p.sb("trash", [128, 1024], BF16)
    p.identb = p.sb("identb", [128, 128], BF16)
    p.maskb = p.sb("maskb", [128, 2048], BF16)
    p.cos0b = p.sb("cos0b", [128, 256], F32)
    p.sin0b = p.sb("sin0b", [128, 256], F32)
    p.cos1b = p.sb("cos1b", [128, 512], F32)
    p.sin1b = p.sb("sin1b", [128, 512], F32)
    p.mcol = p.sb("mcol", [128, 8], F32)
    p.acol = p.sb("acol", [128, 8], F32)
    p.stat = p.sb("stat", [128, 8], F32)
    p.statS = p.sb("statS", [128, 4], F32)
    p.birow = p.sb("birow", [1, 3072], BF16)
    p.onesc = p.sb("onesc", [1, 128], BF16)
    p.onesf = p.sb("onesf", [1, 128], F32)
    p.onek = p.sb("onek", [128, 2], BF16)
    p.qn = p.sb("qn", [128, 32], F32)
    p.negb = p.sb("negb", [128, 32], F32)
    p.knrow = p.sb("knrow", [8, 1024], BF16)
    p.smallf = p.sb("smallf", [1, 128], F32)
    p.linv = p.sb("linv", [1, 512], F32)
    p.lbc = p.scr32[:, 0:1024]
    p.gsb = p.sb("gsb", [128, 44], F32)
    p.usb = p.sb("usb", [128, 44], F32)
    p.cenb = p.sb("cenb", [128, 8], F32)
    p.slob = p.sb("slob", [128, 8], F32)
    p.l1tmp = p.sb("l1tmp", [128, 64], F32)
    p.ccol = p.sb("ccol", [64, 1], F32)
    p.kvib = p.sb("kvib", [128, 192], mybir.dt.int32)
    p.trviews = [p.psD[:, k * 64:(k + 1) * 64].bitcast(BF16) for k in range(16)]


TR_ORDER = [0, 8, 1, 9, 2, 10, 3, 11, 4, 12, 5, 13, 6, 14, 7, 15]


class TrSeq:

    def __init__(self, p):
        self.p = p
        self.k = 0
        self.view_free = [None] * 16
        self.hist = []

    def transpose(self, src, dst, extra_dep=(), rows=128):
        p = self.p
        P = src.shape[0]
        F = src.shape[-1]
        v = TR_ORDER[self.k % 16]
        trv = p.trviews[v][0:F, 0:P]
        if len(self.hist) >= 2:
            p.wt(self.hist[-2])
        p.wt(self.view_free[v], *extra_dep)
        tt = p.inc("eT", p.te.transpose(trv, src, p.identb[0:P, 0:P]))
        p.ws(tt)
        tev = p.inc("eS", p.se.copy(dst, trv))
        self.view_free[v] = tev
        self.hist.append(tev)
        self.k += 1
        return tev


def emit_prologue(p: PB):
    g = p.pe
    p.inc("dg", g.dma_start(out=p.identb[:, :], in_=p.inputs["ident"][:, :]), 16)
    p.inc("dg", g.dma_start(out=p.maskb[:, :], in_=p.inputs["masks"][:, :]), 16)
    for nm, dst, w in (("cos0", p.cos0b, 32), ("sin0", p.sin0b, 32),
                       ("cos1", p.cos1b, 64), ("sin1", p.sin1b, 64)):
        p.inc("ds", p.sy.dma_start(
            out=dst[:, :].rearrange("p (i w) -> p i w", w=w),
            in_=p.inputs[nm].rearrange("(i p) w -> p i w", p=128)), 16)
    p.inc("dg", g.dma_start(out=p.gsb[:, :], in_=p.inputs["l0_gs"][:, :]), 16)
    p.inc("dg", g.dma_start(out=p.usb[:, :], in_=p.inputs["l0_us"][:, :]), 16)
    p.inc("dg", g.dma_start(out=p.cenb[:, :], in_=p.inputs["cen_bc"][:, :]), 16)
    p.inc("dg", g.dma_start(out=p.slob[:, :], in_=p.inputs["slo_bc"][:, :]), 16)
    p.inc("eV", p.ve.memset(p.onesc[:, :], 1.0))
    p.inc("eV", p.ve.memset(p.onesf[:, :], 1.0))
    p.inc("eV", p.ve.memset(p.onek[:, :], 1.0))


def emit_stream_load(p: PB, src_ap):
    return p.inc("ds", p.sy.dma_start(
        out=p.stream[:, :].rearrange("p (i d) -> p i d", d=D),
        in_=src_ap.rearrange("(i p) d -> p i d", p=128)), 16)


def emit_stream_store(p: PB, dst_ap):
    return p.inc("ds", p.sy.dma_start(
        out=dst_ap.rearrange("(i p) d -> p i d", p=128),
        in_=p.stream[:, :].rearrange("p (i d) -> p i d", d=D)), 16)


def emit_ln(p: PB, mode, lnw_view=None):
    stg_ring = Ring([p.ptbuf[:, 0:2048], p.ptbuf[:, 2048:4096]])
    trseq = TrSeq(p)
    prev_act_read = None
    for i in range(NT):
        st = p.stream[:, i * D:(i + 1) * D]
        tv = p.inc("eV", p.ve.tensor_reduce(p.stat[:, 0:1], st, AX.X, ALU.add))
        p.ws(prev_act_read)
        p.se.activation(
            p.trash[:, 0:1024], st[:, 0:1024], AF.Square, bias=0.0, scale=1.0,
            accum_out=p.statS[:, 0:1])
        ts_ = p.inc("eS", p.se.activation(
            p.trash[:, 0:1024], st[:, 1024:2048], AF.Square, bias=0.0, scale=1.0,
            accum_out=p.statS[:, 1:2]))
        p.wv(tv, ts_)
        t1 = p.inc("eV", p.ve.tensor_scalar_mul(p.mcol[:, i:i + 1], p.stat[:, 0:1], 1.0 / D))
        t2 = p.inc("eV", p.ve.tensor_tensor(p.stat[:, 2:3], p.statS[:, 0:1], p.statS[:, 1:2], ALU.add))
        p.wv(t2)
        t3 = p.inc("eV", p.ve.tensor_scalar_mul(p.stat[:, 2:3], p.stat[:, 2:3], 1.0 / D))
        prev_act_read = p.tok("eV")
        p.wv(t1)
        t4 = p.inc("eV", p.ve.tensor_tensor(
            p.stat[:, 3:4], p.mcol[:, i:i + 1], p.mcol[:, i:i + 1], ALU.mult))
        p.wv(t3, t4)
        t5 = p.inc("eV", p.ve.tensor_tensor(
            p.stat[:, 4:5], p.stat[:, 2:3], p.stat[:, 3:4], ALU.subtract))
        p.wv(t5)
        if mode == "l1attn":
            tden = p.inc("eV", p.ve.tensor_scalar(
                p.stat[:, 5:6], p.stat[:, 4:5], 1.0 + 1e-6, 1e-11, ALU.mult, ALU.add))
        else:
            tden = p.inc("eV", p.ve.tensor_scalar(
                p.stat[:, 5:6], p.stat[:, 4:5], 1.0, 1e-5, ALU.mult, ALU.add))
        p.ws(tden)
        tsq = p.inc("eS", p.se.sqrt(p.statS[:, 2:3], p.stat[:, 5:6]))
        p.wv(tsq)
        trec = p.inc("eV", p.ve.reciprocal(p.acol[:, i:i + 1], p.statS[:, 2:3]))
        p.wv(trec)
        sj, stg = stg_ring.next()
        p.wv(stg_ring.free[sj])
        if mode == "l1attn":
            p.inc("eV", p.ve.tensor_scalar(
                p.scr32[:, 0:2048], st, p.mcol[:, i:i + 1], p.acol[:, i:i + 1],
                ALU.subtract, ALU.mult))
            txn = p.inc("eV", p.ve.tensor_tensor(
                stg, p.scr32[:, 0:2048], lnw_view, ALU.mult))
        else:
            txn = p.inc("eV", p.ve.tensor_scalar(
                stg, st, p.mcol[:, i:i + 1], p.acol[:, i:i + 1],
                ALU.subtract, ALU.mult))
        stg_ring.ready[sj] = txn
        for kc in range(KC):
            trseq.transpose(
                stg[:, kc * 128:(kc + 1) * 128],
                p.abuf[:, kc * 1024 + i * 128: kc * 1024 + (i + 1) * 128],
                extra_dep=(txn,))
        stg_ring.free[sj] = p.tok("eS")
    return p.tok("eS")


def emit_ffn0(p: PB, first_time=True):
    HT = p.bigbuf
    gate, up = p.inputs["l0_gate"], p.inputs["l0_up"]
    dd = p.ddown

    evac_done = None
    stream_evac = [None] * NT
    slot_last = [None, None]
    for half in range(2):
        gu_free = list(slot_last)
        psum_free = None
        for j in range(22):
            m = half * 22 + j
            k = j % 2
            gslot = p.wbuf[:, k * 4096:k * 4096 + 2048]
            uslot = p.wbuf[:, k * 4096 + 2048:k * 4096 + 4096]
            p.wp(gu_free[k])
            tg = p.inc("dg", p.pe.dma_start(
                out=gslot.rearrange("p (kc m) -> p kc m", m=128), in_=gate[m]), 16)
            tu = p.inc("dg", p.pe.dma_start(
                out=uslot.rearrange("p (kc m) -> p kc m", m=128), in_=up[m]), 16)
            p.wt(tg, tu, psum_free, evac_done)
            mm = None
            for ng in range(2):
                for kc in range(KC):
                    mm = p.te.matmul(
                        p.psA[:, ng * 512:(ng + 1) * 512],
                        gslot[:, kc * 128:(kc + 1) * 128],
                        p.abuf[:, kc * 1024 + ng * 512: kc * 1024 + (ng + 1) * 512],
                        start=(kc == 0), stop=(kc == KC - 1))
            tG = p.inc("eT", mm)
            for ng in range(2):
                for kc in range(KC):
                    mm = p.te.matmul(
                        p.psB[:, ng * 512:(ng + 1) * 512],
                        uslot[:, kc * 128:(kc + 1) * 128],
                        p.abuf[:, kc * 1024 + ng * 512: kc * 1024 + (ng + 1) * 512],
                        start=(kc == 0), stop=(kc == KC - 1))
            tU = p.inc("eT", mm)
            gu_free[k] = tU
            gstage = p.ptbuf[:, 0:1024]
            p.ws(tG, psum_free)
            tsg = p.inc("eS", p.se.activation(
                gstage, p.psA[:, 0:1024], AF.Silu, bias=0.0,
                scale=p.gsb[:, m:m + 1]))
            p.wv(tU, tsg)
            th = p.inc("eV", p.ve.scalar_tensor_tensor(
                HT[:, j * 1024:(j + 1) * 1024], p.psB[:, 0:1024],
                p.usb[:, m:m + 1], gstage, ALU.mult, ALU.mult))
            psum_free = th
        h_done = p.tok("eV")
        dn_free = [p.tok("eT"), p.tok("eT")]
        tmm = None
        for tp in range(NT // 2):
            tts = (2 * tp, 2 * tp + 1)
            for j in range(22):
                k = j % 2
                dslot = p.wbuf[:, k * 4096:k * 4096 + 2048]
                p.wy(dn_free[k])
                tload = p.inc("ds", p.sy.dma_start(out=dslot, in_=dd[half * 22 + j]), 16)
                p.wt(tload, h_done)
                if j == 0:
                    p.wt(evac_done)
                mm = None
                for ti_, tt in enumerate(tts):
                    for ng in range(4):
                        pst = (p.psA, p.psB, p.psC, p.psD)[ti_ * 2 + ng // 2]
                        mm = p.te.matmul(
                            pst[:, (ng % 2) * 512:(ng % 2 + 1) * 512],
                            HT[:, j * 1024 + tt * 128: j * 1024 + (tt + 1) * 128],
                            dslot[:, ng * 512:(ng + 1) * 512],
                            start=(j == 0), stop=(j == 21))
                tmm = p.inc("eT", mm)
                dn_free[k] = tmm
                slot_last[k] = tmm
            p.wv(tmm)
            if half == 1:
                p.wv(stream_evac[tts[0]], stream_evac[tts[1]])
            for ti_, tt in enumerate(tts):
                for ngg in range(2):
                    pst = (p.psA, p.psB, p.psC, p.psD)[ti_ * 2 + ngg]
                    ssl = slice(tt * D + ngg * 1024, tt * D + (ngg + 1) * 1024)
                    p.inc("eV", p.ve.tensor_tensor(
                        p.stream[:, ssl], p.stream[:, ssl], pst[:, 0:1024], ALU.add))
                stream_evac[tt] = p.tok("eV")
            evac_done = p.tok("eV")
        p.wt(p.tok("eT"))
    return p.tok("eV")


def emit_down_precast(p: PB):
    down = p.inputs["l0_down"]
    dd = p.ddown
    tds = p.inc("ds", p.sy.dma_start(out=p.scr32[:, 0:2048], in_=p.inputs["l0_ds"][:, :]), 16)
    slot_free = [None, None]
    for m in range(IC):
        k = m % 2
        slot = p.wbuf[:, k * 4096:k * 4096 + 2048]
        slot2 = p.wbuf[:, k * 4096 + 2048:k * 4096 + 4096]
        p.wp(slot_free[k])
        tl = p.inc("dg", p.pe.dma_start(out=slot, in_=down[m]), 16)
        p.wv(tl, tds)
        tsc = p.inc("eV", p.ve.tensor_tensor(slot2, slot, p.scr32[:, 0:2048], ALU.mult))
        p.wy(tsc)
        slot_free[k] = p.inc("ds", p.sy.dma_start(out=dd[m], in_=slot2), 16)
    return p.tok("ds")



import math

QT_OFF = 0
KT_OFF = 16384
VR_OFF = 20480


def attn_params(l):
    if l == 0:
        return dict(G=G0, HD=HD0, H=H0, scale=1.0 / 8.0,
                    Wv=8 * 65, vstride=65, ones_col=True,
                    wq="l0_wq", wk="l0_wk", wv="l0_wv", wo="l0_wo",
                    bq="l0_bq", bk="l0_bk", bv="l0_bv", bo="l0_bo")
    return dict(G=G1, HD=HD1, H=H1, scale=1.0 / math.sqrt(128.0),
                Wv=4 * 128 + 8, vstride=128, ones_col=False,
                wq="l1_wq", wk="l1_wk", wv="l1_wv", wo="l1_wo",
                bq="l1_bq", bk="l1_bk", bv="l1_bv", bo=None)


def emit_qkv(p: PB, l):
    pr = attn_params(l)
    G, HD, H = pr["G"], pr["HD"], pr["H"]
    hw = HD // 2
    cosb = p.cos0b if l == 0 else p.cos1b
    sinb = p.sin0b if l == 0 else p.sin1b
    tb1 = p.inc("dg", p.pe.dma_start(out=p.birow[0:1, 0:2048], in_=p.inputs[pr["bq"]][:, :]), 16)
    tb2 = p.inc("dg", p.pe.dma_start(out=p.birow[0:1, 2048:2560], in_=p.inputs[pr["bk"]][:, :]), 16)
    tb3 = p.inc("dg", p.pe.dma_start(out=p.birow[0:1, 2560:3072], in_=p.inputs[pr["bv"]][:, :]), 16)
    tones = None
    if pr["ones_col"]:
        tones = p.inc("eV", p.ve.memset(
            p.bigbuf[:, VR_OFF:VR_OFF + 8 * pr["Wv"]]
            .rearrange("p (t g dd) -> p t g dd", g=G, dd=65)[:, :, :, 64:65], 1.0))
    else:
        tones = p.inc("eV", p.ve.memset(
            p.bigbuf[:, VR_OFF:VR_OFF + 8 * pr["Wv"]]
            .rearrange("p (t w) -> p t w", w=pr["Wv"])[:, :, 512:520], 1.0))
    tqn = p.inc("eV", p.ve.memset(p.qn[:, 0:H], 0.0))
    p.wv(tqn)

    trseq = TrSeq(p)
    pslots = [p.psA[:, 0:256], p.psB[:, 0:256], p.psC[:, 0:256]]
    slot_read = [None, None, None]
    sidx = 0
    stg_slots = [p.ptbuf[:, 2048 + 256 * k:2048 + 256 * (k + 1)] for k in range(2)]
    stg_free = [None, None]
    wslots = [p.wbuf[:, 0:4096], p.wbuf[:, 4096:8192]]
    w_free = [None, None]
    p_scnt = 0
    projs = [("q", pr["wq"], 8, 0), ("k", pr["wk"], 2, 2048), ("v", pr["wv"], 2, 2560)]
    for proj, wname, nslab, boff in projs:
        W = p.inputs[wname]
        for s in range(nslab):
            wk_ = p_scnt % 2
            p_scnt += 1
            wslot = wslots[wk_]
            p.wp(w_free[wk_])
            twl = p.inc("dg", p.pe.dma_start(
                out=wslot.rearrange("p (kc n) -> p kc n", n=256), in_=W[s]), 16)
            for ti in range(NT):
                j = sidx % 3
                ps = pslots[j]
                sidx += 1
                p.wt(slot_read[j], twl, tb1, tb2, tb3)
                for kc in range(KC):
                    p.te.matmul(ps, p.abuf[:, kc * 1024 + ti * 128:kc * 1024 + (ti + 1) * 128],
                                wslot[:, kc * 256:(kc + 1) * 256],
                                start=(kc == 0), stop=False)
                tmm = p.inc("eT", p.te.matmul(
                    ps, p.onesc[0:1, 0:128], p.birow[0:1, boff + s * 256:boff + s * 256 + 256],
                    start=False, stop=True))
                w_free[wk_] = tmm
                if proj == "v":
                    p.wv(tmm, tones)
                    if pr["ones_col"]:
                        dst = (p.bigbuf[:, VR_OFF + ti * pr["Wv"]:VR_OFF + (ti + 1) * pr["Wv"]]
                               .rearrange("p (g dd) -> p g dd", dd=65)[:, 4 * s:4 * s + 4, 0:64])
                        srcv = ps.rearrange("p (g dd) -> p g dd", dd=64)
                    else:
                        dst = p.bigbuf[:, VR_OFF + ti * pr["Wv"] + s * 256:VR_OFF + ti * pr["Wv"] + s * 256 + 256]
                        srcv = ps
                    tev = p.inc("eV", p.ve.tensor_copy(dst, srcv))
                    slot_read[j] = tev
                else:
                    nh = 256 // HD
                    C3 = cosb[:, ti * hw:(ti + 1) * hw].unsqueeze(1).broadcast_to([128, nh, hw])
                    S3 = sinb[:, ti * hw:(ti + 1) * hw].unsqueeze(1).broadcast_to([128, nh, hw])
                    ps3 = ps.rearrange("p (h w) -> p h w", w=HD)
                    psev, psod = ps3[:, :, 0:hw], ps3[:, :, hw:HD]
                    t1 = p.scr32[:, 2048:2176].rearrange("p (h w) -> p h w", w=hw)[:, 0:nh]
                    t2 = p.scr32[:, 2176:2304].rearrange("p (h w) -> p h w", w=hw)[:, 0:nh]
                    sk = sidx % 2
                    stg = stg_slots[sk]
                    stg3 = stg.rearrange("p (h w) -> p h w", w=HD)
                    p.wv(tmm, stg_free[sk])
                    p.inc("eV", p.ve.tensor_tensor(t1, psev, C3, ALU.mult))
                    i2 = p.inc("eV", p.ve.tensor_tensor(t2, psod, S3, ALU.mult))
                    p.wv(i2)
                    i3 = p.inc("eV", p.ve.tensor_tensor(stg3[:, :, 0:hw], t1, t2, ALU.subtract))
                    p.inc("eV", p.ve.tensor_tensor(t1, psod, C3, ALU.mult))
                    i5 = p.inc("eV", p.ve.tensor_tensor(t2, psev, S3, ALU.mult))
                    p.wv(i5)
                    i6 = p.inc("eV", p.ve.tensor_tensor(stg3[:, :, hw:HD], t1, t2, ALU.add))
                    slot_read[j] = i5
                    p.ws(i3, i6)
                    sq = p.inc("eS", p.se.activation(p.trash[:, 0:256], stg, AF.Square,
                                                     bias=0.0, scale=1.0))
                    p.wv(sq)
                    if proj == "q":
                        tr_ = p.inc("eV", p.ve.tensor_reduce(
                            p.stat[:, 0:nh], p.trash[:, 0:256].rearrange("p (h w) -> p h w", w=HD),
                            AX.X, ALU.add))
                        p.wv(tr_)
                        hb = s * nh
                        tmx = p.inc("eV", p.ve.tensor_tensor(
                            p.qn[:, hb:hb + nh], p.qn[:, hb:hb + nh], p.stat[:, 0:nh], ALU.max))
                        p.wv(tmx)
                    else:
                        gb = s * nh
                        tr_ = p.inc("eV", p.ve.tensor_reduce(
                            p.l1tmp[:, ti * 8 + gb:ti * 8 + gb + nh],
                            p.trash[:, 0:256].rearrange("p (h w) -> p h w", w=HD),
                            AX.X, ALU.add))
                        p.wv(tr_)
                    for jj in range(2):
                        kc_out = s * 2 + jj
                        base = QT_OFF if proj == "q" else KT_OFF
                        trseq.transpose(
                            stg[:, jj * 128:(jj + 1) * 128],
                            p.bigbuf[:, base + kc_out * 1024 + ti * 128:base + kc_out * 1024 + (ti + 1) * 128],
                            extra_dep=(i3, i6))
                    stg_free[sk] = p.tok("eS")
    return trseq


def emit_gather(p: PB, l, trseq):
    pr = attn_params(l)
    G, HD, H = pr["G"], pr["HD"], pr["H"]
    rep = H // G
    cc_in = p.cc_in[l]
    cc_out = p.cc_out[l]
    Wv0 = pr["Wv"]
    tz = p.inc("eV", p.ve.memset(p.trash[:, 0:1024], 0.0))
    p.wy(tz)
    for r0 in range(520, 1544, 128):
        r1 = min(r0 + 128, 1544)
        p.inc("ds", p.sy.dma_start(out=cc_in[r0:r1, Wv0:1040], in_=p.trash[0:r1 - r0, 0:1040 - Wv0]), 16)
    for r0 in range(0, 520, 128):
        r1 = min(r0 + 128, 520)
        p.inc("ds", p.sy.dma_start(out=cc_in[r0:r1, 1024:1040], in_=p.trash[0:r1 - r0, 0:16]), 16)
    for r0 in range(12352, 13120, 128):
        p.inc("ds", p.sy.dma_start(out=cc_out[r0:r0 + 128, 0:1024], in_=p.trash[:, 0:1024]), 16)
        p.inc("ds", p.sy.dma_start(out=cc_out[r0:r0 + 128, 1024:1040], in_=p.trash[:, 0:16]), 16)
    p.wv(p.tok("ds"))
    for ti in range(NT):
        tc_ = p.inc("eV", p.ve.tensor_copy(p.trash[:, 256:256 + 8], p.l1tmp[:, ti * 8:ti * 8 + 8]))
        p.wv(tc_)
        trseq.transpose(p.trash[:, 256:264], p.knrow[0:8, ti * 128:(ti + 1) * 128],
                        extra_dep=(tc_,), rows=8)
    knt = p.tok("eS")
    for kc in range(4):
        p.wy(p.tok("eS"))
        p.inc("ds", p.sy.dma_start(
            out=cc_in[kc * 128:(kc + 1) * 128, 0:1024],
            in_=p.bigbuf[:, KT_OFF + kc * 1024:KT_OFF + (kc + 1) * 1024]), 16)
    p.wy(knt)
    p.inc("ds", p.sy.dma_start(out=cc_in[512:520, 0:1024], in_=p.knrow[0:8, :]), 16)
    Wv = pr["Wv"]
    for ti in range(NT):
        p.inc("ds", p.sy.dma_start(
            out=cc_in[520 + 128 * ti:520 + 128 * (ti + 1), 0:Wv],
            in_=p.bigbuf[:, VR_OFF + ti * Wv:VR_OFF + (ti + 1) * Wv]), 16)
    p.wp(p.tok("ds"), p.tok("eV"), p.tok("eS"))
    tcc = p.inc("cc", p.pe.collective_compute(
        "AllGather", ALU.bypass, replica_groups=[list(range(8))],
        ins=[cc_in[:, :].opt()], outs=[cc_out[0:12352, :].opt()]))
    p.cc_done = tcc

    tq0 = p.inc("eV", p.ve.tensor_copy(p.trash[:, 256:256 + H], p.qn[:, 0:H]))
    p.wv(tq0)
    tq1 = trseq.transpose(p.trash[:, 256:256 + H], p.ptbuf[0:H, 0:128], rows=H, extra_dep=(tq0,))
    p.wv(tq1)
    tq2 = p.inc("eV", p.ve.tensor_reduce(p.stat[0:H, 0:1], p.ptbuf[0:H, 0:128], AX.X, ALU.max))
    p.wv(tq2)
    tq3 = p.inc("eV", p.ve.tensor_copy(p.trash[0:H, 256:257], p.stat[0:H, 0:1]))
    p.wv(tq3)
    tq4 = trseq.transpose(p.trash[0:H, 256:257], p.smallf[0:1, 0:H], rows=1, extra_dep=(tq3,))
    p.wp(tcc)
    p.wy(tcc)
    tkm = None
    for r in range(NCORE):
        p.wy(tkm)
        tld = p.inc("ds", p.sy.dma_start(
            out=p.knrow[0:8, :], in_=cc_out[r * 1544 + 512:r * 1544 + 520, 0:1024]), 16)
        p.wv(tld)
        tr_ = p.inc("eV", p.ve.tensor_reduce(p.stat[0:8, 1:2], p.knrow[0:8, :], AX.X, ALU.max))
        p.wv(tr_)
        if r == 0:
            tkm = p.inc("eV", p.ve.tensor_copy(p.stat[0:8, 2:3], p.stat[0:8, 1:2]))
        else:
            tkm = p.inc("eV", p.ve.tensor_tensor(
                p.stat[0:8, 2:3], p.stat[0:8, 2:3], p.stat[0:8, 1:2], ALU.max))
        p.wv(tkm)
    tk3 = p.inc("eV", p.ve.tensor_copy(p.trash[0:8, 256:257], p.stat[0:8, 2:3]))
    p.wv(tk3)
    tk4 = trseq.transpose(p.trash[0:8, 256:257], p.smallf[0:1, 32:40], rows=1, extra_dep=(tk3,))
    p.wv(tq4, tk4)
    kr = p.smallf[0:1, 32:32 + G].unsqueeze(2).broadcast_to([1, G, rep])
    tn1 = p.inc("eV", p.ve.tensor_tensor(
        p.smallf[0:1, 64:64 + H].rearrange("o (g r) -> o g r", r=rep),
        p.smallf[0:1, 0:H].rearrange("o (g r) -> o g r", r=rep), kr, ALU.add))
    p.wv(tn1)
    tn2 = p.inc("eV", p.ve.tensor_scalar_mul(
        p.smallf[0:1, 96:96 + H], p.smallf[0:1, 64:64 + H], -1.02 * pr["scale"] * 0.5))
    p.wv(tn2)
    tn3 = p.inc("eV", p.ve.tensor_copy(p.trash[0:1, 256:256 + H], p.smallf[0:1, 96:96 + H]))
    p.wt(tn3)
    tn4 = p.inc("eT", p.te.matmul(p.psA[:, 0:H], p.onesc[0:1, 0:128],
                                  p.trash[0:1, 256:256 + H], start=True, stop=True))
    p.wv(tn4)
    tn5 = p.inc("eV", p.ve.tensor_copy(p.negb[:, 0:H], p.psA[:, 0:H]))
    p.wv(tn5)
    return tn5


def emit_attn_loop(p: PB, l):
    pr = attn_params(l)
    G, HD, H = pr["G"], pr["HD"], pr["H"]
    rep = H // G
    cc_out = p.cc_out[l]
    Wv = pr["Wv"]
    kslots = [p.kvbuf[:, 0:512], p.kvbuf[:, 512:1024]]
    vslot_w = 544
    vslots = [p.kvbuf[:, 1024 + k * vslot_w:1024 + k * vslot_w + Wv] for k in range(2)]
    ptslots = [p.ptbuf[:, 512 * k:512 * (k + 1)] for k in range(4)]
    k_free = [None, None]
    v_free = [None, None]
    pt_free = [None, None, None, None]
    sA_read, sB_read = None, None
    o_read = [None, None]
    nbatch = 1 if l == 0 else 2
    hpb = 4 if l == 0 else 2
    kcnt, vcnt, ptcnt = 0, 0, 0
    ev_scr_read = None
    if not hasattr(p, "cc_flat"):
        p.cc_flat = {}
    p.cc_flat[l] = cc_out.rearrange("r (c e) -> (r c) e", e=8)
    pend = [None]

    def _emit_av(d):
        p.wt(d["texp"], d["tmask"], d["tvl"])
        if d["start"]:
            p.wt(o_read[0], o_read[1])
        tav = p.inc("eT", p.te.matmul(
            d["ovs"], d["vls"], d["pt"], start=d["start"], stop=d["stop"]))
        pt_free[d["ptk"]] = tav
        if l == 1:
            tav = p.inc("eT", p.te.matmul(
                d["lvs"], d["ind"], d["pt"], start=d["start"], stop=d["stop"]))
            pt_free[d["ptk"]] = tav
        k_free[d["kk"]] = p.tok("eT")
        v_free[d["vv"]] = p.tok("eT")

    for qhalf in range(2):
        NIT = 8 if qhalf == 0 else 16
        qsl = slice(qhalf * 512, qhalf * 512 + 512)
        p.wy(p.tok("dg"))
        p.kvib_ready = p.inc("ds", p.sy.dma_start(
            out=p.kvib[:, :], in_=p.inputs["kvi"][l, qhalf]), 16)
        p.wp(p.kvib_ready)
        for g in range(G):
            for hb in range(nbatch):
                heads = [g * rep + hb * hpb + i for i in range(hpb)]
                if l == 0:
                    ovs = [p.psC[0:65, 0:512], p.psC[0:65, 512:1024],
                           p.psD[0:65, 0:512], p.psD[0:65, 512:1024]]
                    lvs = [None] * 4
                else:
                    ovs = [p.psC[:, 0:512], p.psC[:, 512:1024]]
                    lvs = [p.psD[0:1, 0:512], p.psD[0:1, 512:1024]]
                for it in range(NIT):
                    kk = kcnt % 2
                    kcnt += 1
                    p.wp(k_free[kk], p.kvib_ready)
                    tkl = p.inc("dg", p.pe.indirect_dma_start(
                        kslots[kk][:, :], None, p.cc_flat[l],
                        bass.IndirectOffsetOnAxis(p.kvib[:, it * G + g:it * G + g + 1], 0)), 16)
                    for j in range(4):
                        vv = vcnt % 2
                        vcnt += 1
                        p.wp(v_free[vv])
                        tvl = p.inc("dg", p.pe.indirect_dma_start(
                            vslots[vv], None, p.cc_flat[l],
                            bass.IndirectOffsetOnAxis(
                                p.kvib[:, 16 * G + it * 4 + j:16 * G + it * 4 + j + 1], 0)), 16)
                        for hi, h in enumerate(heads):
                            use_a = (ptcnt % 2 == 0)
                            sps = p.psA[:, 0:512] if use_a else p.psB[:, 0:512]
                            p.wt(tkl, sA_read if use_a else sB_read)
                            qrow = (h % 2) * 64 if l == 0 else 0
                            qchunk = h // 2 if l == 0 else h
                            tsm = p.inc("eT", p.te.matmul(
                                sps, kslots[kk][qrow:qrow + HD, j * 128:(j + 1) * 128],
                                p.bigbuf[qrow:qrow + HD,
                                         QT_OFF + qchunk * 1024 + qhalf * 512:
                                         QT_OFF + qchunk * 1024 + qhalf * 512 + 512],
                                start=True, stop=True))
                            pt = ptslots[ptcnt % 4]
                            ptk = ptcnt % 4
                            ptcnt += 1
                            p.ws(tsm, pt_free[ptk])
                            texp = p.inc("eS", p.se.activation(
                                pt, sps, AF.Exp, bias=p.negb[:, h:h + 1], scale=pr["scale"]))
                            if use_a:
                                sA_read = texp
                            else:
                                sB_read = texp
                            tmask = None
                            if it == 0:
                                p.wv(texp)
                                tmask = p.inc("eV", p.ve.tensor_tensor(
                                    pt, pt, p.maskb[:, j * 512:(j + 1) * 512], ALU.mult))
                            if pend[0] is not None:
                                _emit_av(pend[0])
                            pend[0] = dict(
                                ovs=ovs[hi], lvs=lvs[hi],
                                vls=vslots[vv][:, g * pr["vstride"]:g * pr["vstride"] + (65 if l == 0 else 128)],
                                ind=vslots[vv][:, 512:513],
                                pt=pt, ptk=ptk, texp=texp, tmask=tmask, tvl=tvl,
                                start=(it == 0 and j == 0),
                                stop=(it == NIT - 1 and j == 3),
                                kk=kk, vv=vv)
                if pend[0] is not None:
                    _emit_av(pend[0])
                    pend[0] = None
                tavl = p.tok("eT")
                for hi, h in enumerate(heads):
                    if l == 0:
                        lrow = ovs[hi][64:65, :]
                        orows = ovs[hi][0:64, :]
                        dst = p.abuf[(h % 2) * 64:(h % 2) * 64 + 64,
                                     (h // 2) * 1024 + qhalf * 512:(h // 2) * 1024 + qhalf * 512 + 512]
                        drows = 64
                    else:
                        lrow = lvs[hi]
                        orows = ovs[hi]
                        dst = p.abuf[:, h * 1024 + qhalf * 512:h * 1024 + qhalf * 512 + 512]
                        drows = 128
                    p.wv(tavl)
                    tr1 = p.inc("eV", p.ve.reciprocal(p.linv[0:1, 0:512], lrow))
                    p.wt(tr1, sA_read, sB_read)
                    tb = p.inc("eT", p.te.matmul(
                        p.psA[0:drows, 0:512], p.onesf[0:1, 0:drows],
                        p.linv[0:1, 0:512], start=True, stop=True))
                    p.ws(tb, ev_scr_read)
                    tcp = p.inc("eS", p.se.copy(p.scr32[0:drows, 0:512], p.psA[0:drows, 0:512]))
                    sA_read = tcp
                    p.wv(tcp)
                    tev = p.inc("eV", p.ve.tensor_tensor(dst, orows, p.scr32[0:drows, 0:512], ALU.mult))
                    ev_scr_read = tev
                    if l == 0:
                        o_read[hi // 2] = tev
                    else:
                        o_read[0] = tev
                        o_read[1] = tev
    return p.tok("eV")


def emit_outproj(p: PB, l):
    pr = attn_params(l)
    W = p.inputs[pr["wo"]]
    has_bias = pr["bo"] is not None
    if has_bias:
        tbo = p.inc("dg", p.pe.dma_start(out=p.birow[0:1, 0:2048], in_=p.inputs[pr["bo"]][:, :]), 16)
    else:
        tbo = None
    wslots = [p.wbuf[:, 0:4096], p.wbuf[:, 4096:8192]]
    w_free = [None, None]
    pslots = [p.psA[:, 0:256], p.psB[:, 0:256], p.psC[:, 0:256]]
    slot_read = [None, None, None]
    sidx = 0
    for s in range(8):
        wk_ = s % 2
        wslot = wslots[wk_]
        p.wp(w_free[wk_])
        twl = p.inc("dg", p.pe.dma_start(
            out=wslot.rearrange("p (kc n) -> p kc n", n=256), in_=W[s]), 16)
        for ti in range(NT):
            j = sidx % 3
            ps = pslots[j]
            sidx += 1
            p.wt(slot_read[j], twl, tbo)
            mm = None
            for kc in range(KC):
                mm = p.te.matmul(ps, p.abuf[:, kc * 1024 + ti * 128:kc * 1024 + (ti + 1) * 128],
                                 wslot[:, kc * 256:(kc + 1) * 256],
                                 start=(kc == 0), stop=(not has_bias and kc == KC - 1))
            if has_bias:
                mm = p.te.matmul(
                    ps, p.onesc[0:1, 0:128], p.birow[0:1, s * 256:s * 256 + 256],
                    start=False, stop=True)
            tmm = p.inc("eT", mm)
            w_free[wk_] = tmm
            p.wv(tmm)
            ssl = slice(ti * D + s * 256, ti * D + s * 256 + 256)
            tev = p.inc("eV", p.ve.tensor_tensor(
                p.stream[:, ssl], p.stream[:, ssl], ps, ALU.add))
            slot_read[j] = tev
    return p.tok("eV")



F1_SW, F1_LU, F1_MD, F1_DW, F1_LD, F1_BW = 0, 1024, 2048, 2176, 2304, 4352


def emit_ffn1(p: PB):
    bb = p.bigbuf
    tl = []
    for nm, off, rows, width in (("sw_t", F1_SW, 128, 1024), ("lu_t", F1_LU, 128, 1024),
                                 ("md_t", F1_MD, 128, 128), ("dw_t", F1_DW, 128, 128),
                                 ("ld_cat", F1_LD, 64, 2048), ("bw", F1_BW, 8, 2048)):
        src = p.inputs[nm]
        ap = src if src.ndim == 2 else src.rearrange("p a b -> p (a b)")
        tl.append(p.inc("dg", p.pe.dma_start(out=bb[0:rows, off:off + width], in_=ap), 16))
    ti1 = p.inc("eP", p.pe.iota(p.trash[0:8, 512:576], [[1, 8], [0, 8]],
                                base=0, channel_multiplier=-1,
                                allow_small_or_imprecise_dtypes=True))
    p.wv(ti1)
    te1 = p.inc("eV", p.ve.tensor_single_scalar(
        p.trash[0:8, 512:576], p.trash[0:8, 512:576], 0.0, ALU.is_equal))
    p.wt(*tl)
    mm = None
    for n in range(8):
        for kc in range(KC):
            mm = p.te.matmul(p.psA[0:1, 8 * n:8 * n + 8],
                             bb[:, F1_MD + kc * 8 + n:F1_MD + kc * 8 + n + 1],
                             bb[:, F1_SW + kc * 64 + 8 * n:F1_SW + kc * 64 + 8 * n + 8],
                             start=(kc == 0), stop=(kc == KC - 1))
    t_wmd = p.inc("eT", mm)
    for n in range(8):
        for kc in range(KC):
            mm = p.te.matmul(p.psB[0:1, n:n + 1],
                             bb[:, F1_MD + kc * 8 + n:F1_MD + kc * 8 + n + 1],
                             bb[:, F1_MD + kc * 8 + n:F1_MD + kc * 8 + n + 1],
                             start=(kc == 0), stop=(kc == KC - 1))
    t_mdsq = p.inc("eT", mm)
    for n in range(8):
        for kc in range(KC):
            mm = p.te.matmul(p.psC[0:1, 8 * n:8 * n + 8],
                             bb[:, F1_DW + kc * 8 + n:F1_DW + kc * 8 + n + 1],
                             bb[:, F1_LU + kc * 64 + 8 * n:F1_LU + kc * 64 + 8 * n + 8],
                             start=(kc == 0), stop=(kc == KC - 1))
    t_cc = p.inc("eT", mm)
    p.wv(t_wmd, t_mdsq, t_cc)
    tw1 = p.inc("eV", p.ve.tensor_copy(p.smallf[0:1, 0:64], p.psA[0:1, 0:64]))
    tw2 = p.inc("eV", p.ve.tensor_copy(p.smallf[0:1, 64:72], p.psB[0:1, 0:8]))
    tw3 = p.inc("eV", p.ve.tensor_copy(p.trash[0:1, 576:640], p.psC[0:1, 0:64]))
    trseq = TrSeq(p)
    tc4 = trseq.transpose(p.trash[0:1, 576:640], p.ccol[0:64, 0:1], extra_dep=(tw3,))
    p.wv(tw1, tw2)
    tb1 = p.inc("eV", p.ve.tensor_copy(p.trash[0:1, 640:704], p.smallf[0:1, 0:64]))
    tb2 = p.inc("eV", p.ve.tensor_copy(p.trash[0:1, 704:712], p.smallf[0:1, 64:72]))
    p.wt(tb1, tb2, tw1, tw2)
    tm1 = p.inc("eT", p.te.matmul(p.psD[:, 0:64], p.onesc[0:1, 0:128],
                                  p.trash[0:1, 640:704], start=True, stop=True))
    tm2 = p.inc("eT", p.te.matmul(p.psD[:, 64:72], p.onesc[0:1, 0:128],
                                  p.trash[0:1, 704:712], start=True, stop=True))
    p.wv(tm1, tm2)
    tb3 = p.inc("eV", p.ve.tensor_copy(p.l1tmp[:, 0:64], p.psD[:, 0:64]))
    tb4 = p.inc("eV", p.ve.tensor_copy(p.negb[:, 0:8], p.psD[:, 64:72]))
    p.wv(tb3, tb4)

    Q0, VSQ, SCR, GAT, CMX, OH, MCOL = 0, 64, 72, 80, 88, 96, 104
    psAB_free = p.tok("eV")
    psD_free = p.tok("eV")
    psC_free = p.tok("eV")
    ohT_t = None
    st_upd = None
    for ti in range(NT):
        xsl = slice(ti * 128, (ti + 1) * 128)
        p.wt(psAB_free, st_upd)
        mm = None
        for kc in range(KC):
            lhs = p.abuf[:, kc * 1024 + ti * 128:kc * 1024 + (ti + 1) * 128]
            mm = p.te.matmul(p.psA[:, 0:64], lhs,
                             bb[:, F1_SW + kc * 64:F1_SW + (kc + 1) * 64],
                             start=(kc == 0), stop=(kc == KC - 1))
        tA = p.inc("eT", mm)
        for kc in range(KC):
            lhs = p.abuf[:, kc * 1024 + ti * 128:kc * 1024 + (ti + 1) * 128]
            mm = p.te.matmul(p.psA[:, 64:72], lhs,
                             bb[:, F1_MD + kc * 8:F1_MD + (kc + 1) * 8],
                             start=(kc == 0), stop=(kc == KC - 1))
        tB = p.inc("eT", mm)
        for kc in range(KC):
            mm = p.te.matmul(p.psB[0:64, 0:128],
                             bb[:, F1_LU + kc * 64:F1_LU + (kc + 1) * 64],
                             p.abuf[:, kc * 1024 + ti * 128:kc * 1024 + (ti + 1) * 128],
                             start=(kc == 0), stop=(kc == KC - 1))
        tC = p.inc("eT", mm)
        p.wv(tA, tB)
        t1 = p.inc("eV", p.ve.tensor_tensor(
            p.scr32[:, MCOL + 1:MCOL + 2], p.acol[:, ti:ti + 1], p.acol[:, ti:ti + 1], ALU.mult))
        p.wv(t1)
        t2 = p.inc("eV", p.ve.tensor_scalar(
            p.scr32[:, MCOL:MCOL + 1], p.scr32[:, MCOL + 1:MCOL + 2],
            -float(D) * 1e-5, float(D), ALU.mult, ALU.add))
        tq = p.inc("eV", p.ve.tensor_tensor(
            p.scr32[:, Q0:Q0 + 64], p.psA[:, 0:64], p.l1tmp[:, 0:64], ALU.subtract))
        p.ws(tq)
        tsq = p.inc("eS", p.se.activation(p.trash[:, 0:64], p.scr32[:, Q0:Q0 + 64],
                                          AF.Square, bias=0.0, scale=1.0))
        p.wv(tsq, t2)
        tr_ = p.inc("eV", p.ve.tensor_reduce(
            p.scr32[:, SCR:SCR + 8],
            p.trash[:, 0:64].rearrange("p (n s) -> p n s", s=8), AX.X, ALU.add))
        tv1 = p.inc("eV", p.ve.tensor_scalar(
            p.scr32[:, VSQ:VSQ + 8], p.psA[:, 64:72], -2.0,
            p.scr32[:, MCOL:MCOL + 1], ALU.mult, ALU.add))
        p.wv(tv1)
        tv2 = p.inc("eV", p.ve.tensor_tensor(
            p.scr32[:, VSQ:VSQ + 8], p.scr32[:, VSQ:VSQ + 8], p.negb[:, 0:8], ALU.add))
        psAB_free = tv2
        p.wv(tv2)
        tv3 = p.inc("eV", p.ve.reciprocal(p.scr32[:, OH:OH + 8], p.scr32[:, VSQ:VSQ + 8]))
        p.wv(tv3, tr_)
        tsc = p.inc("eV", p.ve.tensor_tensor(
            p.scr32[:, SCR:SCR + 8], p.scr32[:, SCR:SCR + 8], p.scr32[:, OH:OH + 8], ALU.mult))
        p.wv(tsc)
        tg1 = p.inc("eV", p.ve.tensor_tensor(
            p.scr32[:, GAT:GAT + 8], p.scr32[:, SCR:SCR + 8], p.cenb[:, :], ALU.subtract))
        p.wv(tg1)
        tg2 = p.inc("eV", p.ve.tensor_tensor(
            p.scr32[:, GAT:GAT + 8], p.scr32[:, GAT:GAT + 8], p.slob[:, :], ALU.mult))
        p.ws(tg2)
        tg3 = p.inc("eS", p.se.activation(p.scr32[:, GAT:GAT + 8], p.scr32[:, GAT:GAT + 8],
                                          AF.Sigmoid, bias=0.0, scale=1.0))
        p.wv(tg3)
        tmx = p.inc("eV", p.ve.tensor_reduce(
            p.scr32[:, MCOL + 2:MCOL + 3], p.scr32[:, GAT:GAT + 8], AX.X, ALU.max))
        tcm = p.inc("eV", p.ve.memset(p.scr32[:, CMX:CMX + 1], -1.0))
        for n in range(1, 8):
            p.wv(tcm)
            tcm = p.inc("eV", p.ve.tensor_tensor(
                p.scr32[:, CMX + n:CMX + n + 1], p.scr32[:, CMX + n - 1:CMX + n],
                p.scr32[:, GAT + n - 1:GAT + n], ALU.max))
        p.wv(tcm, tmx)
        to1 = p.inc("eV", p.ve.tensor_scalar(
            p.scr32[:, OH:OH + 8], p.scr32[:, GAT:GAT + 8],
            p.scr32[:, MCOL + 2:MCOL + 3], None, ALU.is_ge))
        to2 = p.inc("eV", p.ve.tensor_scalar(
            p.scr32[:, SCR:SCR + 8], p.scr32[:, CMX:CMX + 8],
            p.scr32[:, MCOL + 2:MCOL + 3], None, ALU.is_lt))
        p.wv(to1, to2)
        to3 = p.inc("eV", p.ve.tensor_tensor(
            p.scr32[:, OH:OH + 8], p.scr32[:, OH:OH + 8], p.scr32[:, SCR:SCR + 8], ALU.mult))
        p.wv(to3)
        tob = p.inc("eV", p.ve.tensor_copy(p.trash[:, 64:72], p.scr32[:, OH:OH + 8]))
        tohT = trseq.transpose(p.trash[:, 64:72], p.knrow[0:8, ti * 128:(ti + 1) * 128],
                               extra_dep=(tob,), rows=8)
        p.wt(tohT, te1, psC_free)
        trep = p.inc("eT", p.te.matmul(
            p.psC[0:64, 0:128], p.trash[0:8, 512:576],
            p.knrow[0:8, ti * 128:(ti + 1) * 128], start=True, stop=True))
        p.wv(tC, tc4)
        tmod = p.inc("eV", p.ve.tensor_scalar(
            p.scr32[0:64, 128:256], p.psB[0:64, 0:128], p.ccol[0:64, 0:1], None, ALU.subtract))
        p.wv(trep, tmod)
        trc = p.inc("eV", p.ve.tensor_copy(p.trash[0:64, 128:256], p.psC[0:64, 0:128]))
        psC_free = trc
        p.wv(trc)
        tmm2 = p.inc("eV", p.ve.tensor_tensor(
            p.ptbuf[0:64, 0:128], p.scr32[0:64, 128:256], p.trash[0:64, 128:256], ALU.mult))
        p.wt(tmm2, psD_free, psAB_free)
        for ng in range(4):
            pst = p.psD if ng < 2 else p.psA
            psl = pst[:, (ng % 2) * 512:(ng % 2 + 1) * 512]
            p.te.matmul(psl, p.ptbuf[0:64, 0:128],
                        bb[0:64, F1_LD + ng * 512:F1_LD + (ng + 1) * 512],
                        start=True, stop=False)
            mm = p.te.matmul(psl, p.knrow[0:8, ti * 128:(ti + 1) * 128],
                             bb[0:8, F1_BW + ng * 512:F1_BW + (ng + 1) * 512],
                             start=False, stop=True)
        tdg = p.inc("eT", mm)
        tS1 = p.inc("eV", p.ve.tensor_scalar(
            p.scr32[:, MCOL + 3:MCOL + 4], p.scr32[:, MCOL + 2:MCOL + 3],
            -1.0, 1.0, ALU.mult, ALU.add))
        txq = p.inc("eV", p.ve.tensor_scalar(
            p.scr32[:, 512:2560], p.stream[:, ti * D:(ti + 1) * D],
            p.mcol[:, ti:ti + 1], p.acol[:, ti:ti + 1], ALU.subtract, ALU.mult))
        p.wv(tdg, tS1, txq, st_upd)
        t_u1 = p.inc("eV", p.ve.scalar_tensor_tensor(
            p.stream[:, ti * D:ti * D + 1024], p.psD[:, 0:1024],
            p.scr32[:, MCOL + 2:MCOL + 3], p.stream[:, ti * D:ti * D + 1024],
            ALU.mult, ALU.add))
        p.wv(t_u1)
        t_u2 = p.inc("eV", p.ve.scalar_tensor_tensor(
            p.stream[:, ti * D + 1024:(ti + 1) * D], p.psA[:, 0:1024],
            p.scr32[:, MCOL + 2:MCOL + 3], p.stream[:, ti * D + 1024:(ti + 1) * D],
            ALU.mult, ALU.add))
        p.wv(t_u2)
        t_u3 = p.inc("eV", p.ve.scalar_tensor_tensor(
            p.stream[:, ti * D:(ti + 1) * D], p.scr32[:, 512:2560],
            p.scr32[:, MCOL + 3:MCOL + 4], p.stream[:, ti * D:(ti + 1) * D],
            ALU.mult, ALU.add))
        st_upd = t_u3
        psD_free = t_u1
        psAB_free = t_u2
    return p.tok("eV")



INPUT_SPECS = {
    "xin": (TC, D), "cos0": (TC, 32), "sin0": (TC, 32),
    "cos1": (TC, 64), "sin1": (TC, 64),
    "ident": (128, 128), "masks": (128, 2048),
    "l0_wq": (8, 128, 16, 256), "l0_wk": (2, 128, 16, 256),
    "l0_wv": (2, 128, 16, 256), "l0_wo": (8, 128, 16, 256),
    "l0_bq": (1, 2048), "l0_bk": (1, 512), "l0_bv": (1, 512), "l0_bo": (1, 2048),
    "l0_gate": (44, 128, 16, 128), "l0_up": (44, 128, 16, 128),
    "l0_down": (44, 128, 2048),
    "l0_gs": (128, 44), "l0_us": (128, 44), "l0_ds": (128, 2048),
    "l1_wq": (8, 128, 16, 256), "l1_wk": (2, 128, 16, 256),
    "l1_wv": (2, 128, 16, 256), "l1_wo": (8, 128, 16, 256),
    "l1_bq": (1, 2048), "l1_bk": (1, 512), "l1_bv": (1, 512),
    "l1_lnw": (128, 2048),
    "sw_t": (128, 16, 64), "lu_t": (128, 16, 64), "ld_cat": (64, 2048),
    "md_t": (128, 16, 8), "dw_t": (128, 16, 8), "bw": (8, 2048),
    "cen_bc": (128, 8), "slo_bc": (128, 8),
}


def declare_all(p: PB):
    for k, shp in INPUT_SPECS.items():
        p.din(k, shp, F32)
    p.din("kvi", (2, 2, 128, 192), mybir.dt.int32)


def build_full():
    p = PB(0)
    declare_all(p)
    yout = p.dout("yout", [TC, D], F32)
    L_alloc = alloc_common(p)
    p.ddown = p.dint("ddown", [44, 128, 2048], BF16)
    p.cc_in = {0: p.dint("cc_in0", [1544, 1040], BF16),
               1: p.dint("cc_in1", [1544, 1040], BF16)}
    p.cc_out = {0: p.dint("cc_out0", [13120, 1040], BF16),
                1: p.dint("cc_out1", [13120, 1040], BF16)}
    emit_prologue(p)
    tld = emit_stream_load(p, p.inputs["xin"])
    p.wv(tld); p.ws(tld)
    emit_ln(p, "plain")
    emit_down_precast(p)
    p.join()
    trseq = emit_qkv(p, 0); p.join()
    emit_gather(p, 0, trseq); p.join()
    emit_attn_loop(p, 0); p.join()
    emit_outproj(p, 0); p.join()
    emit_ln(p, "plain"); p.join()
    emit_ffn0(p); p.join()
    p.inc("dg", p.pe.dma_start(out=p.bigbuf[:, 16384:18432], in_=p.inputs["l1_lnw"][:, :]), 16)
    p.join()
    emit_ln(p, "l1attn", lnw_view=p.bigbuf[:, 16384:18432]); p.join()
    trseq = emit_qkv(p, 1); p.join()
    emit_gather(p, 1, trseq); p.join()
    emit_attn_loop(p, 1); p.join()
    emit_outproj(p, 1); p.join()
    emit_ln(p, "plain"); p.join()
    emit_ffn1(p); p.join()
    tst = emit_stream_store(p, yout)
    p.wy(tst)
    return p


_PROG = None


def kernel(**inputs):
    global _PROG
    g = host_prep(inputs)
    x_full = np.asarray(inputs["L0_attn_x"])[0]
    maps = per_core_inputs(g, x_full)
    if _PROG is None:
        _PROG = build_full()
    p = _PROG
    fmaps = [{k: mm[k] for k in p.inputs} for mm in maps]
    res = run_bass_kernel_spmd(p.nc, fmaps, core_ids=list(range(NCORE)))
    y = np.zeros((T, D), np.float32)
    for c in range(NCORE):
        y[core_token_rows(c)] = res.results[c]["yout"]
    return y.reshape(1, T, D)


# version 05
# speedup vs baseline: 1.5523x; 1.0435x over previous
import numpy as np
import ml_dtypes
import concourse.bass as bass
import concourse.mybir as mybir
from concourse.bass_utils import run_bass_kernel_spmd
from contextlib import ExitStack

F32 = mybir.dt.float32
BF16 = mybir.dt.bfloat16
AF = mybir.ActivationFunctionType
ALU = mybir.AluOpType
AX = mybir.AxisListType

D = 2048
T = 8192
TC = 1024
NT = 8
KC = 16
INTER = 5632
IC = 44
NCORE = 8
H0, G0, HD0 = 32, 8, 64
H1, G1, HD1 = 16, 4, 128
BF = ml_dtypes.bfloat16


def core_tiles(c):
    return [c, 15 - c]


def core_token_rows(c):
    t0, t1 = core_tiles(c)
    return np.concatenate([np.arange(512 * t0, 512 * t0 + 512),
                           np.arange(512 * t1, 512 * t1 + 512)])


def rope_perm(width, hd):
    p = []
    for h in range(width // hd):
        base = h * hd
        p += [base + 2 * i for i in range(hd // 2)]
        p += [base + 2 * i + 1 for i in range(hd // 2)]
    return np.array(p)


def host_prep(inp):
    g = {}
    f32 = np.float32
    perm_q = rope_perm(2048, 64)
    perm_k = rope_perm(512, 64)
    Wq = np.ascontiguousarray(inp["L0_attn_Wq"][:, perm_q])
    Wk = np.ascontiguousarray(inp["L0_attn_Wk"][:, perm_k])
    def rhs_slabs(W, nslab, nw):
        return np.ascontiguousarray(
            W.reshape(16, 128, nslab, nw).transpose(2, 1, 0, 3))
    g["l0_wq"] = rhs_slabs(Wq, 8, 256)
    g["l0_wk"] = rhs_slabs(Wk, 2, 256)
    g["l0_wv"] = rhs_slabs(inp["L0_attn_Wv"], 2, 256)
    g["l0_wo"] = rhs_slabs(inp["L0_attn_Wo"], 8, 256)
    g["l0_bq"] = np.ascontiguousarray(inp["L0_attn_bq"][perm_q].reshape(1, 2048))
    g["l0_bk"] = np.ascontiguousarray(inp["L0_attn_bk"][perm_k].reshape(1, 512))
    g["l0_bv"] = np.ascontiguousarray(inp["L0_attn_bv"].reshape(1, 512))
    g["l0_bo"] = np.ascontiguousarray(inp["L0_attn_bo"].reshape(1, 2048))
    gw, uw, dw = inp["L0_ffn_gate_w"], inp["L0_ffn_up_w"], inp["L0_ffn_down_w"]
    g["l0_gate"] = np.ascontiguousarray(gw.reshape(44, 128, 16, 128).transpose(0, 3, 2, 1))
    g["l0_up"] = np.ascontiguousarray(uw.reshape(44, 128, 16, 128).transpose(0, 3, 2, 1))
    g["l0_down"] = np.ascontiguousarray(dw.T.reshape(44, 128, 2048))
    g["l0_gs"] = np.ascontiguousarray(inp["L0_ffn_gate_s"].reshape(44, 128).T)
    g["l0_us"] = np.ascontiguousarray(inp["L0_ffn_up_s"].reshape(44, 128).T)
    g["l0_ds"] = np.ascontiguousarray(np.broadcast_to(inp["L0_ffn_down_s"], (128, 2048)))
    g["l1_wq"] = rhs_slabs(np.ascontiguousarray(inp["L1_attn_q_project"].T), 8, 256)
    g["l1_wk"] = rhs_slabs(np.ascontiguousarray(inp["L1_attn_k_project"].T), 2, 256)
    g["l1_wv"] = rhs_slabs(np.ascontiguousarray(inp["L1_attn_v_project"].T), 2, 256)
    g["l1_wo"] = rhs_slabs(np.ascontiguousarray(inp["L1_attn_o_project"].T), 8, 256)
    g["l1_bq"] = np.ascontiguousarray(inp["L1_attn_q_bias"].reshape(1, 2048))
    g["l1_bk"] = np.ascontiguousarray(inp["L1_attn_k_bias"].reshape(1, 512))
    g["l1_bv"] = np.ascontiguousarray(inp["L1_attn_v_bias"].reshape(1, 512))
    g["l1_lnw"] = np.ascontiguousarray(np.broadcast_to(inp["L1_attn_ln_weight"], (128, 2048)))
    sw, lu, ld = inp["L1_ffn_select_weight"], inp["L1_ffn_lora_update"], inp["L1_ffn_lora_degen"]
    md = inp["L1_ffn_select_mean_diff"]
    def cat_t(a):
        return np.ascontiguousarray(
            a.transpose(1, 0, 2).reshape(16, 128, 64).transpose(1, 0, 2))
    g["sw_t"] = cat_t(sw)
    g["lu_t"] = cat_t(lu)
    g["ld_cat"] = np.ascontiguousarray(ld.transpose(0, 2, 1).reshape(64, 2048))
    def colT(a):
        return np.ascontiguousarray(a.T.reshape(16, 128, 8).transpose(1, 0, 2))
    g["md_t"] = colT(md)
    g["dw_t"] = colT(inp["L1_ffn_debias_w"])
    g["bw"] = np.ascontiguousarray(inp["L1_ffn_bias_w"])
    g["cen_bc"] = np.ascontiguousarray(np.broadcast_to(inp["L1_ffn_imp_center"], (128, 8)))
    g["slo_bc"] = np.ascontiguousarray(np.broadcast_to(inp["L1_ffn_imp_slope"], (128, 8)))
    g["ident"] = np.eye(128, dtype=f32)
    kp = np.arange(128)[:, None]
    qf = np.arange(512)[None, :]
    g["masks"] = np.ascontiguousarray(np.concatenate(
        [(qf >= kp + 128 * j).astype(f32) for j in range(4)], axis=1))
    pos = np.arange(T, dtype=np.float64)
    inv0 = 10000.0 ** (-np.arange(0, 64, 2).astype(np.float64) / 64.0)
    inv1 = 10000.0 ** (-np.arange(0, 128, 2).astype(np.float64) / 128.0)
    ang0 = (pos[:, None] * inv0[None, :]).astype(f32)
    ang1 = (pos[:, None] * inv1[None, :]).astype(f32)
    g["cos0_full"] = np.cos(ang0).astype(f32)
    g["sin0_full"] = np.sin(ang0).astype(f32)
    g["cos1_full"] = np.cos(ang1).astype(f32)
    g["sin1_full"] = np.sin(ang1).astype(f32)
    return g


def kvi_table(c):
    t = np.zeros((2, 2, 128, 192), np.int32)
    ZROW = 12352
    pp = np.arange(128)
    for l in range(2):
        G = 8 if l == 0 else 4
        HD = 64 if l == 0 else 128
        for qh in range(2):
            t5top = c if qh == 0 else 15 - c
            for it in range(16):
                t5 = t5top - it
                if t5 >= 0:
                    rank = t5 if t5 < 8 else 15 - t5
                    half = 0 if t5 < 8 else 1
                    kbase, kcol = rank * 1544, half * 512
                    vbase = rank * 1544 + 520 + 512 * half
                else:
                    kbase, kcol, vbase = ZROW, 0, ZROW
                for g in range(G):
                    if it * G + g < 16 * G:
                        rowk = kbase + (0 if t5 < 0 else g * HD) + (pp % HD)
                        t[l, qh, :, it * G + g] = (rowk * 1040 + kcol) // 8
                for j in range(4):
                    rowv = vbase + (0 if t5 < 0 else 128 * j) + pp
                    t[l, qh, :, 16 * G + it * 4 + j] = (rowv * 1040) // 8
    return t


def per_core_inputs(g, x_full):
    maps = []
    for c in range(NCORE):
        rows = core_token_rows(c)
        m = {k: v for k, v in g.items() if not k.endswith("_full")}
        m["xin"] = np.ascontiguousarray(x_full[rows])
        m["kvi"] = kvi_table(c)
        m["cos0"] = np.ascontiguousarray(g["cos0_full"][rows])
        m["sin0"] = np.ascontiguousarray(g["sin0_full"][rows])
        m["cos1"] = np.ascontiguousarray(g["cos1_full"][rows])
        m["sin1"] = np.ascontiguousarray(g["sin1_full"][rows])
        maps.append(m)
    return maps



class PB:

    def __init__(self, core_id):
        self.c = core_id
        self.nc = bass.Bass("TRN2", target_bir_lowering=False, debug=False)
        self.ctx = ExitStack()
        self.cnt = {}
        self.sems = {}
        self.hw = {}
        self.inputs = {}

    def din(self, name, shape, dtype=F32):
        ap = self.nc.dram_tensor(name, list(shape), dtype, kind="ExternalInput").ap()
        self.inputs[name] = ap
        return ap

    def dout(self, name, shape, dtype=F32):
        return self.nc.dram_tensor(name, list(shape), dtype, kind="ExternalOutput").ap()

    def dint(self, name, shape, dtype=BF16, **kw):
        return self.nc.dram_tensor(name, list(shape), dtype, kind="Internal", **kw).ap()

    def sb(self, name, shape, dtype):
        return self.ctx.enter_context(self.nc.sbuf_tensor(name, list(shape), dtype))

    def psum(self, name, shape, dtype=F32):
        return self.ctx.enter_context(self.nc.psum_tensor(name, list(shape), dtype))

    def sem(self, name):
        s = self.ctx.enter_context(self.nc.semaphore(name))
        self.sems[name] = s
        self.cnt[name] = 0
        return s

    def inc(self, name, inst, k=1):
        inst.then_inc(self.sems[name], k)
        self.cnt[name] += k
        return (name, self.cnt[name])

    def tok(self, name):
        return (name, self.cnt[name])

    def wait_eng(self, eng, engname, toks):
        for t in toks:
            if t is None:
                continue
            name, v = t
            if v <= 0:
                continue
            key = (engname, name)
            if self.hw.get(key, 0) < v:
                eng.wait_ge(self.sems[name], v)
                self.hw[key] = v

    @property
    def te(self): return self.nc.tensor
    @property
    def ve(self): return self.nc.vector
    @property
    def se(self): return self.nc.scalar
    @property
    def pe(self): return self.nc.gpsimd
    @property
    def sy(self): return self.nc.sync

    def wt(self, *toks): self.wait_eng(self.te, "T", toks)
    def wv(self, *toks): self.wait_eng(self.ve, "V", toks)
    def ws(self, *toks): self.wait_eng(self.se, "S", toks)
    def wp(self, *toks): self.wait_eng(self.pe, "P", toks)
    def wy(self, *toks): self.wait_eng(self.sy, "Y", toks)

    def all_toks(self):
        return [self.tok(s) for s in self.sems]

    def join(self):
        toks = self.all_toks()
        self.wt(*toks); self.wv(*toks); self.ws(*toks); self.wp(*toks); self.wy(*toks)
        return toks


class Ring:

    def __init__(self, views):
        self.views = views
        self.ready = [None] * len(views)
        self.free = [None] * len(views)
        self.i = 0

    def next(self):
        j = self.i % len(self.views)
        self.i += 1
        return j, self.views[j]


def alloc_common(p: PB):
    for s in ("dg", "ds", "da", "cc", "eT", "eV", "eS", "eP"):
        p.sem(s)
    p.psA = p.psum("psA", [128, 1024])
    p.psB = p.psum("psB", [128, 1024])
    p.psC = p.psum("psC", [128, 1024])
    p.psD = p.psum("psD", [128, 1024])
    p.stream = p.sb("stream", [128, NT * D], F32)
    p.abuf = p.sb("abuf", [128, 16384], BF16)
    p.bigbuf = p.sb("bigbuf", [128, 24704], BF16)
    p.scr32 = p.sb("scr32", [128, 2560], F32)
    p.wbuf = p.sb("wbuf", [128, 8192], BF16)
    p.ptbuf = p.sb("ptbuf", [128, 4096], BF16)
    p.kvbuf = p.sb("kvbuf", [128, 2112], BF16)
    p.trash = p.sb("trash", [128, 1024], BF16)
    p.identb = p.sb("identb", [128, 128], BF16)
    p.maskb = p.sb("maskb", [128, 2048], BF16)
    p.cos0b = p.sb("cos0b", [128, 256], F32)
    p.sin0b = p.sb("sin0b", [128, 256], F32)
    p.cos1b = p.sb("cos1b", [128, 512], F32)
    p.sin1b = p.sb("sin1b", [128, 512], F32)
    p.mcol = p.sb("mcol", [128, 8], F32)
    p.acol = p.sb("acol", [128, 8], F32)
    p.stat = p.sb("stat", [128, 8], F32)
    p.statS = p.sb("statS", [128, 4], F32)
    p.birow = p.sb("birow", [1, 3072], BF16)
    p.onesc = p.sb("onesc", [1, 128], BF16)
    p.onesf = p.sb("onesf", [1, 128], F32)
    p.onek = p.sb("onek", [128, 2], BF16)
    p.qn = p.sb("qn", [128, 32], F32)
    p.negb = p.sb("negb", [128, 32], F32)
    p.knrow = p.sb("knrow", [8, 1024], BF16)
    p.smallf = p.sb("smallf", [1, 128], F32)
    p.linv = p.sb("linv", [1, 512], F32)
    p.lbc = p.scr32[:, 0:1024]
    p.gsb = p.sb("gsb", [128, 44], F32)
    p.usb = p.sb("usb", [128, 44], F32)
    p.cenb = p.sb("cenb", [128, 8], F32)
    p.slob = p.sb("slob", [128, 8], F32)
    p.l1tmp = p.sb("l1tmp", [128, 64], F32)
    p.ccol = p.sb("ccol", [64, 1], F32)
    p.kvib = p.sb("kvib", [128, 192], mybir.dt.int32)
    p.trviews = [p.psD[:, k * 64:(k + 1) * 64].bitcast(BF16) for k in range(16)]


TR_ORDER = [0, 8, 1, 9, 2, 10, 3, 11, 4, 12, 5, 13, 6, 14, 7, 15]


class TrSeq:

    def __init__(self, p):
        self.p = p
        self.k = 0
        self.view_free = [None] * 16
        self.hist = []

    def transpose(self, src, dst, extra_dep=(), rows=128):
        p = self.p
        P = src.shape[0]
        F = src.shape[-1]
        v = TR_ORDER[self.k % 16]
        trv = p.trviews[v][0:F, 0:P]
        if len(self.hist) >= 2:
            p.wt(self.hist[-2])
        p.wt(self.view_free[v], *extra_dep)
        tt = p.inc("eT", p.te.transpose(trv, src, p.identb[0:P, 0:P]))
        p.ws(tt)
        tev = p.inc("eS", p.se.copy(dst, trv))
        self.view_free[v] = tev
        self.hist.append(tev)
        self.k += 1
        return tev


def emit_prologue(p: PB):
    g = p.pe
    p.inc("dg", g.dma_start(out=p.identb[:, :], in_=p.inputs["ident"][:, :]), 16)
    p.inc("dg", g.dma_start(out=p.maskb[:, :], in_=p.inputs["masks"][:, :]), 16)
    for nm, dst, w in (("cos0", p.cos0b, 32), ("sin0", p.sin0b, 32),
                       ("cos1", p.cos1b, 64), ("sin1", p.sin1b, 64)):
        p.inc("ds", p.sy.dma_start(
            out=dst[:, :].rearrange("p (i w) -> p i w", w=w),
            in_=p.inputs[nm].rearrange("(i p) w -> p i w", p=128)), 16)
    p.inc("dg", g.dma_start(out=p.gsb[:, :], in_=p.inputs["l0_gs"][:, :]), 16)
    p.inc("dg", g.dma_start(out=p.usb[:, :], in_=p.inputs["l0_us"][:, :]), 16)
    p.inc("dg", g.dma_start(out=p.cenb[:, :], in_=p.inputs["cen_bc"][:, :]), 16)
    p.inc("dg", g.dma_start(out=p.slob[:, :], in_=p.inputs["slo_bc"][:, :]), 16)
    p.inc("eV", p.ve.memset(p.onesc[:, :], 1.0))
    p.inc("eV", p.ve.memset(p.onesf[:, :], 1.0))
    p.inc("eV", p.ve.memset(p.onek[:, :], 1.0))


def emit_stream_load(p: PB, src_ap):
    return p.inc("ds", p.sy.dma_start(
        out=p.stream[:, :].rearrange("p (i d) -> p i d", d=D),
        in_=src_ap.rearrange("(i p) d -> p i d", p=128)), 16)


def emit_stream_store(p: PB, dst_ap):
    return p.inc("ds", p.sy.dma_start(
        out=dst_ap.rearrange("(i p) d -> p i d", p=128),
        in_=p.stream[:, :].rearrange("p (i d) -> p i d", d=D)), 16)


def emit_ln(p: PB, mode, lnw_view=None):
    stg_ring = Ring([p.ptbuf[:, 0:2048], p.ptbuf[:, 2048:4096]])
    trseq = TrSeq(p)
    prev_act_read = None
    for i in range(NT):
        st = p.stream[:, i * D:(i + 1) * D]
        tv = p.inc("eV", p.ve.tensor_reduce(p.stat[:, 0:1], st, AX.X, ALU.add))
        p.ws(prev_act_read)
        p.se.activation(
            p.trash[:, 0:1024], st[:, 0:1024], AF.Square, bias=0.0, scale=1.0,
            accum_out=p.statS[:, 0:1])
        ts_ = p.inc("eS", p.se.activation(
            p.trash[:, 0:1024], st[:, 1024:2048], AF.Square, bias=0.0, scale=1.0,
            accum_out=p.statS[:, 1:2]))
        p.wv(tv, ts_)
        t1 = p.inc("eV", p.ve.tensor_scalar_mul(p.mcol[:, i:i + 1], p.stat[:, 0:1], 1.0 / D))
        t2 = p.inc("eV", p.ve.tensor_tensor(p.stat[:, 2:3], p.statS[:, 0:1], p.statS[:, 1:2], ALU.add))
        p.wv(t2)
        t3 = p.inc("eV", p.ve.tensor_scalar_mul(p.stat[:, 2:3], p.stat[:, 2:3], 1.0 / D))
        prev_act_read = p.tok("eV")
        p.wv(t1)
        t4 = p.inc("eV", p.ve.tensor_tensor(
            p.stat[:, 3:4], p.mcol[:, i:i + 1], p.mcol[:, i:i + 1], ALU.mult))
        p.wv(t3, t4)
        t5 = p.inc("eV", p.ve.tensor_tensor(
            p.stat[:, 4:5], p.stat[:, 2:3], p.stat[:, 3:4], ALU.subtract))
        p.wv(t5)
        if mode == "l1attn":
            tden = p.inc("eV", p.ve.tensor_scalar(
                p.stat[:, 5:6], p.stat[:, 4:5], 1.0 + 1e-6, 1e-11, ALU.mult, ALU.add))
        else:
            tden = p.inc("eV", p.ve.tensor_scalar(
                p.stat[:, 5:6], p.stat[:, 4:5], 1.0, 1e-5, ALU.mult, ALU.add))
        p.ws(tden)
        tsq = p.inc("eS", p.se.sqrt(p.statS[:, 2:3], p.stat[:, 5:6]))
        p.wv(tsq)
        trec = p.inc("eV", p.ve.reciprocal(p.acol[:, i:i + 1], p.statS[:, 2:3]))
        p.wv(trec)
        sj, stg = stg_ring.next()
        p.wv(stg_ring.free[sj])
        if mode == "l1attn":
            p.inc("eV", p.ve.tensor_scalar(
                p.scr32[:, 0:2048], st, p.mcol[:, i:i + 1], p.acol[:, i:i + 1],
                ALU.subtract, ALU.mult))
            txn = p.inc("eV", p.ve.tensor_tensor(
                stg, p.scr32[:, 0:2048], lnw_view, ALU.mult))
        else:
            txn = p.inc("eV", p.ve.tensor_scalar(
                stg, st, p.mcol[:, i:i + 1], p.acol[:, i:i + 1],
                ALU.subtract, ALU.mult))
        stg_ring.ready[sj] = txn
        for kc in range(KC):
            trseq.transpose(
                stg[:, kc * 128:(kc + 1) * 128],
                p.abuf[:, kc * 1024 + i * 128: kc * 1024 + (i + 1) * 128],
                extra_dep=(txn,))
        stg_ring.free[sj] = p.tok("eS")
    return p.tok("eS")


def emit_ffn0(p: PB, first_time=True):
    HT = p.bigbuf
    gate, up = p.inputs["l0_gate"], p.inputs["l0_up"]
    dd = p.ddown

    evac_done = None
    stream_evac = [None] * NT
    slot_last = [None, None]
    for half in range(2):
        gu_free = list(slot_last)
        psum_free = None
        for j in range(22):
            m = half * 22 + j
            k = j % 2
            gslot = p.wbuf[:, k * 4096:k * 4096 + 2048]
            uslot = p.wbuf[:, k * 4096 + 2048:k * 4096 + 4096]
            p.wp(gu_free[k])
            tg = p.inc("dg", p.pe.dma_start(
                out=gslot.rearrange("p (kc m) -> p kc m", m=128), in_=gate[m]), 16)
            tu = p.inc("dg", p.pe.dma_start(
                out=uslot.rearrange("p (kc m) -> p kc m", m=128), in_=up[m]), 16)
            p.wt(tg, tu, psum_free, evac_done)
            mm = None
            for ng in range(2):
                for kc in range(KC):
                    mm = p.te.matmul(
                        p.psA[:, ng * 512:(ng + 1) * 512],
                        gslot[:, kc * 128:(kc + 1) * 128],
                        p.abuf[:, kc * 1024 + ng * 512: kc * 1024 + (ng + 1) * 512],
                        start=(kc == 0), stop=(kc == KC - 1))
            tG = p.inc("eT", mm)
            for ng in range(2):
                for kc in range(KC):
                    mm = p.te.matmul(
                        p.psB[:, ng * 512:(ng + 1) * 512],
                        uslot[:, kc * 128:(kc + 1) * 128],
                        p.abuf[:, kc * 1024 + ng * 512: kc * 1024 + (ng + 1) * 512],
                        start=(kc == 0), stop=(kc == KC - 1))
            tU = p.inc("eT", mm)
            gu_free[k] = tU
            gstage = p.ptbuf[:, 0:1024]
            p.ws(tG, psum_free)
            tsg = p.inc("eS", p.se.activation(
                gstage, p.psA[:, 0:1024], AF.Silu, bias=0.0,
                scale=p.gsb[:, m:m + 1]))
            p.wv(tU, tsg)
            th = p.inc("eV", p.ve.scalar_tensor_tensor(
                HT[:, j * 1024:(j + 1) * 1024], p.psB[:, 0:1024],
                p.usb[:, m:m + 1], gstage, ALU.mult, ALU.mult))
            psum_free = th
        h_done = p.tok("eV")
        dn_free = [p.tok("eT"), p.tok("eT")]
        tmm = None
        for tp in range(NT // 2):
            tts = (2 * tp, 2 * tp + 1)
            for j in range(22):
                k = j % 2
                dslot = p.wbuf[:, k * 4096:k * 4096 + 2048]
                p.wy(dn_free[k])
                tload = p.inc("ds", p.sy.dma_start(out=dslot, in_=dd[half * 22 + j]), 16)
                p.wt(tload, h_done)
                if j == 0:
                    p.wt(evac_done)
                mm = None
                for ti_, tt in enumerate(tts):
                    for ng in range(4):
                        pst = (p.psA, p.psB, p.psC, p.psD)[ti_ * 2 + ng // 2]
                        mm = p.te.matmul(
                            pst[:, (ng % 2) * 512:(ng % 2 + 1) * 512],
                            HT[:, j * 1024 + tt * 128: j * 1024 + (tt + 1) * 128],
                            dslot[:, ng * 512:(ng + 1) * 512],
                            start=(j == 0), stop=(j == 21))
                tmm = p.inc("eT", mm)
                dn_free[k] = tmm
                slot_last[k] = tmm
            p.wv(tmm)
            if half == 1:
                p.wv(stream_evac[tts[0]], stream_evac[tts[1]])
            for ti_, tt in enumerate(tts):
                for ngg in range(2):
                    pst = (p.psA, p.psB, p.psC, p.psD)[ti_ * 2 + ngg]
                    ssl = slice(tt * D + ngg * 1024, tt * D + (ngg + 1) * 1024)
                    p.inc("eV", p.ve.tensor_tensor(
                        p.stream[:, ssl], p.stream[:, ssl], pst[:, 0:1024], ALU.add))
                stream_evac[tt] = p.tok("eV")
            evac_done = p.tok("eV")
        p.wt(p.tok("eT"))
    return p.tok("eV")


def emit_down_precast(p: PB):
    down = p.inputs["l0_down"]
    dd = p.ddown
    tds = p.inc("ds", p.sy.dma_start(out=p.scr32[:, 0:2048], in_=p.inputs["l0_ds"][:, :]), 16)
    slot_free = [None, None]
    for m in range(IC):
        k = m % 2
        slot = p.wbuf[:, k * 4096:k * 4096 + 2048]
        slot2 = p.wbuf[:, k * 4096 + 2048:k * 4096 + 4096]
        p.wp(slot_free[k])
        tl = p.inc("dg", p.pe.dma_start(out=slot, in_=down[m]), 16)
        p.wv(tl, tds)
        tsc = p.inc("eV", p.ve.tensor_tensor(slot2, slot, p.scr32[:, 0:2048], ALU.mult))
        p.wy(tsc)
        slot_free[k] = p.inc("ds", p.sy.dma_start(out=dd[m], in_=slot2), 16)
    return p.tok("ds")



import math

QT_OFF = 0
KT_OFF = 16384
VR_OFF = 20480


def attn_params(l):
    if l == 0:
        return dict(G=G0, HD=HD0, H=H0, scale=1.0 / 8.0,
                    Wv=8 * 65, vstride=65, ones_col=True,
                    wq="l0_wq", wk="l0_wk", wv="l0_wv", wo="l0_wo",
                    bq="l0_bq", bk="l0_bk", bv="l0_bv", bo="l0_bo")
    return dict(G=G1, HD=HD1, H=H1, scale=1.0 / math.sqrt(128.0),
                Wv=4 * 128 + 8, vstride=128, ones_col=False,
                wq="l1_wq", wk="l1_wk", wv="l1_wv", wo="l1_wo",
                bq="l1_bq", bk="l1_bk", bv="l1_bv", bo=None)


def emit_qkv(p: PB, l):
    pr = attn_params(l)
    G, HD, H = pr["G"], pr["HD"], pr["H"]
    hw = HD // 2
    cosb = p.cos0b if l == 0 else p.cos1b
    sinb = p.sin0b if l == 0 else p.sin1b
    tb1 = p.inc("dg", p.pe.dma_start(out=p.birow[0:1, 0:2048], in_=p.inputs[pr["bq"]][:, :]), 16)
    tb2 = p.inc("dg", p.pe.dma_start(out=p.birow[0:1, 2048:2560], in_=p.inputs[pr["bk"]][:, :]), 16)
    tb3 = p.inc("dg", p.pe.dma_start(out=p.birow[0:1, 2560:3072], in_=p.inputs[pr["bv"]][:, :]), 16)
    tones = None
    if pr["ones_col"]:
        tones = p.inc("eV", p.ve.memset(
            p.bigbuf[:, VR_OFF:VR_OFF + 8 * pr["Wv"]]
            .rearrange("p (t g dd) -> p t g dd", g=G, dd=65)[:, :, :, 64:65], 1.0))
    else:
        tones = p.inc("eV", p.ve.memset(
            p.bigbuf[:, VR_OFF:VR_OFF + 8 * pr["Wv"]]
            .rearrange("p (t w) -> p t w", w=pr["Wv"])[:, :, 512:520], 1.0))
    tqn = p.inc("eV", p.ve.memset(p.qn[:, 0:H], 0.0))
    p.wv(tqn)

    trseq = TrSeq(p)
    pslots = [p.psA[:, 0:256], p.psB[:, 0:256], p.psC[:, 0:256]]
    slot_read = [None, None, None]
    sidx = 0
    stg_slots = [p.ptbuf[:, 2048 + 256 * k:2048 + 256 * (k + 1)] for k in range(2)]
    stg_free = [None, None]
    wslots = [p.wbuf[:, 0:4096], p.wbuf[:, 4096:8192]]
    w_free = [None, None]
    p_scnt = 0
    projs = [("q", pr["wq"], 8, 0), ("k", pr["wk"], 2, 2048), ("v", pr["wv"], 2, 2560)]
    for proj, wname, nslab, boff in projs:
        W = p.inputs[wname]
        for s in range(nslab):
            wk_ = p_scnt % 2
            p_scnt += 1
            wslot = wslots[wk_]
            p.wp(w_free[wk_])
            twl = p.inc("dg", p.pe.dma_start(
                out=wslot.rearrange("p (kc n) -> p kc n", n=256), in_=W[s]), 16)
            for ti in range(NT):
                j = sidx % 3
                ps = pslots[j]
                sidx += 1
                p.wt(slot_read[j], twl, tb1, tb2, tb3)
                for kc in range(KC):
                    p.te.matmul(ps, p.abuf[:, kc * 1024 + ti * 128:kc * 1024 + (ti + 1) * 128],
                                wslot[:, kc * 256:(kc + 1) * 256],
                                start=(kc == 0), stop=False)
                tmm = p.inc("eT", p.te.matmul(
                    ps, p.onesc[0:1, 0:128], p.birow[0:1, boff + s * 256:boff + s * 256 + 256],
                    start=False, stop=True))
                w_free[wk_] = tmm
                if proj == "v":
                    p.wv(tmm, tones)
                    if pr["ones_col"]:
                        dst = (p.bigbuf[:, VR_OFF + ti * pr["Wv"]:VR_OFF + (ti + 1) * pr["Wv"]]
                               .rearrange("p (g dd) -> p g dd", dd=65)[:, 4 * s:4 * s + 4, 0:64])
                        srcv = ps.rearrange("p (g dd) -> p g dd", dd=64)
                    else:
                        dst = p.bigbuf[:, VR_OFF + ti * pr["Wv"] + s * 256:VR_OFF + ti * pr["Wv"] + s * 256 + 256]
                        srcv = ps
                    tev = p.inc("eV", p.ve.tensor_copy(dst, srcv))
                    slot_read[j] = tev
                else:
                    nh = 256 // HD
                    C3 = cosb[:, ti * hw:(ti + 1) * hw].unsqueeze(1).broadcast_to([128, nh, hw])
                    S3 = sinb[:, ti * hw:(ti + 1) * hw].unsqueeze(1).broadcast_to([128, nh, hw])
                    ps3 = ps.rearrange("p (h w) -> p h w", w=HD)
                    psev, psod = ps3[:, :, 0:hw], ps3[:, :, hw:HD]
                    t1 = p.scr32[:, 2048:2176].rearrange("p (h w) -> p h w", w=hw)[:, 0:nh]
                    t2 = p.scr32[:, 2176:2304].rearrange("p (h w) -> p h w", w=hw)[:, 0:nh]
                    sk = sidx % 2
                    stg = stg_slots[sk]
                    stg3 = stg.rearrange("p (h w) -> p h w", w=HD)
                    p.wv(tmm, stg_free[sk])
                    p.inc("eV", p.ve.tensor_tensor(t1, psev, C3, ALU.mult))
                    i2 = p.inc("eV", p.ve.tensor_tensor(t2, psod, S3, ALU.mult))
                    p.wv(i2)
                    i3 = p.inc("eV", p.ve.tensor_tensor(stg3[:, :, 0:hw], t1, t2, ALU.subtract))
                    p.inc("eV", p.ve.tensor_tensor(t1, psod, C3, ALU.mult))
                    i5 = p.inc("eV", p.ve.tensor_tensor(t2, psev, S3, ALU.mult))
                    p.wv(i5)
                    i6 = p.inc("eV", p.ve.tensor_tensor(stg3[:, :, hw:HD], t1, t2, ALU.add))
                    slot_read[j] = i5
                    p.ws(i3, i6)
                    sq = p.inc("eS", p.se.activation(p.trash[:, 0:256], stg, AF.Square,
                                                     bias=0.0, scale=1.0))
                    p.wv(sq)
                    if proj == "q":
                        tr_ = p.inc("eV", p.ve.tensor_reduce(
                            p.stat[:, 0:nh], p.trash[:, 0:256].rearrange("p (h w) -> p h w", w=HD),
                            AX.X, ALU.add))
                        p.wv(tr_)
                        hb = s * nh
                        tmx = p.inc("eV", p.ve.tensor_tensor(
                            p.qn[:, hb:hb + nh], p.qn[:, hb:hb + nh], p.stat[:, 0:nh], ALU.max))
                        p.wv(tmx)
                    else:
                        gb = s * nh
                        tr_ = p.inc("eV", p.ve.tensor_reduce(
                            p.l1tmp[:, ti * 8 + gb:ti * 8 + gb + nh],
                            p.trash[:, 0:256].rearrange("p (h w) -> p h w", w=HD),
                            AX.X, ALU.add))
                        p.wv(tr_)
                    for jj in range(2):
                        kc_out = s * 2 + jj
                        base = QT_OFF if proj == "q" else KT_OFF
                        trseq.transpose(
                            stg[:, jj * 128:(jj + 1) * 128],
                            p.bigbuf[:, base + kc_out * 1024 + ti * 128:base + kc_out * 1024 + (ti + 1) * 128],
                            extra_dep=(i3, i6))
                    stg_free[sk] = p.tok("eS")
    return trseq


def emit_gather(p: PB, l, trseq):
    pr = attn_params(l)
    G, HD, H = pr["G"], pr["HD"], pr["H"]
    rep = H // G
    cc_in = p.cc_in[l]
    cc_out = p.cc_out[l]
    Wv0 = pr["Wv"]
    tz = p.inc("eV", p.ve.memset(p.trash[:, 0:1024], 0.0))
    p.wy(tz)
    for r0 in range(520, 1544, 128):
        r1 = min(r0 + 128, 1544)
        p.inc("ds", p.sy.dma_start(out=cc_in[r0:r1, Wv0:1040], in_=p.trash[0:r1 - r0, 0:1040 - Wv0]), 16)
    for r0 in range(0, 520, 128):
        r1 = min(r0 + 128, 520)
        p.inc("ds", p.sy.dma_start(out=cc_in[r0:r1, 1024:1040], in_=p.trash[0:r1 - r0, 0:16]), 16)
    for r0 in range(12352, 13120, 128):
        p.inc("ds", p.sy.dma_start(out=cc_out[r0:r0 + 128, 0:1024], in_=p.trash[:, 0:1024]), 16)
        p.inc("ds", p.sy.dma_start(out=cc_out[r0:r0 + 128, 1024:1040], in_=p.trash[:, 0:16]), 16)
    p.wv(p.tok("ds"))
    for ti in range(NT):
        tc_ = p.inc("eV", p.ve.tensor_copy(p.trash[:, 256:256 + 8], p.l1tmp[:, ti * 8:ti * 8 + 8]))
        p.wv(tc_)
        trseq.transpose(p.trash[:, 256:264], p.knrow[0:8, ti * 128:(ti + 1) * 128],
                        extra_dep=(tc_,), rows=8)
    knt = p.tok("eS")
    for kc in range(4):
        p.wy(p.tok("eS"))
        p.inc("ds", p.sy.dma_start(
            out=cc_in[kc * 128:(kc + 1) * 128, 0:1024],
            in_=p.bigbuf[:, KT_OFF + kc * 1024:KT_OFF + (kc + 1) * 1024]), 16)
    p.wy(knt)
    p.inc("ds", p.sy.dma_start(out=cc_in[512:520, 0:1024], in_=p.knrow[0:8, :]), 16)
    Wv = pr["Wv"]
    for ti in range(NT):
        p.inc("ds", p.sy.dma_start(
            out=cc_in[520 + 128 * ti:520 + 128 * (ti + 1), 0:Wv],
            in_=p.bigbuf[:, VR_OFF + ti * Wv:VR_OFF + (ti + 1) * Wv]), 16)
    p.wp(p.tok("ds"), p.tok("eV"), p.tok("eS"))
    tcc = p.inc("cc", p.pe.collective_compute(
        "AllGather", ALU.bypass, replica_groups=[list(range(8))],
        ins=[cc_in[:, :].opt()], outs=[cc_out[0:12352, :].opt()]))
    p.cc_done = tcc

    tq0 = p.inc("eV", p.ve.tensor_copy(p.trash[:, 256:256 + H], p.qn[:, 0:H]))
    p.wv(tq0)
    tq1 = trseq.transpose(p.trash[:, 256:256 + H], p.ptbuf[0:H, 0:128], rows=H, extra_dep=(tq0,))
    p.wv(tq1)
    tq2 = p.inc("eV", p.ve.tensor_reduce(p.stat[0:H, 0:1], p.ptbuf[0:H, 0:128], AX.X, ALU.max))
    p.wv(tq2)
    tq3 = p.inc("eV", p.ve.tensor_copy(p.trash[0:H, 256:257], p.stat[0:H, 0:1]))
    p.wv(tq3)
    tq4 = trseq.transpose(p.trash[0:H, 256:257], p.smallf[0:1, 0:H], rows=1, extra_dep=(tq3,))
    p.wp(tcc)
    p.wy(tcc)
    tkm = None
    for r in range(NCORE):
        p.wy(tkm)
        tld = p.inc("ds", p.sy.dma_start(
            out=p.knrow[0:8, :], in_=cc_out[r * 1544 + 512:r * 1544 + 520, 0:1024]), 16)
        p.wv(tld)
        tr_ = p.inc("eV", p.ve.tensor_reduce(p.stat[0:8, 1:2], p.knrow[0:8, :], AX.X, ALU.max))
        p.wv(tr_)
        if r == 0:
            tkm = p.inc("eV", p.ve.tensor_copy(p.stat[0:8, 2:3], p.stat[0:8, 1:2]))
        else:
            tkm = p.inc("eV", p.ve.tensor_tensor(
                p.stat[0:8, 2:3], p.stat[0:8, 2:3], p.stat[0:8, 1:2], ALU.max))
        p.wv(tkm)
    tk3 = p.inc("eV", p.ve.tensor_copy(p.trash[0:8, 256:257], p.stat[0:8, 2:3]))
    p.wv(tk3)
    tk4 = trseq.transpose(p.trash[0:8, 256:257], p.smallf[0:1, 32:40], rows=1, extra_dep=(tk3,))
    p.wv(tq4, tk4)
    kr = p.smallf[0:1, 32:32 + G].unsqueeze(2).broadcast_to([1, G, rep])
    tn1 = p.inc("eV", p.ve.tensor_tensor(
        p.smallf[0:1, 64:64 + H].rearrange("o (g r) -> o g r", r=rep),
        p.smallf[0:1, 0:H].rearrange("o (g r) -> o g r", r=rep), kr, ALU.add))
    p.wv(tn1)
    tn2 = p.inc("eV", p.ve.tensor_scalar_mul(
        p.smallf[0:1, 96:96 + H], p.smallf[0:1, 64:64 + H], -1.02 * pr["scale"] * 0.5))
    p.wv(tn2)
    tn3 = p.inc("eV", p.ve.tensor_copy(p.trash[0:1, 256:256 + H], p.smallf[0:1, 96:96 + H]))
    p.wt(tn3)
    tn4 = p.inc("eT", p.te.matmul(p.psA[:, 0:H], p.onesc[0:1, 0:128],
                                  p.trash[0:1, 256:256 + H], start=True, stop=True))
    p.wv(tn4)
    tn5 = p.inc("eV", p.ve.tensor_copy(p.negb[:, 0:H], p.psA[:, 0:H]))
    p.wv(tn5)
    return tn5


def emit_attn_loop(p: PB, l):
    pr = attn_params(l)
    G, HD, H = pr["G"], pr["HD"], pr["H"]
    rep = H // G
    cc_out = p.cc_out[l]
    Wv = pr["Wv"]
    kslots = [p.kvbuf[:, 0:512], p.kvbuf[:, 512:1024]]
    vslot_w = 544
    vslots = [p.kvbuf[:, 1024 + k * vslot_w:1024 + k * vslot_w + Wv] for k in range(2)]
    ptslots = [p.ptbuf[:, 512 * k:512 * (k + 1)] for k in range(4)]
    k_free = [None, None]
    v_free = [None, None]
    pt_free = [None, None, None, None]
    sA_read, sB_read = None, None
    o_read = [None, None]
    nbatch = 1 if l == 0 else 2
    hpb = 4 if l == 0 else 2
    kcnt, vcnt, ptcnt = 0, 0, 0
    ev_scr_read = None
    if not hasattr(p, "cc_flat"):
        p.cc_flat = {}
    p.cc_flat[l] = cc_out.rearrange("r (c e) -> (r c) e", e=8)
    pend = [[]]

    def _emit_av(d):
        p.wt(d["texp"], d["tmask"], d["tvl"])
        if d["start"]:
            p.wt(o_read[0], o_read[1])
        tav = p.inc("eT", p.te.matmul(
            d["ovs"], d["vls"], d["pt"], start=d["start"], stop=d["stop"]))
        pt_free[d["ptk"]] = tav
        if l == 1:
            tav = p.inc("eT", p.te.matmul(
                d["lvs"], d["ind"], d["pt"], start=d["start"], stop=d["stop"]))
            pt_free[d["ptk"]] = tav
        k_free[d["kk"]] = p.tok("eT")
        v_free[d["vv"]] = p.tok("eT")

    for qhalf in range(2):
        NIT = 8 if qhalf == 0 else 16
        qsl = slice(qhalf * 512, qhalf * 512 + 512)
        p.wy(p.tok("dg"))
        p.kvib_ready = p.inc("ds", p.sy.dma_start(
            out=p.kvib[:, :], in_=p.inputs["kvi"][l, qhalf]), 16)
        p.wp(p.kvib_ready)
        for g in range(G):
            for hb in range(nbatch):
                heads = [g * rep + hb * hpb + i for i in range(hpb)]
                if l == 0:
                    ovs = [p.psC[0:65, 0:512], p.psC[0:65, 512:1024],
                           p.psD[0:65, 0:512], p.psD[0:65, 512:1024]]
                    lvs = [None] * 4
                else:
                    ovs = [p.psC[:, 0:512], p.psC[:, 512:1024]]
                    lvs = [p.psD[0:1, 0:512], p.psD[0:1, 512:1024]]
                for it in range(NIT):
                    kk = kcnt % 2
                    kcnt += 1
                    p.wp(k_free[kk], p.kvib_ready)
                    tkl = p.inc("dg", p.pe.indirect_dma_start(
                        kslots[kk][:, :], None, p.cc_flat[l],
                        bass.IndirectOffsetOnAxis(p.kvib[:, it * G + g:it * G + g + 1], 0)), 16)
                    for j in range(4):
                        vv = vcnt % 2
                        vcnt += 1
                        p.wp(v_free[vv])
                        tvl = p.inc("dg", p.pe.indirect_dma_start(
                            vslots[vv], None, p.cc_flat[l],
                            bass.IndirectOffsetOnAxis(
                                p.kvib[:, 16 * G + it * 4 + j:16 * G + it * 4 + j + 1], 0)), 16)
                        for hp in range(0, hpb, 2):
                            descs = []
                            for dh in range(2):
                                hi = hp + dh
                                h = heads[hi]
                                use_a = (ptcnt % 2 == 0)
                                sps = p.psA[:, 0:512] if use_a else p.psB[:, 0:512]
                                p.wt(tkl, sA_read if use_a else sB_read)
                                qrow = (h % 2) * 64 if l == 0 else 0
                                qchunk = h // 2 if l == 0 else h
                                tsm = p.inc("eT", p.te.matmul(
                                    sps, kslots[kk][qrow:qrow + HD, j * 128:(j + 1) * 128],
                                    p.bigbuf[qrow:qrow + HD,
                                             QT_OFF + qchunk * 1024 + qhalf * 512:
                                             QT_OFF + qchunk * 1024 + qhalf * 512 + 512],
                                    start=True, stop=True))
                                pt = ptslots[ptcnt % 4]
                                ptk = ptcnt % 4
                                ptcnt += 1
                                p.ws(tsm, pt_free[ptk])
                                texp = p.inc("eS", p.se.activation(
                                    pt, sps, AF.Exp, bias=p.negb[:, h:h + 1], scale=pr["scale"]))
                                if use_a:
                                    sA_read = texp
                                else:
                                    sB_read = texp
                                tmask = None
                                if it == 0:
                                    p.wv(texp)
                                    tmask = p.inc("eV", p.ve.tensor_tensor(
                                        pt, pt, p.maskb[:, j * 512:(j + 1) * 512], ALU.mult))
                                descs.append(dict(
                                    ovs=ovs[hi], lvs=lvs[hi],
                                    vls=vslots[vv][:, g * pr["vstride"]:g * pr["vstride"] + (65 if l == 0 else 128)],
                                    ind=vslots[vv][:, 512:513],
                                    pt=pt, ptk=ptk, texp=texp, tmask=tmask, tvl=tvl,
                                    start=(it == 0 and j == 0),
                                    stop=(it == NIT - 1 and j == 3),
                                    kk=kk, vv=vv))
                            for d in pend[0]:
                                _emit_av(d)
                            pend[0] = descs
                for d in pend[0]:
                    _emit_av(d)
                pend[0] = []
                tavl = p.tok("eT")
                for hi, h in enumerate(heads):
                    if l == 0:
                        lrow = ovs[hi][64:65, :]
                        orows = ovs[hi][0:64, :]
                        dst = p.abuf[(h % 2) * 64:(h % 2) * 64 + 64,
                                     (h // 2) * 1024 + qhalf * 512:(h // 2) * 1024 + qhalf * 512 + 512]
                        drows = 64
                    else:
                        lrow = lvs[hi]
                        orows = ovs[hi]
                        dst = p.abuf[:, h * 1024 + qhalf * 512:h * 1024 + qhalf * 512 + 512]
                        drows = 128
                    p.wv(tavl)
                    tr1 = p.inc("eV", p.ve.reciprocal(p.linv[0:1, 0:512], lrow))
                    p.wt(tr1, sA_read, sB_read)
                    tb = p.inc("eT", p.te.matmul(
                        p.psA[0:drows, 0:512], p.onesf[0:1, 0:drows],
                        p.linv[0:1, 0:512], start=True, stop=True))
                    p.ws(tb, ev_scr_read)
                    tcp = p.inc("eS", p.se.copy(p.scr32[0:drows, 0:512], p.psA[0:drows, 0:512]))
                    sA_read = tcp
                    p.wv(tcp)
                    tev = p.inc("eV", p.ve.tensor_tensor(dst, orows, p.scr32[0:drows, 0:512], ALU.mult))
                    ev_scr_read = tev
                    if l == 0:
                        o_read[hi // 2] = tev
                    else:
                        o_read[0] = tev
                        o_read[1] = tev
    return p.tok("eV")


def emit_outproj(p: PB, l):
    pr = attn_params(l)
    W = p.inputs[pr["wo"]]
    has_bias = pr["bo"] is not None
    if has_bias:
        tbo = p.inc("dg", p.pe.dma_start(out=p.birow[0:1, 0:2048], in_=p.inputs[pr["bo"]][:, :]), 16)
    else:
        tbo = None
    wslots = [p.wbuf[:, 0:4096], p.wbuf[:, 4096:8192]]
    w_free = [None, None]
    pslots = [p.psA[:, 0:256], p.psB[:, 0:256], p.psC[:, 0:256]]
    slot_read = [None, None, None]
    sidx = 0
    for s in range(8):
        wk_ = s % 2
        wslot = wslots[wk_]
        p.wp(w_free[wk_])
        twl = p.inc("dg", p.pe.dma_start(
            out=wslot.rearrange("p (kc n) -> p kc n", n=256), in_=W[s]), 16)
        for ti in range(NT):
            j = sidx % 3
            ps = pslots[j]
            sidx += 1
            p.wt(slot_read[j], twl, tbo)
            mm = None
            for kc in range(KC):
                mm = p.te.matmul(ps, p.abuf[:, kc * 1024 + ti * 128:kc * 1024 + (ti + 1) * 128],
                                 wslot[:, kc * 256:(kc + 1) * 256],
                                 start=(kc == 0), stop=(not has_bias and kc == KC - 1))
            if has_bias:
                mm = p.te.matmul(
                    ps, p.onesc[0:1, 0:128], p.birow[0:1, s * 256:s * 256 + 256],
                    start=False, stop=True)
            tmm = p.inc("eT", mm)
            w_free[wk_] = tmm
            p.wv(tmm)
            ssl = slice(ti * D + s * 256, ti * D + s * 256 + 256)
            tev = p.inc("eV", p.ve.tensor_tensor(
                p.stream[:, ssl], p.stream[:, ssl], ps, ALU.add))
            slot_read[j] = tev
    return p.tok("eV")



F1_SW, F1_LU, F1_MD, F1_DW, F1_LD, F1_BW = 0, 1024, 2048, 2176, 2304, 4352


def emit_ffn1(p: PB):
    bb = p.bigbuf
    tl = []
    for nm, off, rows, width in (("sw_t", F1_SW, 128, 1024), ("lu_t", F1_LU, 128, 1024),
                                 ("md_t", F1_MD, 128, 128), ("dw_t", F1_DW, 128, 128),
                                 ("ld_cat", F1_LD, 64, 2048), ("bw", F1_BW, 8, 2048)):
        src = p.inputs[nm]
        ap = src if src.ndim == 2 else src.rearrange("p a b -> p (a b)")
        tl.append(p.inc("dg", p.pe.dma_start(out=bb[0:rows, off:off + width], in_=ap), 16))
    ti1 = p.inc("eP", p.pe.iota(p.trash[0:8, 512:576], [[1, 8], [0, 8]],
                                base=0, channel_multiplier=-1,
                                allow_small_or_imprecise_dtypes=True))
    p.wv(ti1)
    te1 = p.inc("eV", p.ve.tensor_single_scalar(
        p.trash[0:8, 512:576], p.trash[0:8, 512:576], 0.0, ALU.is_equal))
    p.wt(*tl)
    mm = None
    for n in range(8):
        for kc in range(KC):
            mm = p.te.matmul(p.psA[0:1, 8 * n:8 * n + 8],
                             bb[:, F1_MD + kc * 8 + n:F1_MD + kc * 8 + n + 1],
                             bb[:, F1_SW + kc * 64 + 8 * n:F1_SW + kc * 64 + 8 * n + 8],
                             start=(kc == 0), stop=(kc == KC - 1))
    t_wmd = p.inc("eT", mm)
    for n in range(8):
        for kc in range(KC):
            mm = p.te.matmul(p.psB[0:1, n:n + 1],
                             bb[:, F1_MD + kc * 8 + n:F1_MD + kc * 8 + n + 1],
                             bb[:, F1_MD + kc * 8 + n:F1_MD + kc * 8 + n + 1],
                             start=(kc == 0), stop=(kc == KC - 1))
    t_mdsq = p.inc("eT", mm)
    for n in range(8):
        for kc in range(KC):
            mm = p.te.matmul(p.psC[0:1, 8 * n:8 * n + 8],
                             bb[:, F1_DW + kc * 8 + n:F1_DW + kc * 8 + n + 1],
                             bb[:, F1_LU + kc * 64 + 8 * n:F1_LU + kc * 64 + 8 * n + 8],
                             start=(kc == 0), stop=(kc == KC - 1))
    t_cc = p.inc("eT", mm)
    p.wv(t_wmd, t_mdsq, t_cc)
    tw1 = p.inc("eV", p.ve.tensor_copy(p.smallf[0:1, 0:64], p.psA[0:1, 0:64]))
    tw2 = p.inc("eV", p.ve.tensor_copy(p.smallf[0:1, 64:72], p.psB[0:1, 0:8]))
    tw3 = p.inc("eV", p.ve.tensor_copy(p.trash[0:1, 576:640], p.psC[0:1, 0:64]))
    trseq = TrSeq(p)
    tc4 = trseq.transpose(p.trash[0:1, 576:640], p.ccol[0:64, 0:1], extra_dep=(tw3,))
    p.wv(tw1, tw2)
    tb1 = p.inc("eV", p.ve.tensor_copy(p.trash[0:1, 640:704], p.smallf[0:1, 0:64]))
    tb2 = p.inc("eV", p.ve.tensor_copy(p.trash[0:1, 704:712], p.smallf[0:1, 64:72]))
    p.wt(tb1, tb2, tw1, tw2)
    tm1 = p.inc("eT", p.te.matmul(p.psD[:, 0:64], p.onesc[0:1, 0:128],
                                  p.trash[0:1, 640:704], start=True, stop=True))
    tm2 = p.inc("eT", p.te.matmul(p.psD[:, 64:72], p.onesc[0:1, 0:128],
                                  p.trash[0:1, 704:712], start=True, stop=True))
    p.wv(tm1, tm2)
    tb3 = p.inc("eV", p.ve.tensor_copy(p.l1tmp[:, 0:64], p.psD[:, 0:64]))
    tb4 = p.inc("eV", p.ve.tensor_copy(p.negb[:, 0:8], p.psD[:, 64:72]))
    p.wv(tb3, tb4)

    Q0, VSQ, SCR, GAT, CMX, OH, MCOL = 0, 64, 72, 80, 88, 96, 104
    psAB_free = p.tok("eV")
    psD_free = p.tok("eV")
    psC_free = p.tok("eV")
    ohT_t = None
    st_upd = None
    for ti in range(NT):
        xsl = slice(ti * 128, (ti + 1) * 128)
        p.wt(psAB_free, st_upd)
        mm = None
        for kc in range(KC):
            lhs = p.abuf[:, kc * 1024 + ti * 128:kc * 1024 + (ti + 1) * 128]
            mm = p.te.matmul(p.psA[:, 0:64], lhs,
                             bb[:, F1_SW + kc * 64:F1_SW + (kc + 1) * 64],
                             start=(kc == 0), stop=(kc == KC - 1))
        tA = p.inc("eT", mm)
        for kc in range(KC):
            lhs = p.abuf[:, kc * 1024 + ti * 128:kc * 1024 + (ti + 1) * 128]
            mm = p.te.matmul(p.psA[:, 64:72], lhs,
                             bb[:, F1_MD + kc * 8:F1_MD + (kc + 1) * 8],
                             start=(kc == 0), stop=(kc == KC - 1))
        tB = p.inc("eT", mm)
        for kc in range(KC):
            mm = p.te.matmul(p.psB[0:64, 0:128],
                             bb[:, F1_LU + kc * 64:F1_LU + (kc + 1) * 64],
                             p.abuf[:, kc * 1024 + ti * 128:kc * 1024 + (ti + 1) * 128],
                             start=(kc == 0), stop=(kc == KC - 1))
        tC = p.inc("eT", mm)
        p.wv(tA, tB)
        t1 = p.inc("eV", p.ve.tensor_tensor(
            p.scr32[:, MCOL + 1:MCOL + 2], p.acol[:, ti:ti + 1], p.acol[:, ti:ti + 1], ALU.mult))
        p.wv(t1)
        t2 = p.inc("eV", p.ve.tensor_scalar(
            p.scr32[:, MCOL:MCOL + 1], p.scr32[:, MCOL + 1:MCOL + 2],
            -float(D) * 1e-5, float(D), ALU.mult, ALU.add))
        tq = p.inc("eV", p.ve.tensor_tensor(
            p.scr32[:, Q0:Q0 + 64], p.psA[:, 0:64], p.l1tmp[:, 0:64], ALU.subtract))
        p.ws(tq)
        tsq = p.inc("eS", p.se.activation(p.trash[:, 0:64], p.scr32[:, Q0:Q0 + 64],
                                          AF.Square, bias=0.0, scale=1.0))
        p.wv(tsq, t2)
        tr_ = p.inc("eV", p.ve.tensor_reduce(
            p.scr32[:, SCR:SCR + 8],
            p.trash[:, 0:64].rearrange("p (n s) -> p n s", s=8), AX.X, ALU.add))
        tv1 = p.inc("eV", p.ve.tensor_scalar(
            p.scr32[:, VSQ:VSQ + 8], p.psA[:, 64:72], -2.0,
            p.scr32[:, MCOL:MCOL + 1], ALU.mult, ALU.add))
        p.wv(tv1)
        tv2 = p.inc("eV", p.ve.tensor_tensor(
            p.scr32[:, VSQ:VSQ + 8], p.scr32[:, VSQ:VSQ + 8], p.negb[:, 0:8], ALU.add))
        psAB_free = tv2
        p.wv(tv2)
        tv3 = p.inc("eV", p.ve.reciprocal(p.scr32[:, OH:OH + 8], p.scr32[:, VSQ:VSQ + 8]))
        p.wv(tv3, tr_)
        tsc = p.inc("eV", p.ve.tensor_tensor(
            p.scr32[:, SCR:SCR + 8], p.scr32[:, SCR:SCR + 8], p.scr32[:, OH:OH + 8], ALU.mult))
        p.wv(tsc)
        tg1 = p.inc("eV", p.ve.tensor_tensor(
            p.scr32[:, GAT:GAT + 8], p.scr32[:, SCR:SCR + 8], p.cenb[:, :], ALU.subtract))
        p.wv(tg1)
        tg2 = p.inc("eV", p.ve.tensor_tensor(
            p.scr32[:, GAT:GAT + 8], p.scr32[:, GAT:GAT + 8], p.slob[:, :], ALU.mult))
        p.ws(tg2)
        tg3 = p.inc("eS", p.se.activation(p.scr32[:, GAT:GAT + 8], p.scr32[:, GAT:GAT + 8],
                                          AF.Sigmoid, bias=0.0, scale=1.0))
        p.wv(tg3)
        tmx = p.inc("eV", p.ve.tensor_reduce(
            p.scr32[:, MCOL + 2:MCOL + 3], p.scr32[:, GAT:GAT + 8], AX.X, ALU.max))
        tcm = p.inc("eV", p.ve.memset(p.scr32[:, CMX:CMX + 1], -1.0))
        for n in range(1, 8):
            p.wv(tcm)
            tcm = p.inc("eV", p.ve.tensor_tensor(
                p.scr32[:, CMX + n:CMX + n + 1], p.scr32[:, CMX + n - 1:CMX + n],
                p.scr32[:, GAT + n - 1:GAT + n], ALU.max))
        p.wv(tcm, tmx)
        to1 = p.inc("eV", p.ve.tensor_scalar(
            p.scr32[:, OH:OH + 8], p.scr32[:, GAT:GAT + 8],
            p.scr32[:, MCOL + 2:MCOL + 3], None, ALU.is_ge))
        to2 = p.inc("eV", p.ve.tensor_scalar(
            p.scr32[:, SCR:SCR + 8], p.scr32[:, CMX:CMX + 8],
            p.scr32[:, MCOL + 2:MCOL + 3], None, ALU.is_lt))
        p.wv(to1, to2)
        to3 = p.inc("eV", p.ve.tensor_tensor(
            p.scr32[:, OH:OH + 8], p.scr32[:, OH:OH + 8], p.scr32[:, SCR:SCR + 8], ALU.mult))
        p.wv(to3)
        tob = p.inc("eV", p.ve.tensor_copy(p.trash[:, 64:72], p.scr32[:, OH:OH + 8]))
        tohT = trseq.transpose(p.trash[:, 64:72], p.knrow[0:8, ti * 128:(ti + 1) * 128],
                               extra_dep=(tob,), rows=8)
        p.wt(tohT, te1, psC_free)
        trep = p.inc("eT", p.te.matmul(
            p.psC[0:64, 0:128], p.trash[0:8, 512:576],
            p.knrow[0:8, ti * 128:(ti + 1) * 128], start=True, stop=True))
        p.wv(tC, tc4)
        tmod = p.inc("eV", p.ve.tensor_scalar(
            p.scr32[0:64, 128:256], p.psB[0:64, 0:128], p.ccol[0:64, 0:1], None, ALU.subtract))
        p.wv(trep, tmod)
        trc = p.inc("eV", p.ve.tensor_copy(p.trash[0:64, 128:256], p.psC[0:64, 0:128]))
        psC_free = trc
        p.wv(trc)
        tmm2 = p.inc("eV", p.ve.tensor_tensor(
            p.ptbuf[0:64, 0:128], p.scr32[0:64, 128:256], p.trash[0:64, 128:256], ALU.mult))
        p.wt(tmm2, psD_free, psAB_free)
        for ng in range(4):
            pst = p.psD if ng < 2 else p.psA
            psl = pst[:, (ng % 2) * 512:(ng % 2 + 1) * 512]
            p.te.matmul(psl, p.ptbuf[0:64, 0:128],
                        bb[0:64, F1_LD + ng * 512:F1_LD + (ng + 1) * 512],
                        start=True, stop=False)
            mm = p.te.matmul(psl, p.knrow[0:8, ti * 128:(ti + 1) * 128],
                             bb[0:8, F1_BW + ng * 512:F1_BW + (ng + 1) * 512],
                             start=False, stop=True)
        tdg = p.inc("eT", mm)
        tS1 = p.inc("eV", p.ve.tensor_scalar(
            p.scr32[:, MCOL + 3:MCOL + 4], p.scr32[:, MCOL + 2:MCOL + 3],
            -1.0, 1.0, ALU.mult, ALU.add))
        txq = p.inc("eV", p.ve.tensor_scalar(
            p.scr32[:, 512:2560], p.stream[:, ti * D:(ti + 1) * D],
            p.mcol[:, ti:ti + 1], p.acol[:, ti:ti + 1], ALU.subtract, ALU.mult))
        p.wv(tdg, tS1, txq, st_upd)
        t_u1 = p.inc("eV", p.ve.scalar_tensor_tensor(
            p.stream[:, ti * D:ti * D + 1024], p.psD[:, 0:1024],
            p.scr32[:, MCOL + 2:MCOL + 3], p.stream[:, ti * D:ti * D + 1024],
            ALU.mult, ALU.add))
        p.wv(t_u1)
        t_u2 = p.inc("eV", p.ve.scalar_tensor_tensor(
            p.stream[:, ti * D + 1024:(ti + 1) * D], p.psA[:, 0:1024],
            p.scr32[:, MCOL + 2:MCOL + 3], p.stream[:, ti * D + 1024:(ti + 1) * D],
            ALU.mult, ALU.add))
        p.wv(t_u2)
        t_u3 = p.inc("eV", p.ve.scalar_tensor_tensor(
            p.stream[:, ti * D:(ti + 1) * D], p.scr32[:, 512:2560],
            p.scr32[:, MCOL + 3:MCOL + 4], p.stream[:, ti * D:(ti + 1) * D],
            ALU.mult, ALU.add))
        st_upd = t_u3
        psD_free = t_u1
        psAB_free = t_u2
    return p.tok("eV")



INPUT_SPECS = {
    "xin": (TC, D), "cos0": (TC, 32), "sin0": (TC, 32),
    "cos1": (TC, 64), "sin1": (TC, 64),
    "ident": (128, 128), "masks": (128, 2048),
    "l0_wq": (8, 128, 16, 256), "l0_wk": (2, 128, 16, 256),
    "l0_wv": (2, 128, 16, 256), "l0_wo": (8, 128, 16, 256),
    "l0_bq": (1, 2048), "l0_bk": (1, 512), "l0_bv": (1, 512), "l0_bo": (1, 2048),
    "l0_gate": (44, 128, 16, 128), "l0_up": (44, 128, 16, 128),
    "l0_down": (44, 128, 2048),
    "l0_gs": (128, 44), "l0_us": (128, 44), "l0_ds": (128, 2048),
    "l1_wq": (8, 128, 16, 256), "l1_wk": (2, 128, 16, 256),
    "l1_wv": (2, 128, 16, 256), "l1_wo": (8, 128, 16, 256),
    "l1_bq": (1, 2048), "l1_bk": (1, 512), "l1_bv": (1, 512),
    "l1_lnw": (128, 2048),
    "sw_t": (128, 16, 64), "lu_t": (128, 16, 64), "ld_cat": (64, 2048),
    "md_t": (128, 16, 8), "dw_t": (128, 16, 8), "bw": (8, 2048),
    "cen_bc": (128, 8), "slo_bc": (128, 8),
}


def declare_all(p: PB):
    for k, shp in INPUT_SPECS.items():
        p.din(k, shp, F32)
    p.din("kvi", (2, 2, 128, 192), mybir.dt.int32)


def build_full():
    p = PB(0)
    declare_all(p)
    yout = p.dout("yout", [TC, D], F32)
    L_alloc = alloc_common(p)
    p.ddown = p.dint("ddown", [44, 128, 2048], BF16)
    p.cc_in = {0: p.dint("cc_in0", [1544, 1040], BF16),
               1: p.dint("cc_in1", [1544, 1040], BF16)}
    p.cc_out = {0: p.dint("cc_out0", [13120, 1040], BF16),
                1: p.dint("cc_out1", [13120, 1040], BF16)}
    emit_prologue(p)
    tld = emit_stream_load(p, p.inputs["xin"])
    p.wv(tld); p.ws(tld)
    emit_ln(p, "plain")
    emit_down_precast(p)
    p.join()
    trseq = emit_qkv(p, 0); p.join()
    emit_gather(p, 0, trseq); p.join()
    emit_attn_loop(p, 0); p.join()
    emit_outproj(p, 0); p.join()
    emit_ln(p, "plain"); p.join()
    emit_ffn0(p); p.join()
    p.inc("dg", p.pe.dma_start(out=p.bigbuf[:, 16384:18432], in_=p.inputs["l1_lnw"][:, :]), 16)
    p.join()
    emit_ln(p, "l1attn", lnw_view=p.bigbuf[:, 16384:18432]); p.join()
    trseq = emit_qkv(p, 1); p.join()
    emit_gather(p, 1, trseq); p.join()
    emit_attn_loop(p, 1); p.join()
    emit_outproj(p, 1); p.join()
    emit_ln(p, "plain"); p.join()
    emit_ffn1(p); p.join()
    tst = emit_stream_store(p, yout)
    p.wy(tst)
    return p


_PROG = None


def kernel(**inputs):
    global _PROG
    g = host_prep(inputs)
    x_full = np.asarray(inputs["L0_attn_x"])[0]
    maps = per_core_inputs(g, x_full)
    if _PROG is None:
        _PROG = build_full()
    p = _PROG
    fmaps = [{k: mm[k] for k in p.inputs} for mm in maps]
    res = run_bass_kernel_spmd(p.nc, fmaps, core_ids=list(range(NCORE)))
    y = np.zeros((T, D), np.float32)
    for c in range(NCORE):
        y[core_token_rows(c)] = res.results[c]["yout"]
    return y.reshape(1, T, D)
